# version 01
# baseline (speedup 1.0000x reference)
import numpy as np
from contextlib import ExitStack

import concourse.bass as bass
import concourse.mybir as mybir
from concourse.bass_utils import run_bass_kernel_spmd

F32 = mybir.dt.float32
BF16 = mybir.dt.bfloat16
Alu = mybir.AluOpType
Act = mybir.ActivationFunctionType

NCORES = 8
B, L, D = 2, 16384, 1024
TPC = 4096
NT = 8
TT = 512
NSUB = 4
NG = 8
NE = 32
EPL = 4
DI = 512
EPS = 1e-5
RG = [[0, 1, 2, 3, 4, 5, 6, 7]]
EMAT = DI * D
WSH = EPL * 3 * EMAT
QTR = 1024
NQ = TPC // QTR
ENGS = ("tensor", "vector", "scalar", "sync", "gpsimd")


class Sem:
    def __init__(self, nc, es, name):
        self.h = es.enter_context(nc.semaphore(name))
        self.n = 0


class MK:
    def __init__(self, taps=()):
        self.taps = taps
        self.nc = bass.Bass("TRN2", target_bir_lowering=False, debug=False)
        self.es = ExitStack()
        self.ops = {e: [] for e in ENGS}
        self.tap_outs = {}
        self.bank_guard = {}

    def sched(self, eng, fn, inc=None, k=1):
        if inc is not None:
            inc.n += k

            def thunk(e, fn=fn, inc=inc, k=k):
                fn(e).then_inc(inc.h, k)
        else:
            def thunk(e, fn=fn):
                fn(e)
        self.ops[eng].append(thunk)

    def wait(self, eng, sem, val=None):
        v = sem.n if val is None else val
        self.ops[eng].append(lambda e, h=sem.h, v=v: e.wait_ge(h, v))

    def dma(self, eng, out, in_, **kw):
        sem = {"sync": self.dS, "scalar": self.dA, "gpsimd": self.dG}[eng]
        self.sched(eng, lambda e, o=out, i=in_, kw=kw: e.dma_start(out=o, in_=i, **kw),
                   inc=sem, k=16)
        return sem.n

    def barrier(self):
        self.wait("sync", self.dS)
        self.wait("scalar", self.dA)
        self.wait("gpsimd", self.dG)
        marks = {}
        for e in ENGS:
            s = self.fence[e]
            self.sched(e, lambda en: en.nop(), inc=s)
            marks[e] = s.n
        for e in ENGS:
            for o in ENGS:
                if o != e:
                    self.wait(e, self.fence[o], marks[o])

    def emit(self, blk):
        for e in ENGS:
            getattr(blk, e)(lambda eng, e=e: [f(eng) for f in self.ops[e]])

    def csem(self, eng):
        return {"vector": self.cV, "scalar": self.cS, "gpsimd": self.cG}[eng]

    def ts(self, eng, out, in0, s1, s2, op0, op1=Alu.bypass, accum=None):
        self.sched(eng, lambda e: e.tensor_scalar(
            out=out, in0=in0, scalar1=s1, scalar2=s2, op0=op0, op1=op1,
            accum_out=accum), inc=self.csem(eng))

    def tt(self, eng, out, in0, in1, op):
        self.sched(eng, lambda e: e.tensor_tensor(out=out, in0=in0, in1=in1, op=op),
                   inc=self.csem(eng))

    def stt(self, eng, out, in0, scalar, in1, op0, op1):
        self.sched(eng, lambda e: e.scalar_tensor_tensor(
            out=out, in0=in0, scalar=scalar, in1=in1, op0=op0, op1=op1),
            inc=self.csem(eng))

    def act(self, out, in_, func, bias=0.0, scale=1.0, accum=None):
        self.sched("scalar", lambda e: e.activation(
            out=out, in_=in_, func=func, bias=bias, scale=scale,
            accum_out=accum), inc=self.cS)

    def scopy(self, out, in_):
        self.sched("scalar", lambda e: e.copy(out=out, in_=in_), inc=self.cS)

    def vcopy(self, out, in_):
        self.sched("vector", lambda e: e.tensor_copy(out=out, in_=in_), inc=self.cV)

    def memset(self, ap, val):
        self.sched("vector", lambda e: e.memset(ap, val), inc=self.cV)

    def mm(self, out, lhsT, rhs, start, stop):
        self.sched("tensor", lambda e: e.matmul(
            out=out, lhsT=lhsT, rhs=rhs, start=start, stop=stop), inc=self.cT)

    def tr(self, out, in_, ident):
        self.sched("tensor", lambda e: e.transpose(
            out=out, in_=in_, identity=ident), inc=self.cT)

    def use_bank(self, bank):
        g = self.bank_guard.get(bank)
        if g is not None:
            sem, mark = g
            self.wait("tensor", sem, mark)

    def guard_bank(self, bank, eng):
        self.bank_guard[bank] = (self.csem(eng), self.csem(eng).n)

    def vsync(self):
        self.wait("vector", self.cV)

    def ln_finish(self, s_sum, s_sq, mean, rstd, tmp, nelem=D):
        inv = 1.0 / nelem
        self.ts("vector", mean, s_sum, inv, None, Alu.mult)
        self.ts("vector", rstd, s_sq, inv, None, Alu.mult)
        self.vsync()
        self.tt("vector", tmp, mean, mean, Alu.mult)
        self.vsync()
        self.tt("vector", rstd, rstd, tmp, Alu.subtract)
        self.vsync()
        self.ts("vector", rstd, rstd, EPS, None, Alu.add)
        vm = self.cV.n
        self.wait("scalar", self.cV, vm)
        self.act(rstd, rstd, Act.Ln)
        self.act(rstd, rstd, Act.Exp, scale=-0.5)

    def build(self):
        nc, es = self.nc, self.es
        dram = lambda n, s, k="ExternalInput", d=F32: nc.dram_tensor(
            n, s, d, kind=k).ap()
        self.sbuf = lambda n, s, d=F32: es.enter_context(nc.sbuf_tensor(n, s, d))
        sbuf = self.sbuf

        self.xs = dram("xs", [TPC, D])
        self.xp = dram("xp", [1, D])
        self.a8 = dram("a8", [1, 8])
        self.sel8 = dram("sel8", [8, 1])
        self.pmask = dram("pmask", [1, 1])
        names = ["ln_att_g", "ln_att_b", "ln_ffn_g", "ln_ffn_b", "td", "tf",
                 "mk", "mv", "mr", "fmk", "fmr"]
        self.wvec = {n: dram("w_" + n, [1, D]) for n in names}
        self.w_wk = dram("w_wk", [D, D])
        self.w_wv = dram("w_wv", [D, D])
        self.w_wr = dram("w_wr", [D, D])
        self.w_wo = dram("w_wo", [D, D])
        self.w_fwk = dram("w_fwk", [D, 4 * D])
        self.w_fwv = dram("w_fwv", [4 * D, D])
        self.w_fwr = dram("w_fwr", [D, D])
        self.w_gw = dram("w_gw", [NE, D])
        self.w_gb = dram("w_gb", [1, NE])
        self.w_meg = dram("w_meg", [EPL, DI, D])
        self.w_meu = dram("w_meu", [EPL, DI, D])
        self.w_med = dram("w_med", [EPL, D, DI])
        self.w_sg = dram("w_sg", [D, D])
        self.w_su = dram("w_su", [D, D])
        self.w_sd = dram("w_sd", [D, D])
        self.y = dram("y", [TPC, D], k="ExternalOutput")

        self.s_num = dram("s_num", [NT, 128, NG * TT], k="Internal")
        self.s_den = dram("s_den", [NT, 128, NG * TT], k="Internal")
        self.s_r = dram("s_r", [NT, 128, NG * TT], k="Internal", d=BF16)
        self.s_u0 = dram("s_u0", [TPC, D], k="Internal")
        self.s_xb = dram("s_xb", [TPC, D], k="Internal")
        self.s_x1 = dram("s_x1", [TPC, D], k="Internal")
        self.s_u1 = dram("s_u1", [NT, 128, NG * TT], k="Internal", d=BF16)
        self.s_kf = dram("s_kf", [NT, 128, 32 * TT], k="Internal", d=BF16)
        self.s_rf = dram("s_rf", [NT, 128, NG * TT], k="Internal", d=BF16)
        self.s_shT = dram("s_shT", [3, 128, NG * D], k="Internal", d=BF16)
        self.cc1_in = dram("cc1_in", [1, 2 * D], k="Internal")
        self.cc1_out = dram("cc1_out", [8, 2 * D], k="Internal")
        self.cc2_in = dram("cc2_in", [1, D], k="Internal")
        self.cc2_out = dram("cc2_out", [8, D], k="Internal")
        self.ccw_in = dram("ccw_in", [1, WSH], k="Internal", d=BF16)
        self.ccw_out = dram("ccw_out", [8, WSH], k="Internal", d=BF16)

        if "h2" in self.taps:
            self.s_h2 = dram("s_h2", [NT, 128, NG * TT], k="Internal", d=BF16)
            self.s_xk2 = dram("s_xk2", [NT, 128, NG * TT], k="Internal", d=BF16)
        for t in self.taps:
            src = {"num": self.s_num, "den": self.s_den, "r": self.s_r,
                   "u0": self.s_u0, "xb": self.s_xb, "x1": self.s_x1,
                   "u1": self.s_u1, "kf": self.s_kf, "rf": self.s_rf,
                   "h2": getattr(self, "s_h2", None),
                   "xk2": getattr(self, "s_xk2", None)}[t]
            self.tap_outs[t] = (nc.dram_tensor(
                "tap_" + t, list(src.shape), src.dtype,
                kind="ExternalOutput").ap(), src)

        self.c_id = nc.inline_tensor(np.eye(128, dtype=np.float32), "c_id").ap()
        self.c_ramp = nc.inline_tensor(
            np.arange(TT, dtype=np.float32)[None, :], "c_ramp").ap()
        self.c_ones = nc.inline_tensor(
            np.ones((1, 128), dtype=np.float32), "c_ones").ap()

        self.ident32 = sbuf("ident32", [128, 128])
        self.ident16 = sbuf("ident16", [128, 128], BF16)
        self.ones32 = sbuf("ones32", [1, 128])
        self.ones16 = sbuf("ones16", [1, 128], BF16)
        self.ramp01 = sbuf("ramp01", [128, TT])
        self.rampsb = sbuf("rampsb", [1, TT])
        NCON = 18
        self.cvr = sbuf("cfm", [128, NCON, NG])
        (self.C_MK, self.C_MV, self.C_MR, self.C_FMK, self.C_FMR, self.C_LAG,
         self.C_LAB, self.C_LFG, self.C_LFB, self.C_TD, self.C_TF, self.C_W,
         self.C_EW, self.C_EU, self.C_PIN, self.C_QIN, self.C_WTB,
         self.C_TMP) = range(NCON)
        self.a8sb = sbuf("a8sb", [1, 8])
        self.a8bc = sbuf("a8bc", [128, 8])
        self.pmask_sb = sbuf("pmask_sb", [1, 1])
        self.sel8_sb = sbuf("sel8_sb", [8, 1])
        self.gwT = sbuf("gwT", [128, NG, NE], BF16)
        self.gbias_row = sbuf("gbias_row", [1, NE])
        self.gbias_bc = sbuf("gbias_bc", [128, NE])
        self.bnd_h1 = sbuf("bnd_h1", [128, NG, 1], BF16)
        self.bnd_h2 = sbuf("bnd_h2", [128, NG, 1], BF16)
        self.row_a = sbuf("row_a", [1, D])
        self.row_b = sbuf("row_b", [1, D])
        self.row_gx = sbuf("row_gx", [1, D])
        self.row_bx = sbuf("row_bx", [1, D])
        self.row16 = sbuf("row16", [1, D], BF16)
        self.rst = sbuf("rst", [1, 8])
        self.pq_j = sbuf("pq_j", [128, 8, 16])
        self.coefs = sbuf("coefs", [128, NG, 8])

        self.pb = [es.enter_context(nc.psum_tensor(f"pb{i}", [128, 512], F32))
                   for i in range(8)]
        self.pb16 = [p.ap().bitcast(BF16) for p in self.pb]

        for s in ("dG", "dS", "dA", "cT", "cV", "cS", "cG"):
            setattr(self, s, Sem(nc, es, s))
        self.fence = {e: Sem(nc, es, "f_" + e) for e in ENGS}

        blk = es.enter_context(nc.Block())

        with ExitStack() as stg_scope:
            pst = lambda n, s, d=BF16: stg_scope.enter_context(
                nc.sbuf_tensor(n, s, d))
            self.stg = [pst("stg0", [128, 4096])]
            self.wtb = pst("wtb", [128, 4096])

            self.p0()
            self.p1a_and_exchange()
            self.barrier()
            self.p1b()
            self.barrier()
            self.p2a()
            self.barrier()
            self.p2b()
            self.barrier()
        self.p3()
        self.finish()

        self.emit(blk)
        es.close()
        return nc

    def load_cvec(self, col, src):
        self.dma("gpsimd", self.cvr[:, col, :],
                 src.rearrange("o (g p) -> (p o) g", p=128),
                 allow_slow_non_contiguous=True)

    def p0(self):
        C, cv = self, self.cvr
        self.dma("sync", self.ident32[:], self.c_id)
        self.dma("sync", self.ones32[:], self.c_ones)
        self.dma("sync", self.rampsb[:], self.c_ramp)
        self.dma("sync", self.gbias_row[:], self.w_gb)
        self.dma("gpsimd", self.ident16[:], self.c_id)
        self.dma("gpsimd", self.ones16[:], self.c_ones)
        self.dma("gpsimd", self.a8sb[:], self.a8)
        self.dma("gpsimd", self.pmask_sb[:], self.pmask)
        self.dma("gpsimd", self.sel8_sb[:], self.sel8)
        self.dma("gpsimd", self.row_gx[:], self.wvec["ln_att_g"])
        self.dma("gpsimd", self.row_bx[:], self.wvec["ln_att_b"])
        for col, src in ((C.C_MK, "mk"), (C.C_MV, "mv"), (C.C_MR, "mr"),
                         (C.C_FMK, "fmk"), (C.C_FMR, "fmr"),
                         (C.C_LAG, "ln_att_g"), (C.C_LAB, "ln_att_b"),
                         (C.C_LFG, "ln_ffn_g"), (C.C_LFB, "ln_ffn_b"),
                         (C.C_TD, "td"), (C.C_TF, "tf")):
            self.load_cvec(col, self.wvec[src])
        self.dma("gpsimd", self.row_a[:], self.xp)
        gLD = self.dG.n
        self.gLD = gLD
        self.dS_consts = self.dS.n

        self.wait("scalar", self.dG, gLD)
        self.act(cv[:, C.C_W, :], cv[:, C.C_TD, :], Act.Exp)
        sW = self.cS.n
        self.wait("vector", self.cS, sW)
        self.ts("vector", cv[:, C.C_W, :], cv[:, C.C_W, :], -1.0, None, Alu.mult)
        vW = self.cV.n
        self.wait("scalar", self.cV, vW)
        self.act(cv[:, C.C_EW, :], cv[:, C.C_W, :], Act.Exp)
        self.act(cv[:, C.C_EU, :], cv[:, C.C_TF, :], Act.Exp)

        self.wait("tensor", self.dS)
        self.wait("tensor", self.dG, gLD)
        self.mm(self.pb[6][:, 0:TT], self.ones32[:], self.rampsb[:], True, True)
        self.mm(self.pb[7][:, 0:8], self.ones32[:], self.a8sb[:], True, True)
        self.mm(self.pb[7][:, 8:8 + NE], self.ones32[:], self.gbias_row[:],
                True, True)
        tBC = self.cT.n
        self.wait("vector", self.cT, tBC)
        self.vcopy(self.ramp01[:], self.pb[6][:, 0:TT])
        self.vcopy(self.a8bc[:], self.pb[7][:, 0:8])
        self.vcopy(self.gbias_bc[:], self.pb[7][:, 8:8 + NE])

        for b in (6, 7):
            self.guard_bank(b, "vector")
        self.pb_free_v = self.cV.n

    def build_gwT(self, gw16):
        self.wait("tensor", self.dG)
        for hc in range(NG):
            self.tr(self.pb16[5][:, hc * 64:hc * 64 + NE],
                    gw16[:, hc * 128:(hc + 1) * 128],
                    self.ident16[0:NE, 0:NE])
        tGW = self.cT.n
        self.wait("vector", self.cT, tGW)
        for hc in range(NG):
            self.vcopy(self.gwT[:, hc, :], self.pb16[5][:, hc * 64:hc * 64 + NE])
        self.guard_bank(5, "vector")

    def p0_staging(self):
        mats = []
        for el in range(EPL):
            mats.append((self.w_meg[el], 4, 8,
                         ("ccw", (el * 3 + 0) * EMAT, EMAT)))
            mats.append((self.w_meu[el], 4, 8,
                         ("ccw", (el * 3 + 1) * EMAT, EMAT)))
            mats.append((self.w_med[el], 8, 4,
                         ("ccw", (el * 3 + 2) * EMAT, EMAT)))
        for mi, src in enumerate((self.w_sg, self.w_su, self.w_sd)):
            for half in range(2):
                mats.append((src[:, half * 512:(half + 1) * 512], 8, 4,
                             ("shT", mi, half)))
        store_mark = 0
        for i, (src, nrb, ncb, dest) in enumerate(mats):
            stg = self.stg[0]
            if i >= 2:
                pass
            ldm = self.dma(
                "gpsimd",
                stg.ap().rearrange("p (rb c) -> p rb c", rb=nrb),
                src.rearrange("(rb p) c -> p rb c", p=128))
            self.wait("sync", self.dG, ldm)
            if i >= 1:
                self.wait("sync", self.dG, store_mark)
            for rb in range(nrb):
                for cb in range(ncb):
                    self.sched("sync", lambda e, rb=rb, cb=cb, nrb=nrb, ncb=ncb,
                               stg=stg: e.dma_start_transpose(
                                   out=self.wtb[:, (cb * nrb + rb) * 128:
                                                (cb * nrb + rb) * 128 + 128],
                                   in_=stg[:, (rb * ncb + cb) * 128:
                                           (rb * ncb + cb) * 128 + 128]),
                               inc=self.dS, k=16)
            trm = self.dS.n
            self.wait("gpsimd", self.dS, trm)
            if dest[0] == "ccw":
                _, off, n = dest
                store_mark = self.dma(
                    "gpsimd",
                    self.ccw_in[0, off:off + n].rearrange("(p f) -> p f", p=128),
                    self.wtb[:])
            else:
                _, mi, half = dest
                store_mark = self.dma(
                    "gpsimd", self.s_shT[mi, :, half * 4096:(half + 1) * 4096],
                    self.wtb[:])
        self.wait("gpsimd", self.dG)
        self.sched("gpsimd", lambda e: e.collective_compute(
            "AllGather", Alu.bypass, replica_groups=RG,
            ins=[self.ccw_in], outs=[self.ccw_out]), inc=self.cG)
        self.agw_done = self.cG.n

    def row_ln(self, dst_bnd, g_row, b_row):
        self.act(self.row_b[:], self.row_a[:], Act.Identity,
                 accum=self.rst[:, 0:1])
        self.act(self.row_b[:], self.row_a[:], Act.Square,
                 accum=self.rst[:, 1:2])
        sM = self.cS.n
        self.wait("vector", self.cS, sM)
        self.ln_finish(self.rst[:, 0:1], self.rst[:, 1:2], self.rst[:, 2:3],
                       self.rst[:, 3:4], self.rst[:, 4:5])
        sM2 = self.cS.n
        self.wait("vector", self.cS, sM2)
        self.ts("vector", self.row_b[:], self.row_a[:], self.rst[:, 2:3],
                self.rst[:, 3:4], Alu.subtract, Alu.mult)
        self.tt("vector", self.row_b[:], self.row_b[:], g_row, Alu.mult)
        self.tt("vector", self.row_b[:], self.row_b[:], b_row, Alu.add)
        self.ts("vector", self.row16[:], self.row_b[:],
                self.pmask_sb[0:1, 0:1], None, Alu.mult)
        vD = self.cV.n
        self.use_bank(6)
        self.wait("tensor", self.cV, vD)
        for g in range(NG):
            self.tr(self.pb16[6][:, g * 64:g * 64 + 1],
                    self.row16[0:1, g * 128:(g + 1) * 128],
                    self.ident16[0:1, 0:1])
        tD = self.cT.n
        self.wait("vector", self.cT, tD)
        for g in range(NG):
            self.vcopy(dst_bnd[:, g, 0:1], self.pb16[6][:, g * 64:g * 64 + 1])
        self.guard_bank(6, "vector")

    def p1a_and_exchange(self):
        C, cv = self, self.cvr
        nc = self.nc
        with ExitStack() as ph:
            psb = lambda n, s, d=F32: ph.enter_context(nc.sbuf_tensor(n, s, d))
            wk_sb = psb("wk_sb", [128, NG, D], BF16)
            wv_sb = psb("wv_sb", [128, NG, D], BF16)
            wr_sb = psb("wr_sb", [128, NG, D], BF16)
            x_tm = psb("x_tm", [128, NSUB, D])
            u0_tm = psb("u0_tm", [128, NSUB, D])
            st = psb("st", [128, NSUB, 10])
            h1_fm = psb("h1_fm", [128, NG, TT + 1], BF16)
            xmix = psb("xmix", [128, NG, TT], BF16)
            ek = psb("ek", [128, NG, TT])
            ekv = psb("ekv", [128, NG, TT])
            q_arr = psb("q_arr", [128, NG, TT + 1])
            p_arr = psb("p_arr", [128, NG, TT + 1])
            h1_tm = x_tm.ap().bitcast(BF16)[:, :, 0:D]
            r4 = x_tm.ap().bitcast(BF16)[:, :, D:2 * D].rearrange(
                "p a (g2 t) -> p a g2 t", g2=2)
            r16 = lambda oc: r4[:, oc // 2, oc % 2, :]
            diff = u0_tm.ap().bitcast(BF16).rearrange(
                "p a b -> p (a b)")[:, 0:NG * TT].rearrange(
                "p (g t) -> p g t", g=NG)
            flat = lambda a: a.ap().rearrange("p a b -> p (a b)")

            gw16 = psb("gw16", [NE, D], BF16)
            for dst, src in ((wk_sb, self.w_wk), (wv_sb, self.w_wv),
                             (wr_sb, self.w_wr)):
                self.dma("gpsimd", dst.ap(),
                         src.rearrange("(ic p) o -> p ic o", p=128))
            self.dma("gpsimd", gw16[:], self.w_gw)
            wLD = self.dG.n
            self.build_gwT(gw16)
            self.p0_staging()

            self.wait("scalar", self.dG, wLD)
            self.wait("vector", self.dG, wLD)
            self.wait("vector", self.dS, self.dS_consts)
            self.row_ln(self.bnd_h1, self.row_gx[:], self.row_bx[:])
            self.memset(q_arr[:, :, TT:TT + 1], 0.0)
            self.memset(p_arr[:, :, TT:TT + 1], 0.0)

            u0_spill = [0] * NT
            nd_spill = [0] * NT
            h1T_mark = [0] * NT
            mmr_mark = [0] * NT
            h1evac_mark = [0] * NT

            for t in range(NT):
                rows = self.xs[t * TT:(t + 1) * TT].rearrange(
                    "(s p) h -> p s h", p=128)
                if t > 0:
                    self.wait("scalar", self.cT, h1T_mark[t - 1])
                    self.wait("scalar", self.dA, nd_spill[t - 1])
                xm = self.dma("scalar", x_tm[:], rows)
                self.wait("scalar", self.dA, xm)
                if t > 0:
                    self.wait("scalar", self.dA, u0_spill[t - 1])
                for s in range(NSUB):
                    self.act(u0_tm[:, s, :], x_tm[:, s, :], Act.Identity,
                             accum=st[:, s, 0:1])
                    self.act(u0_tm[:, s, :], x_tm[:, s, :], Act.Square,
                             accum=st[:, s, 1:2])
                sStat = self.cS.n
                self.wait("vector", self.cS, sStat)
                self.ln_finish(st[:, :, 0], st[:, :, 1], st[:, :, 2],
                               st[:, :, 3], st[:, :, 4])
                sFin = self.cS.n
                self.wait("vector", self.cS, sFin)
                for s in range(NSUB):
                    self.ts("vector", u0_tm[:, s, :], x_tm[:, s, :],
                            st[:, s, 2:3], st[:, s, 3:4], Alu.subtract, Alu.mult)
                vU0 = self.cV.n
                self.wait("scalar", self.cV, vU0)
                u0_spill[t] = self.dma(
                    "scalar", self.s_u0[t * TT:(t + 1) * TT].rearrange(
                        "(s p) h -> p s h", p=128), u0_tm[:])
                for s in range(NSUB):
                    self.act(x_tm[:, s, :], u0_tm[:, s, :], Act.Identity,
                             accum=st[:, s, 5:6])
                    self.act(x_tm[:, s, :], u0_tm[:, s, :], Act.Square,
                             accum=st[:, s, 6:7])
                sStat2 = self.cS.n
                self.wait("vector", self.cS, sStat2)
                self.ln_finish(st[:, :, 5], st[:, :, 6], st[:, :, 7],
                               st[:, :, 8], st[:, :, 9])
                sFin2 = self.cS.n
                self.wait("vector", self.cS, sFin2)
                for s in range(NSUB):
                    self.ts("vector", h1_tm[:, s, :], u0_tm[:, s, :],
                            st[:, s, 7:8], st[:, s, 8:9], Alu.subtract, Alu.mult)
                vH1 = self.cV.n

                if t == 0:
                    self.vcopy(h1_fm[:, :, 0:1], self.bnd_h1[:])
                else:
                    self.vcopy(h1_fm[:, :, 0:1], h1_fm[:, :, TT:TT + 1])
                self.wait("tensor", self.cV, self.cV.n)
                for s in range(NSUB):
                    self.use_bank(4 + s)
                for s in range(NSUB):
                    for g in range(NG):
                        self.tr(self.pb16[4 + s][:, g * 128:(g + 1) * 128],
                                h1_tm[:, s, g * 128:(g + 1) * 128],
                                self.ident16[:])
                h1T_mark[t] = self.cT.n
                self.wait("vector", self.cT, h1T_mark[t])
                for s in range(NSUB):
                    for g in range(NG):
                        self.ts("vector",
                                h1_fm[:, g, 1 + s * 128:1 + (s + 1) * 128],
                                self.pb16[4 + s][:, g * 128:(g + 1) * 128],
                                cv[:, C.C_LAG, g:g + 1], cv[:, C.C_LAB, g:g + 1],
                                Alu.mult, Alu.add)
                h1evac_mark[t] = self.cV.n
                for s in range(NSUB):
                    self.guard_bank(4 + s, "vector")

                if t > 0:
                    self.wait("vector", self.dA, nd_spill[t - 1])
                self.wait("vector", self.dA, u0_spill[t])
                self.tt("vector", diff, h1_fm[:, :, 1:TT + 1],
                        h1_fm[:, :, 0:TT], Alu.subtract)
                if t == 0:
                    self.wait("tensor", self.dG, wLD)
                ek_marks = []
                if t > 0:
                    self.wait("scalar", self.dA, nd_spill[t - 1])
                for mat, (wsb, col) in enumerate(((wk_sb, C.C_MK),
                                                  (wv_sb, C.C_MV),
                                                  (wr_sb, C.C_MR))):
                    if not (t == 0 and mat == 0):
                        self.wait("vector", self.cT,
                                  self._xmix_use if mat == 0 else self.cT.n)
                    for g in range(NG):
                        self.stt("vector", xmix[:, g, :], diff[:, g, :],
                                 cv[:, col, g:g + 1], h1_fm[:, g, 0:TT],
                                 Alu.mult, Alu.add)
                    vMix = self.cV.n
                    self.wait("tensor", self.cV, vMix)
                    for oc in range(NG):
                        bank = (mat * NG + oc) % 3
                        self.use_bank(bank)
                        for ic in range(NG):
                            self.mm(self.pb[bank][:, 0:TT],
                                    wsb[:, ic, oc * 128:(oc + 1) * 128],
                                    xmix[:, ic, :], ic == 0, ic == NG - 1)
                        tMM = self.cT.n
                        self._xmix_use = tMM
                        if mat == 0:
                            self.wait("scalar", self.cT, tMM)
                            self.act(ek[:, oc, :], self.pb[bank][:, 0:TT],
                                     Act.Exp)
                            self.guard_bank(bank, "scalar")
                            ek_marks.append(self.cS.n)
                        elif mat == 1:
                            self.wait("vector", self.cT, tMM)
                            self.wait("vector", self.cS, ek_marks[oc])
                            self.tt("vector", ekv[:, oc, :],
                                    self.pb[bank][:, 0:TT], ek[:, oc, :],
                                    Alu.mult)
                            self.guard_bank(bank, "vector")
                        else:
                            self.wait("scalar", self.cT, tMM)
                            self.act(r16(oc), self.pb[bank][:, 0:TT],
                                     Act.Tanh, scale=0.5)
                            self.guard_bank(bank, "scalar")
                mmr_mark[t] = self.cT.n
                sR = self.cS.n
                self.wait("vector", self.cS, sR)
                self.ts("vector", r4, r4, 0.5, 0.5, Alu.mult, Alu.add)
                for g in range(NG):
                    self.vcopy(q_arr[:, g, 0:1], q_arr[:, g, TT:TT + 1])
                    self.vcopy(p_arr[:, g, 0:1], p_arr[:, g, TT:TT + 1])
                for g in range(NG):
                    self.sched("vector", lambda e, g=g: e.tensor_tensor_scan(
                        out=q_arr[:, g, 1:TT + 1],
                        data0=cv[:, C.C_EW, g:g + 1].to_broadcast([128, TT]),
                        data1=ek[:, g, :], initial=q_arr[:, g, 0:1],
                        op0=Alu.mult, op1=Alu.add), inc=self.cV)
                    self.sched("vector", lambda e, g=g: e.tensor_tensor_scan(
                        out=p_arr[:, g, 1:TT + 1],
                        data0=cv[:, C.C_EW, g:g + 1].to_broadcast([128, TT]),
                        data1=ekv[:, g, :], initial=p_arr[:, g, 0:1],
                        op0=Alu.mult, op1=Alu.add), inc=self.cV)
                for g in range(NG):
                    self.stt("vector", ekv[:, g, :], ekv[:, g, :],
                             cv[:, C.C_EU, g:g + 1], p_arr[:, g, 0:TT],
                             Alu.mult, Alu.add)
                    self.stt("vector", ek[:, g, :], ek[:, g, :],
                             cv[:, C.C_EU, g:g + 1], q_arr[:, g, 0:TT],
                             Alu.mult, Alu.add)
                vND = self.cV.n
                self.wait("scalar", self.cV, vND)
                self.dma("scalar", self.s_num[t], flat(ekv))
                self.dma("scalar", self.s_den[t], flat(ek))
                nd_spill[t] = self.dma("scalar", self.s_r[t], r4)

            self.vcopy(self.pq_j[:, 0, 0:NG], p_arr[:, :, TT])
            self.vcopy(self.pq_j[:, 0, NG:16], q_arr[:, :, TT])
            vPack = self.cV.n
            self.wait("gpsimd", self.cV, vPack)
            self.dma("gpsimd", self.cc1_in[0].rearrange("(p f) -> p f", p=128),
                     self.pq_j[:, 0, :])
            self.wait("gpsimd", self.dG)
            self.sched("gpsimd", lambda e: e.collective_compute(
                "AllGather", Alu.bypass, replica_groups=RG,
                ins=[self.cc1_in], outs=[self.cc1_out]), inc=self.cG)
            ag1 = self.cG.n
            self.wait("gpsimd", self.cG, ag1)
            for j in range(8):
                self.dma("gpsimd", self.pq_j[:, j, :],
                         self.cc1_out[j].rearrange("(p f) -> p f", p=128))
            ccLD = self.dG.n
            self.wait("scalar", self.dG, ccLD)
            for j in range(8):
                self.act(self.coefs[:, :, j], cv[:, C.C_W, :], Act.Exp,
                         scale=self.a8bc[:, j:j + 1])
            sCoef = self.cS.n
            self.wait("vector", self.cS, sCoef)
            self.wait("vector", self.dG, ccLD)
            self.memset(cv[:, C.C_PIN, :], 0.0)
            self.memset(cv[:, C.C_QIN, :], 0.0)
            self.vsync()
            for j in range(8):
                self.tt("vector", cv[:, C.C_TMP, :], self.coefs[:, :, j],
                        self.pq_j[:, j, 0:NG], Alu.mult)
                self.vsync()
                self.tt("vector", cv[:, C.C_PIN, :], cv[:, C.C_PIN, :],
                        cv[:, C.C_TMP, :], Alu.add)
                self.vsync()
                self.tt("vector", cv[:, C.C_TMP, :], self.coefs[:, :, j],
                        self.pq_j[:, j, NG:16], Alu.mult)
                self.vsync()
                self.tt("vector", cv[:, C.C_QIN, :], cv[:, C.C_QIN, :],
                        cv[:, C.C_TMP, :], Alu.add)
                self.vsync()

    def p1b(self):
        C, cv = self, self.cvr
        nc = self.nc
        with ExitStack() as ph:
            psb = lambda n, s, d=F32: ph.enter_context(nc.sbuf_tensor(n, s, d))
            wo_sb = psb("wo_sb", [128, NG, D], BF16)
            nbuf = psb("nbuf", [128, NG, TT])
            dbuf = psb("dbuf", [128, NG, TT])
            rbuf = psb("rbuf", [128, NG, TT], BF16)
            u0_tm = psb("u0b_tm", [128, NSUB, D])
            Dfm = psb("Dfm", [128, NG, TT])
            a_fm = psb("a_fm", [128, NG, TT])
            wkv16 = psb("wkv16", [128, NG, TT], BF16)
            xb_tm = psb("xb_tm", [128, NSUB, D])
            flat = lambda a: a.ap().rearrange("p a b -> p (a b)")

            self.dma("gpsimd", wo_sb.ap(),
                     self.w_wo.rearrange("(ic p) o -> p ic o", p=128))
            wLD = self.dG.n

            consume_mark = [0] * NT
            aT_mark = [0] * NT
            for t in range(NT):
                if t > 0:
                    self.wait("scalar", self.cV, consume_mark[t - 1])
                m1 = self.dma("scalar", flat(nbuf), self.s_num[t])
                m2 = self.dma("scalar", flat(dbuf), self.s_den[t])
                m3 = self.dma("scalar", flat(rbuf), self.s_r[t])
                if t > 0:
                    self.wait("scalar", self.cV, aT_mark[t - 1])
                m4 = self.dma("scalar", u0_tm[:],
                              self.s_u0[t * TT:(t + 1) * TT].rearrange(
                                  "(s p) h -> p s h", p=128))
                self.ts("vector", cv[:, C.C_WTB, :], cv[:, C.C_W, :],
                        float(t * TT), None, Alu.mult)
                vWT = self.cV.n
                self.wait("scalar", self.cV, vWT)
                for g in range(NG):
                    self.act(Dfm[:, g, :], self.ramp01[:], Act.Exp,
                             scale=cv[:, C.C_W, g:g + 1],
                             bias=cv[:, C.C_WTB, g:g + 1])
                sD = self.cS.n
                self.wait("vector", self.cS, sD)
                self.wait("vector", self.dA, m2)
                for g in range(NG):
                    self.stt("vector", nbuf[:, g, :], Dfm[:, g, :],
                             cv[:, C.C_PIN, g:g + 1], nbuf[:, g, :],
                             Alu.mult, Alu.add)
                    self.stt("vector", dbuf[:, g, :], Dfm[:, g, :],
                             cv[:, C.C_QIN, g:g + 1], dbuf[:, g, :],
                             Alu.mult, Alu.add)
                self.sched("vector", lambda e: e.reciprocal(
                    out=flat(dbuf), in_=flat(dbuf)), inc=self.cV)
                self.tt("vector", wkv16.ap().rearrange("p a b -> p (a b)"),
                        flat(nbuf), flat(dbuf), Alu.mult)
                self.wait("vector", self.dA, m3)
                self.tt("vector", wkv16.ap().rearrange("p a b -> p (a b)"),
                        wkv16.ap().rearrange("p a b -> p (a b)"),
                        flat(rbuf), Alu.mult)
                consume_mark[t] = self.cV.n

                self.wait("tensor", self.cV, consume_mark[t])
                if t == 0:
                    self.wait("tensor", self.dG, wLD)
                for oc in range(NG):
                    bank = oc % 3
                    self.use_bank(bank)
                    for ic in range(NG):
                        self.mm(self.pb[bank][:, 0:TT],
                                wo_sb[:, ic, oc * 128:(oc + 1) * 128],
                                wkv16[:, ic, :], ic == 0, ic == NG - 1)
                    tMM = self.cT.n
                    self.wait("vector", self.cT, tMM)
                    self.vcopy(a_fm[:, oc, :], self.pb[bank][:, 0:TT])
                    self.guard_bank(bank, "vector")
                vA = self.cV.n

                self.wait("tensor", self.cV, vA)
                self.wait("scalar", self.dA, m4)
                for s in range(NSUB):
                    b0, b1 = 4 + 2 * (s % 2), 5 + 2 * (s % 2)
                    self.use_bank(b0)
                    self.use_bank(b1)
                    for g in range(NG):
                        bank = b0 if g < 4 else b1
                        self.tr(self.pb[bank].ap()[:, (g % 4) * 128:
                                                   (g % 4) * 128 + 128],
                                a_fm[:, g, s * 128:(s + 1) * 128],
                                self.ident32[:])
                    tS = self.cT.n
                    self.wait("vector", self.cT, tS)
                    self.tt("vector", xb_tm[:, s, 0:512], self.pb[b0][:],
                            u0_tm[:, s, 0:512], Alu.add)
                    self.tt("vector", xb_tm[:, s, 512:1024], self.pb[b1][:],
                            u0_tm[:, s, 512:1024], Alu.add)
                    self.guard_bank(b0, "vector")
                    self.guard_bank(b1, "vector")
                aT_mark[t] = self.cV.n
                self.wait("scalar", self.cV, aT_mark[t])
                self.dma("scalar", self.s_xb[t * TT:(t + 1) * TT].rearrange(
                    "(s p) h -> p s h", p=128), xb_tm[:])
                if t == NT - 1:
                    self.wait("gpsimd", self.cV, aT_mark[t])
                    self.dma("gpsimd", self.cc2_in, xb_tm[127:128, 3, :])
                    self.wait("gpsimd", self.dG)
                    self.sched("gpsimd", lambda e: e.collective_compute(
                        "AllGather", Alu.bypass, replica_groups=RG,
                        ins=[self.cc2_in], outs=[self.cc2_out]), inc=self.cG)
                    self.ag2_done = self.cG.n

    def p2a(self):
        C, cv = self, self.cvr
        nc = self.nc
        with ExitStack() as ph:
            psb = lambda n, s, d=F32: ph.enter_context(nc.sbuf_tensor(n, s, d))
            fwk_sb = psb("fwk_sb", [128, NG, 4 * D], BF16)
            fwr_sb = psb("fwr_sb", [128, NG, D], BF16)
            xb_tm = psb("xb2_tm", [128, NSUB, D])
            st = psb("st2", [128, NSUB, 6])
            h2_fm = psb("h2_fm", [128, NG, TT + 1], BF16)
            kf16 = psb("kf16", [128, 32, TT], BF16)
            xk2 = psb("xk2", [128, NG, TT], BF16)
            xr2 = psb("xr2", [128, NG, TT], BF16)
            ktmp = psb("ktmp", [128, 2, TT])
            cc2_sb = psb("cc2_sb", [8, D])
            h2_tm = kf16.ap()[:, 8:16, :].rearrange(
                "p (a two) t -> p a (two t)", two=2)
            rf4 = xb_tm.ap().bitcast(BF16)[:, :, D:2 * D].rearrange(
                "p a (g2 t) -> p a g2 t", g2=2)
            rf16 = lambda oc: rf4[:, oc // 2, oc % 2, :]
            diff = kf16.ap()[:, 0:NG, 0:TT]
            flat = lambda a: a.ap().rearrange("p a b -> p (a b)")

            self.dma("gpsimd", fwk_sb.ap(),
                     self.w_fwk.rearrange("(ic p) o -> p ic o", p=128))
            self.dma("gpsimd", fwr_sb.ap(),
                     self.w_fwr.rearrange("(ic p) o -> p ic o", p=128))
            wLD = self.dG.n
            self.dma("gpsimd", self.row_gx[:], self.wvec["ln_ffn_g"])
            self.dma("gpsimd", self.row_bx[:], self.wvec["ln_ffn_b"])
            rowLD = self.dG.n

            self.wait("gpsimd", self.cG, self.ag2_done)
            ccm = self.dma("gpsimd", cc2_sb[:], self.cc2_out)
            self.wait("tensor", self.dG, ccm)
            self.use_bank(6)
            self.use_bank(7)
            self.mm(self.pb[6][0:1, 0:512], self.sel8_sb[:],
                    cc2_sb[:, 0:512], True, True)
            self.mm(self.pb[7][0:1, 0:512], self.sel8_sb[:],
                    cc2_sb[:, 512:1024], True, True)
            tSel = self.cT.n
            self.wait("vector", self.cT, tSel)
            self.vcopy(self.row_a[0:1, 0:512], self.pb[6][0:1, 0:512])
            self.vcopy(self.row_a[0:1, 512:1024], self.pb[7][0:1, 0:512])
            self.guard_bank(6, "vector")
            self.guard_bank(7, "vector")
            self.wait("vector", self.dG, rowLD)
            self.row_ln(self.bnd_h2, self.row_gx[:], self.row_bx[:])

            kf_spill = [0] * NT
            h2T_mark = [0] * NT
            mm_mark = [0] * NT
            for t in range(NT):
                if t > 0:
                    self.wait("scalar", self.cT, h2T_mark[t - 1])
                    self.wait("scalar", self.dA, kf_spill[t - 1])
                xm = self.dma("scalar", xb_tm[:],
                              self.s_xb[t * TT:(t + 1) * TT].rearrange(
                                  "(s p) h -> p s h", p=128))
                self.wait("scalar", self.dA, xm)
                kdump = ktmp.ap().rearrange("p a b -> p (a b)")
                for s in range(NSUB):
                    self.act(kdump, xb_tm[:, s, :],
                             Act.Identity, accum=st[:, s, 0:1])
                    self.act(kdump, xb_tm[:, s, :],
                             Act.Square, accum=st[:, s, 1:2])
                sStat = self.cS.n
                self.wait("vector", self.cS, sStat)
                self.ln_finish(st[:, :, 0], st[:, :, 1], st[:, :, 2],
                               st[:, :, 3], st[:, :, 4])
                sFin = self.cS.n
                self.wait("vector", self.cS, sFin)
                if t > 0:
                    self.wait("vector", self.dA, kf_spill[t - 1])
                for s in range(NSUB):
                    self.ts("vector", h2_tm[:, s, :], xb_tm[:, s, :],
                            st[:, s, 2:3], st[:, s, 3:4], Alu.subtract, Alu.mult)
                vH2 = self.cV.n
                if t == 0:
                    self.vcopy(h2_fm[:, :, 0:1], self.bnd_h2[:])
                else:
                    self.vcopy(h2_fm[:, :, 0:1], h2_fm[:, :, TT:TT + 1])
                self.wait("tensor", self.cV, self.cV.n)
                for s in range(NSUB):
                    self.use_bank(4 + s)
                for s in range(NSUB):
                    for g in range(NG):
                        self.tr(self.pb16[4 + s][:, g * 128:(g + 1) * 128],
                                h2_tm[:, s, g * 128:(g + 1) * 128],
                                self.ident16[:])
                h2T_mark[t] = self.cT.n
                self.wait("vector", self.cT, h2T_mark[t])
                for s in range(NSUB):
                    for g in range(NG):
                        self.ts("vector",
                                h2_fm[:, g, 1 + s * 128:1 + (s + 1) * 128],
                                self.pb16[4 + s][:, g * 128:(g + 1) * 128],
                                cv[:, C.C_LFG, g:g + 1], cv[:, C.C_LFB, g:g + 1],
                                Alu.mult, Alu.add)
                for s in range(NSUB):
                    self.guard_bank(4 + s, "vector")

                if t > 0:
                    self.wait("vector", self.dA, kf_spill[t - 1])
                self.tt("vector", diff, h2_fm[:, :, 1:TT + 1],
                        h2_fm[:, :, 0:TT], Alu.subtract)
                if t > 0:
                    self.wait("vector", self.cT, mm_mark[t - 1])
                for dst, col in ((xk2, C.C_FMK), (xr2, C.C_FMR)):
                    for g in range(NG):
                        self.stt("vector", dst[:, g, :], diff[:, g, :],
                                 cv[:, col, g:g + 1], h2_fm[:, g, 0:TT],
                                 Alu.mult, Alu.add)
                vMix = self.cV.n
                if "h2" in self.taps:
                    self.wait("scalar", self.cV, vMix)
                    self.dma("scalar",
                             self.s_h2[t].rearrange("p (g c) -> p g c", g=NG),
                             h2_fm[:, :, 1:TT + 1])
                    self.dma("scalar", self.s_xk2[t], flat(xk2))

                self.wait("tensor", self.cV, vMix)
                if t == 0:
                    self.wait("tensor", self.dG, wLD)
                kf_sq_mark = [0, 0]
                for oc in range(32):
                    bank = oc % 3
                    self.use_bank(bank)
                    for ic in range(NG):
                        self.mm(self.pb[bank][:, 0:TT],
                                fwk_sb[:, ic, oc * 128:(oc + 1) * 128],
                                xk2[:, ic, :], ic == 0, ic == NG - 1)
                    tMM = self.cT.n
                    self.wait("scalar", self.cT, tMM)
                    slot = oc % 2
                    if oc >= 2:
                        self.wait("scalar", self.cV, kf_sq_mark[slot])
                    self.act(ktmp[:, slot, :], self.pb[bank][:, 0:TT], Act.Relu)
                    self.guard_bank(bank, "scalar")
                    sRelu = self.cS.n
                    self.wait("vector", self.cS, sRelu)
                    self.tt("vector", kf16[:, oc, :], ktmp[:, slot, :],
                            ktmp[:, slot, :], Alu.mult)
                    kf_sq_mark[slot] = self.cV.n
                for oc in range(NG):
                    bank = oc % 3
                    self.use_bank(bank)
                    for ic in range(NG):
                        self.mm(self.pb[bank][:, 0:TT],
                                fwr_sb[:, ic, oc * 128:(oc + 1) * 128],
                                xr2[:, ic, :], ic == 0, ic == NG - 1)
                    tMM = self.cT.n
                    self.wait("scalar", self.cT, tMM)
                    self.act(rf16(oc), self.pb[bank][:, 0:TT],
                             Act.Tanh, scale=0.5)
                    self.guard_bank(bank, "scalar")
                mm_mark[t] = self.cT.n
                sRF = self.cS.n
                self.wait("vector", self.cS, sRF)
                self.ts("vector", rf4, rf4, 0.5, 0.5, Alu.mult, Alu.add)
                vRF = self.cV.n
                self.wait("scalar", self.cV, vRF)
                self.dma("scalar", self.s_kf[t], flat(kf16))
                kf_spill[t] = self.dma("scalar", self.s_rf[t], rf4)

    def p2b(self):
        C, cv = self, self.cvr
        nc = self.nc
        with ExitStack() as ph:
            psb = lambda n, s, d=F32: ph.enter_context(nc.sbuf_tensor(n, s, d))
            fwv_sb = psb("fwv_sb", [128, 32, D], BF16)
            kf16 = psb("kf16b", [128, 32, TT], BF16)
            xb_tm = psb("xb3_tm", [128, NSUB, D])
            x0_tm = psb("x0_tm", [128, NSUB, D])
            f_fm = psb("f_fm", [128, NG, TT])
            x1_tm = psb("x1_tm", [128, NSUB, D])
            st = psb("st3", [128, NSUB, 6])
            u1_tm = x0_tm.ap().bitcast(BF16)[:, :, 0:D]
            rf4 = x1_tm.ap().bitcast(BF16)[:, :, 0:D].rearrange(
                "p a (g2 t) -> p a g2 t", g2=2)
            rf16 = lambda oc: rf4[:, oc // 2, oc % 2, :]
            u1buf = f_fm.ap().bitcast(BF16)[:, 4:8, 0:2 * TT].rearrange(
                "p a (g2 t) -> p (a g2) t", g2=2)
            flat = lambda a: a.ap().rearrange("p a b -> p (a b)")

            self.dma("gpsimd", fwv_sb.ap(),
                     self.w_fwv.rearrange("(ic p) o -> p ic o", p=128))
            wLD = self.dG.n

            ld_mark = [0] * NT
            u1T_mark = [0] * NT
            u1_spill = [0] * NT
            for t in range(NT):
                if t > 0:
                    self.wait("scalar", self.cV, ld_mark[t - 1])
                m1 = self.dma("scalar", flat(kf16), self.s_kf[t])
                m2 = self.dma("scalar", rf4, self.s_rf[t])
                m3 = self.dma("scalar", xb_tm[:],
                              self.s_xb[t * TT:(t + 1) * TT].rearrange(
                                  "(s p) h -> p s h", p=128))
                if t > 0:
                    self.wait("scalar", self.cT, u1T_mark[t - 1])
                m4 = self.dma("scalar", x0_tm[:],
                              self.xs[t * TT:(t + 1) * TT].rearrange(
                                  "(s p) h -> p s h", p=128))
                self.wait("tensor", self.dA, m1)
                if t == 0:
                    self.wait("tensor", self.dG, wLD)
                if t > 0:
                    self.wait("vector", self.dA, u1_spill[t - 1])
                for oc in range(NG):
                    bank = oc % 3
                    self.use_bank(bank)
                    for ic in range(32):
                        self.mm(self.pb[bank][:, 0:TT],
                                fwv_sb[:, ic, oc * 128:(oc + 1) * 128],
                                kf16[:, ic, :], ic == 0, ic == 31)
                    tMM = self.cT.n
                    self.wait("vector", self.cT, tMM)
                    if oc == 0:
                        self.wait("vector", self.dA, m2)
                    self.tt("vector", f_fm[:, oc, :], self.pb[bank][:, 0:TT],
                            rf16(oc), Alu.mult)
                    self.guard_bank(bank, "vector")
                vF = self.cV.n

                self.wait("tensor", self.cV, vF)
                self.wait("vector", self.dA, m3)
                for s in range(NSUB):
                    b0, b1 = 4 + 2 * (s % 2), 5 + 2 * (s % 2)
                    self.use_bank(b0)
                    self.use_bank(b1)
                    for g in range(NG):
                        bank = b0 if g < 4 else b1
                        self.tr(self.pb[bank].ap()[:, (g % 4) * 128:
                                                   (g % 4) * 128 + 128],
                                f_fm[:, g, s * 128:(s + 1) * 128],
                                self.ident32[:])
                    tS = self.cT.n
                    self.wait("vector", self.cT, tS)
                    self.tt("vector", x1_tm[:, s, 0:512], self.pb[b0][:],
                            xb_tm[:, s, 0:512], Alu.add)
                    self.tt("vector", x1_tm[:, s, 512:1024], self.pb[b1][:],
                            xb_tm[:, s, 512:1024], Alu.add)
                    self.guard_bank(b0, "vector")
                    self.guard_bank(b1, "vector")
                self.wait("vector", self.dA, m4)
                self.tt("vector", flat(x1_tm), flat(x1_tm), flat(x0_tm), Alu.add)
                vX1 = self.cV.n
                self.wait("scalar", self.cV, vX1)
                self.dma("scalar", self.s_x1[t * TT:(t + 1) * TT].rearrange(
                    "(s p) h -> p s h", p=128), x1_tm[:])

                fdump = f_fm.ap()[:, 0:2, :].rearrange("p a b -> p (a b)")
                for s in range(NSUB):
                    self.act(fdump, x1_tm[:, s, :], Act.Identity,
                             accum=st[:, s, 0:1])
                    self.act(fdump, x1_tm[:, s, :], Act.Square,
                             accum=st[:, s, 1:2])
                sStat = self.cS.n
                self.wait("vector", self.cS, sStat)
                self.ln_finish(st[:, :, 0], st[:, :, 1], st[:, :, 2],
                               st[:, :, 3], st[:, :, 4])
                sFin = self.cS.n
                self.wait("vector", self.cS, sFin)
                for s in range(NSUB):
                    self.ts("vector", u1_tm[:, s, :], x1_tm[:, s, :],
                            st[:, s, 2:3], st[:, s, 3:4], Alu.subtract, Alu.mult)
                vU1 = self.cV.n
                ld_mark[t] = vU1
                self.wait("tensor", self.cV, vU1)
                for s in range(NSUB):
                    self.use_bank(4 + s)
                for s in range(NSUB):
                    for g in range(NG):
                        self.tr(self.pb16[4 + s][:, g * 128:(g + 1) * 128],
                                u1_tm[:, s, g * 128:(g + 1) * 128],
                                self.ident16[:])
                u1T_mark[t] = self.cT.n
                self.wait("vector", self.cT, u1T_mark[t])
                for s in range(NSUB):
                    self.vcopy(
                        u1buf[:, :, s * 128:(s + 1) * 128],
                        self.pb16[4 + s].rearrange("p (g c) -> p g c", g=NG))
                    self.guard_bank(4 + s, "vector")
                vU1e = self.cV.n
                self.wait("scalar", self.cV, vU1e)
                u1_spill[t] = self.dma("scalar", self.s_u1[t],
                                       u1buf.rearrange("p g t -> p (g t)"))

    def p3(self):
        C, cv = self, self.cvr
        nc = self.nc
        with ExitStack() as ph:
            psb = lambda n, s, d=F32: ph.enter_context(nc.sbuf_tensor(n, s, d))
            shg = psb("shg", [128, NG, D], BF16)
            shu = psb("shu", [128, NG, D], BF16)
            shd = shg
            u1q = psb("u1q", [128, NG, QTR], BF16)
            acc = psb("acc", [128, NG, QTR])
            gssh = psb("gssh", [128, NG, QTR], BF16)
            wge = psb("wge0", [128, NG, DI], BF16)
            wue = psb("wue0", [128, NG, DI], BF16)
            wde = psb("wde0", [128, 4, D], BF16)
            gs16 = psb("gs16", [128, 4, QTR], BF16)
            wb_sb = psb("wb_sb", [128, QTR])
            sc_all = psb("sc_all", [128, 8, NE])
            wmat = psb("wmat", [128, 8, NE])
            wfm = psb("wfm", [32, QTR])
            wfm16 = psb("wfm16", [32, QTR], BF16)
            wrow = psb("wrow", [1, QTR], BF16)
            gsum8 = psb("gsum8", [128, 8])
            m8 = psb("m8", [128, 8])
            msk = psb("msk", [128, NE])
            stmp = psb("stmp", [128, 4, TT])
            utmp = psb("utmp", [128, 4, TT])
            dtmp = psb("dtmp", [128, 2, TT])
            x1_tm = psb("x1b_tm", [128, NSUB, D])
            flat = lambda a: a.ap().rearrange("p a b -> p (a b)")
            self.memset(gsum8[:, 4:8], -1e30)
            self.wait("sync", self.cG, self.agw_done)
            self.wait("gpsimd", self.cG, self.agw_done)

            self._ga_mark = 0
            self._a_done = 0
            self._d_done = 0
            self._shg_use = 0
            self._wrow_use = 0
            u1q_use = 0
            ylast = 0
            for q in range(NQ):
                self.wait("sync", self.cT, u1q_use)
                uq = self.dma("sync", u1q[:, :, 0:TT], self.s_u1[2 * q])
                uq = self.dma("sync", u1q[:, :, TT:QTR], self.s_u1[2 * q + 1])
                self.wait("tensor", self.dS, uq)

                for i in range(8):
                    self.use_bank(3)
                    for ic in range(NG):
                        self.mm(self.pb[3][:, i * 64:i * 64 + NE],
                                u1q[:, ic, i * 128:(i + 1) * 128],
                                self.gwT[:, ic, :], ic == 0, ic == NG - 1)
                tLG = self.cT.n
                self.wait("scalar", self.cT, tLG)
                for i in range(8):
                    self.act(sc_all[:, i, :], self.pb[3][:, i * 64:i * 64 + NE],
                             Act.Tanh, scale=0.5)
                self.guard_bank(3, "scalar")
                sLG = self.cS.n
                self.wait("vector", self.cS, sLG)
                self.ts("vector", flat(sc_all), flat(sc_all), 0.5, 0.5,
                        Alu.mult, Alu.add)
                for i in range(8):
                    sc = sc_all[:, i, :]
                    wm = wmat[:, i, :]
                    self.tt("vector", wm, sc, self.gbias_bc[:], Alu.add)
                    self.vsync()
                    for grp in range(4):
                        self.sched("vector", lambda e, grp=grp, wm=wm: e.max(
                            out=m8[:], in_=wm[:, grp * 8:(grp + 1) * 8]),
                            inc=self.cV)
                        self.vsync()
                        self.tt("vector", gsum8[:, grp:grp + 1], m8[:, 0:1],
                                m8[:, 1:2], Alu.add)
                        self.vsync()
                    self.sched("vector", lambda e: e.max(
                        out=m8[:], in_=gsum8[:]), inc=self.cV)
                    self.vsync()
                    self.sched("vector", lambda e: e.tensor_single_scalar(
                        out=gsum8[:, 0:4], in_=gsum8[:, 0:4],
                        scalar=m8[:, 1:2], op=Alu.is_ge), inc=self.cV)
                    self.vsync()
                    self.tt("vector",
                            msk.ap().rearrange("p (g e) -> p g e", g=4),
                            gsum8.ap()[:, 0:4].unsqueeze(2).to_broadcast(
                                [128, 4, 8]),
                            wm.rearrange("p (g e) -> p g e", g=4), Alu.mult)
                    self.vsync()
                    self.sched("vector", lambda e: e.max(
                        out=m8[:], in_=msk[:]), inc=self.cV)
                    self.vsync()
                    self.sched("vector", lambda e: e.tensor_single_scalar(
                        out=msk[:], in_=msk[:], scalar=m8[:, 7:8],
                        op=Alu.is_ge), inc=self.cV)
                    self.vsync()
                    self.tt("vector", wm, sc, msk[:], Alu.mult)
                    self.vsync()
                    self.sched("vector", lambda e, wm=wm: e.reduce_sum(
                        out=m8[:, 0:1], in_=wm, axis=mybir.AxisListType.X),
                        inc=self.cV)
                    self.vsync()
                    self.ts("vector", m8[:, 0:1], m8[:, 0:1], 1e-20, None,
                            Alu.add)
                    self.vsync()
                    self.sched("vector", lambda e: e.reciprocal(
                        out=m8[:, 1:2], in_=m8[:, 0:1]), inc=self.cV)
                    self.vsync()
                    self.ts("vector", m8[:, 1:2], m8[:, 1:2], 2.5, None,
                            Alu.mult)
                    self.vsync()
                    self.sched("vector", lambda e, wm=wm: e.tensor_single_scalar(
                        out=wm, in_=wm, scalar=m8[:, 1:2], op=Alu.mult),
                        inc=self.cV)
                    self.vsync()
                vGate = self.cV.n
                self.wait("tensor", self.cV, vGate)
                self.use_bank(2)
                self.use_bank(3)
                for i in range(8):
                    bk = 2 + i // 4
                    self.tr(self.pb[bk][0:NE, (i % 4) * 128:(i % 4) * 128 + 128],
                            wmat[:, i, :], self.ident32[:])
                tWM = self.cT.n
                self.wait("vector", self.cT, tWM)
                for i in range(8):
                    bk = 2 + i // 4
                    self.vcopy(wfm[:, i * 128:(i + 1) * 128],
                               self.pb[bk][0:NE, (i % 4) * 128:(i % 4) * 128 + 128])
                self.guard_bank(2, "vector")
                self.guard_bank(3, "vector")
                self.vcopy(wfm16[:], wfm[:])
                vWF = self.cV.n

                self.wait("sync", self.cT, self._shg_use)
                self.dma("sync", flat(shg), self.s_shT[0])
                shLD = self.dma("sync", flat(shu), self.s_shT[1])
                self.wait("tensor", self.dS, shLD)
                sh_mark = [0, 0, 0, 0]
                for tk in range(2):
                    for oc in range(NG):
                        b0 = (oc % 2) * 2
                        b1 = b0 + 1
                        self.use_bank(b0)
                        self.use_bank(b1)
                        for ic in range(NG):
                            self.mm(self.pb[b0][:, 0:TT],
                                    shg[:, ic, oc * 128:(oc + 1) * 128],
                                    u1q[:, ic, tk * TT:(tk + 1) * TT],
                                    ic == 0, ic == NG - 1)
                        for ic in range(NG):
                            self.mm(self.pb[b1][:, 0:TT],
                                    shu[:, ic, oc * 128:(oc + 1) * 128],
                                    u1q[:, ic, tk * TT:(tk + 1) * TT],
                                    ic == 0, ic == NG - 1)
                        tMM = self.cT.n
                        self.wait("scalar", self.cT, tMM)
                        self.wait("scalar", self.cV, sh_mark[oc % 4])
                        self.act(stmp[:, oc % 4, :], self.pb[b0][:, 0:TT],
                                 Act.Silu)
                        self.guard_bank(b0, "scalar")
                        sSi = self.cS.n
                        self.wait("vector", self.cS, sSi)
                        self.tt("vector", utmp[:, oc % 4, :],
                                self.pb[b1][:, 0:TT], stmp[:, oc % 4, :],
                                Alu.mult)
                        self.guard_bank(b1, "vector")
                        self.vcopy(gssh[:, oc, tk * TT:(tk + 1) * TT],
                                   utmp[:, oc % 4, :])
                        sh_mark[oc % 4] = self.cV.n
                vSh = self.cV.n
                guDone = self.cT.n
                self.wait("sync", self.cT, guDone)
                shdLD = self.dma("sync", flat(shd), self.s_shT[2])
                self.wait("tensor", self.cV, vSh)
                self.wait("tensor", self.dS, shdLD)
                for tk in range(2):
                    for oc in range(NG):
                        bank = oc % 2
                        self.use_bank(bank)
                        for ic in range(NG):
                            self.mm(self.pb[bank][:, 0:TT],
                                    shd[:, ic, oc * 128:(oc + 1) * 128],
                                    gssh[:, ic, tk * TT:(tk + 1) * TT],
                                    ic == 0, ic == NG - 1)
                        tMM = self.cT.n
                        self.wait("vector", self.cT, tMM)
                        self.vcopy(acc[:, oc, tk * TT:(tk + 1) * TT],
                                   self.pb[bank][:, 0:TT])
                        self.guard_bank(bank, "vector")
                self._shg_use = self.cT.n

                dn_marks = {}
                vShD = self.cV.n
                self.wait("gpsimd", self.cV, vShD)
                ga_mark = getattr(self, "_ga_mark", 0)
                for e in range(NE):
                    self.wait("sync", self.cT, self._a_done)
                    for part, dst in ((0, wge), (1, wue)):
                        src = self.ccw_out[e // EPL,
                                           ((e % EPL) * 3 + part) * EMAT:
                                           ((e % EPL) * 3 + part + 1) * EMAT]
                        ldGU = self.dma("sync", flat(dst),
                                        src.rearrange("(p f) -> p f", p=128))
                    self.wait("sync", self.cT, self._d_done)
                    srcd = self.ccw_out[e // EPL,
                                        ((e % EPL) * 3 + 2) * EMAT:
                                        ((e % EPL) * 3 + 3) * EMAT]
                    ldD = self.dma("sync", flat(wde),
                                   srcd.rearrange("(p f) -> p f", p=128))
                    self.wait("scalar", self.cV, vWF)
                    self.wait("scalar", self.cT, self._wrow_use)
                    wrm = self.dma("scalar", wrow[:], wfm16[e:e + 1, :])
                    self.wait("tensor", self.cV, vWF)
                    self.wait("tensor", self.dA, wrm)
                    self.use_bank(6)
                    self.use_bank(7)
                    self.mm(self.pb[6][:, 0:TT], self.ones16[:],
                            wrow[0:1, 0:TT], True, True)
                    self.mm(self.pb[7][:, 0:TT], self.ones16[:],
                            wrow[0:1, TT:QTR], True, True)
                    self._wrow_use = self.cT.n
                    tWB = self.cT.n
                    self.wait("vector", self.cT, tWB)
                    self.vcopy(wb_sb[:, 0:TT], self.pb[6][:, 0:TT])
                    self.vcopy(wb_sb[:, TT:QTR], self.pb[7][:, 0:TT])
                    self.guard_bank(6, "vector")
                    self.guard_bank(7, "vector")
                    vWB = self.cV.n

                    self.wait("tensor", self.dS, ldGU)
                    self.wait("tensor", self.cV, vWB)
                    for tk in range(2):
                        for oc in range(4):
                            self.use_bank(oc)
                            for ic in range(NG):
                                self.mm(self.pb[oc][:, 0:TT],
                                        wge[:, ic, oc * 128:(oc + 1) * 128],
                                        u1q[:, ic, tk * TT:(tk + 1) * TT],
                                        ic == 0, ic == NG - 1)
                        for oc in range(4):
                            self.use_bank(4 + oc)
                            for ic in range(NG):
                                self.mm(self.pb[4 + oc][:, 0:TT],
                                        wue[:, ic, oc * 128:(oc + 1) * 128],
                                        u1q[:, ic, tk * TT:(tk + 1) * TT],
                                        ic == 0, ic == NG - 1)
                        tGU = self.cT.n
                        self.wait("scalar", self.cT, tGU)
                        self.wait("scalar", self.cV, ga_mark)
                        for oc in range(4):
                            self.act(stmp[:, oc, :], self.pb[oc][:, 0:TT],
                                     Act.Silu)
                            self.guard_bank(oc, "scalar")
                        sGU = self.cS.n
                        self.wait("vector", self.cS, sGU)
                        for oc in range(4):
                            self.tt("vector", utmp[:, oc, :],
                                    self.pb[4 + oc][:, 0:TT],
                                    wb_sb[:, tk * TT:(tk + 1) * TT], Alu.mult)
                            self.guard_bank(4 + oc, "vector")
                        for oc in range(4):
                            self.tt("vector", gs16[:, oc, tk * TT:(tk + 1) * TT],
                                    stmp[:, oc, :], utmp[:, oc, :], Alu.mult)
                        ga_mark = self.cV.n
                    self._ga_mark = ga_mark
                    self._a_done = self.cT.n
                    vGS = self.cV.n
                    self.wait("tensor", self.cV, vGS)
                    self.wait("tensor", self.dS, ldD)
                    for tk in range(2):
                        for oc in range(NG):
                            bank = oc % 2
                            self.use_bank(bank)
                            for ic in range(4):
                                self.mm(self.pb[bank][:, 0:TT],
                                        wde[:, ic, oc * 128:(oc + 1) * 128],
                                        gs16[:, ic, tk * TT:(tk + 1) * TT],
                                        ic == 0, ic == 3)
                            tDN = self.cT.n
                            self.wait("scalar", self.cT, tDN)
                            dslot = oc % 2
                            self.wait("scalar", self.dG,
                                      dn_marks.get(dslot, 0))
                            self.scopy(dtmp[:, dslot, :], self.pb[bank][:, 0:TT])
                            self.guard_bank(bank, "scalar")
                            sCp = self.cS.n
                            self.wait("gpsimd", self.cS, sCp)
                            dn_marks[dslot] = self.dma(
                                "gpsimd", acc[:, oc, tk * TT:(tk + 1) * TT],
                                dtmp[:, dslot, :], accum_op=Alu.add)
                    self._d_done = self.cT.n

                self.wait("vector", self.dG)
                u1q_use = self.cT.n
                for tk in range(2):
                    self.wait("scalar", self.dG, ylast)
                    xm = self.dma("scalar", x1_tm[:],
                                  self.s_x1[q * QTR + tk * TT:
                                            q * QTR + (tk + 1) * TT].rearrange(
                                      "(s p) h -> p s h", p=128))
                    self.wait("vector", self.dA, xm)
                    self.wait("tensor", self.dA, xm)
                    for s in range(NSUB):
                        b0, b1 = 4 + 2 * (s % 2), 5 + 2 * (s % 2)
                        self.use_bank(b0)
                        self.use_bank(b1)
                        for g in range(NG):
                            bank = b0 if g < 4 else b1
                            self.tr(self.pb[bank].ap()[:, (g % 4) * 128:
                                                       (g % 4) * 128 + 128],
                                    acc[:, g, tk * TT + s * 128:
                                        tk * TT + (s + 1) * 128],
                                    self.ident32[:])
                        tS = self.cT.n
                        self.wait("vector", self.cT, tS)
                        self.tt("vector", x1_tm[:, s, 0:512], self.pb[b0][:],
                                x1_tm[:, s, 0:512], Alu.add)
                        self.tt("vector", x1_tm[:, s, 512:1024], self.pb[b1][:],
                                x1_tm[:, s, 512:1024], Alu.add)
                        self.guard_bank(b0, "vector")
                        self.guard_bank(b1, "vector")
                    vOut = self.cV.n
                    self.wait("gpsimd", self.cV, vOut)
                    ylast = self.dma(
                        "gpsimd",
                        self.y[q * QTR + tk * TT:q * QTR + (tk + 1) * TT].rearrange(
                            "(s p) h -> p s h", p=128), x1_tm[:])

    def finish(self):
        for name, (tap, src) in self.tap_outs.items():
            self.wait("gpsimd", self.dA)
            self.wait("gpsimd", self.dS)
            self.dma("gpsimd", tap, src)
        self.wait("gpsimd", self.dG)
        self.wait("sync", self.dG)


_BUILT = {}


def _get_nc(taps=()):
    key = tuple(sorted(taps))
    if key not in _BUILT:
        mk = MK(taps)
        nc = mk.build()
        _BUILT[key] = (nc, mk.tap_outs)
    return _BUILT[key]


def make_in_maps(inputs):
    x = np.ascontiguousarray(np.asarray(inputs["L0_attn_x"],
                                        dtype=np.float32).reshape(-1, D))
    g = lambda n: np.ascontiguousarray(np.asarray(inputs[n], np.float32))
    vec = lambda n: g(n).reshape(1, D)
    common = {
        "w_ln_att_g": vec("L0_attn_ln_att_g"),
        "w_ln_att_b": vec("L0_attn_ln_att_b"),
        "w_ln_ffn_g": vec("L0_attn_ln_ffn_g"),
        "w_ln_ffn_b": vec("L0_attn_ln_ffn_b"),
        "w_td": vec("L0_attn_att_time_decay"),
        "w_tf": vec("L0_attn_att_time_first"),
        "w_mk": vec("L0_attn_att_mix_k"),
        "w_mv": vec("L0_attn_att_mix_v"),
        "w_mr": vec("L0_attn_att_mix_r"),
        "w_fmk": vec("L0_attn_ffn_mix_k"),
        "w_fmr": vec("L0_attn_ffn_mix_r"),
        "w_wk": g("L0_attn_att_wk"),
        "w_wv": g("L0_attn_att_wv"),
        "w_wr": g("L0_attn_att_wr"),
        "w_wo": g("L0_attn_att_wo"),
        "w_fwk": g("L0_attn_ffn_wk"),
        "w_fwv": g("L0_attn_ffn_wv"),
        "w_fwr": g("L0_attn_ffn_wr"),
        "w_gw": g("L0_ffn_gate_w"),
        "w_gb": g("L0_ffn_gate_bias").reshape(1, NE),
        "w_sg": g("L0_ffn_shared_gate"),
        "w_su": g("L0_ffn_shared_up"),
        "w_sd": g("L0_ffn_shared_down"),
    }
    eg = g("L0_ffn_expert_gate")
    eu = g("L0_ffn_expert_up")
    ed = g("L0_ffn_expert_down")
    maps = []
    for c in range(NCORES):
        seq, pos = c // 4, c % 4
        a8 = np.full(8, 1e9, np.float32)
        sel8 = np.zeros((8, 1), np.float32)
        for j in range(NCORES):
            if j // 4 == seq and j < c:
                a8[j] = float(TPC) * (c - 1 - j)
        if pos > 0:
            sel8[c - 1, 0] = 1.0
            xp = x[c * TPC - 1:c * TPC]
            pm = 1.0
        else:
            xp = np.zeros((1, D), np.float32)
            pm = 0.0
        m = dict(common)
        m["xs"] = x[c * TPC:(c + 1) * TPC]
        m["xp"] = np.ascontiguousarray(xp)
        m["a8"] = a8.reshape(1, 8)
        m["sel8"] = sel8
        m["pmask"] = np.array([[pm]], np.float32)
        m["w_meg"] = np.ascontiguousarray(eg[c * EPL:(c + 1) * EPL])
        m["w_meu"] = np.ascontiguousarray(eu[c * EPL:(c + 1) * EPL])
        m["w_med"] = np.ascontiguousarray(ed[c * EPL:(c + 1) * EPL])
        maps.append(m)
    return maps


def kernel(**inputs):
    nc, _ = _get_nc()
    maps = make_in_maps(inputs)
    res = run_bass_kernel_spmd(nc, maps, core_ids=list(range(NCORES)))
    out = np.concatenate([res.results[c]["y"] for c in range(NCORES)], axis=0)
    return out.reshape(B, L, D)


# version 03
# speedup vs baseline: 1.0126x; 1.0126x over previous
import numpy as np
from contextlib import ExitStack

import concourse.bass as bass
import concourse.mybir as mybir
from concourse.bass_utils import run_bass_kernel_spmd

F32 = mybir.dt.float32
BF16 = mybir.dt.bfloat16
Alu = mybir.AluOpType
Act = mybir.ActivationFunctionType

NCORES = 8
B, L, D = 2, 16384, 1024
TPC = 4096
NT = 8
TT = 512
NSUB = 4
NG = 8
NE = 32
EPL = 4
DI = 512
EPS = 1e-5
RG = [[0, 1, 2, 3, 4, 5, 6, 7]]
EMAT = DI * D
WSH = EPL * 3 * EMAT
QTR = 1024
NQ = TPC // QTR
ENGS = ("tensor", "vector", "scalar", "sync", "gpsimd")


class Sem:
    def __init__(self, nc, es, name):
        self.h = es.enter_context(nc.semaphore(name))
        self.n = 0


class MK:
    def __init__(self, taps=()):
        self.taps = taps
        self.nc = bass.Bass("TRN2", target_bir_lowering=False, debug=False)
        self.es = ExitStack()
        self.ops = {e: [] for e in ENGS}
        self.tap_outs = {}
        self.bank_guard = {}

    def sched(self, eng, fn, inc=None, k=1):
        if inc is not None:
            inc.n += k

            def thunk(e, fn=fn, inc=inc, k=k):
                fn(e).then_inc(inc.h, k)
        else:
            def thunk(e, fn=fn):
                fn(e)
        self.ops[eng].append(thunk)

    def wait(self, eng, sem, val=None):
        v = sem.n if val is None else val
        self.ops[eng].append(lambda e, h=sem.h, v=v: e.wait_ge(h, v))

    def dma(self, eng, out, in_, **kw):
        sem = {"sync": self.dS, "scalar": self.dA, "gpsimd": self.dG}[eng]
        self.sched(eng, lambda e, o=out, i=in_, kw=kw: e.dma_start(out=o, in_=i, **kw),
                   inc=sem, k=16)
        return sem.n

    def barrier(self):
        self.wait("sync", self.dS)
        self.wait("scalar", self.dA)
        self.wait("gpsimd", self.dG)
        marks = {}
        for e in ENGS:
            s = self.fence[e]
            self.sched(e, lambda en: en.nop(), inc=s)
            marks[e] = s.n
        for e in ENGS:
            for o in ENGS:
                if o != e:
                    self.wait(e, self.fence[o], marks[o])

    def emit(self, blk):
        for e in ENGS:
            getattr(blk, e)(lambda eng, e=e: [f(eng) for f in self.ops[e]])

    def csem(self, eng):
        return {"vector": self.cV, "scalar": self.cS, "gpsimd": self.cG}[eng]

    def ts(self, eng, out, in0, s1, s2, op0, op1=Alu.bypass, accum=None):
        self.sched(eng, lambda e: e.tensor_scalar(
            out=out, in0=in0, scalar1=s1, scalar2=s2, op0=op0, op1=op1,
            accum_out=accum), inc=self.csem(eng))

    def tt(self, eng, out, in0, in1, op):
        self.sched(eng, lambda e: e.tensor_tensor(out=out, in0=in0, in1=in1, op=op),
                   inc=self.csem(eng))

    def stt(self, eng, out, in0, scalar, in1, op0, op1):
        self.sched(eng, lambda e: e.scalar_tensor_tensor(
            out=out, in0=in0, scalar=scalar, in1=in1, op0=op0, op1=op1),
            inc=self.csem(eng))

    def act(self, out, in_, func, bias=0.0, scale=1.0, accum=None):
        self.sched("scalar", lambda e: e.activation(
            out=out, in_=in_, func=func, bias=bias, scale=scale,
            accum_out=accum), inc=self.cS)

    def scopy(self, out, in_):
        self.sched("scalar", lambda e: e.copy(out=out, in_=in_), inc=self.cS)

    def vcopy(self, out, in_):
        self.sched("vector", lambda e: e.tensor_copy(out=out, in_=in_), inc=self.cV)

    def memset(self, ap, val):
        self.sched("vector", lambda e: e.memset(ap, val), inc=self.cV)

    def mm(self, out, lhsT, rhs, start, stop):
        self.sched("tensor", lambda e: e.matmul(
            out=out, lhsT=lhsT, rhs=rhs, start=start, stop=stop), inc=self.cT)

    def tr(self, out, in_, ident):
        self.sched("tensor", lambda e: e.transpose(
            out=out, in_=in_, identity=ident), inc=self.cT)

    def use_bank(self, bank):
        g = self.bank_guard.get(bank)
        if g is not None:
            sem, mark = g
            self.wait("tensor", sem, mark)

    def guard_bank(self, bank, eng):
        self.bank_guard[bank] = (self.csem(eng), self.csem(eng).n)

    def vsync(self):
        self.wait("vector", self.cV)

    def ln_finish(self, s_sum, s_sq, mean, rstd, tmp, nelem=D):
        inv = 1.0 / nelem
        self.ts("vector", mean, s_sum, inv, None, Alu.mult)
        self.ts("vector", rstd, s_sq, inv, None, Alu.mult)
        self.vsync()
        self.tt("vector", tmp, mean, mean, Alu.mult)
        self.vsync()
        self.tt("vector", rstd, rstd, tmp, Alu.subtract)
        self.vsync()
        self.ts("vector", rstd, rstd, EPS, None, Alu.add)
        vm = self.cV.n
        self.wait("scalar", self.cV, vm)
        self.act(rstd, rstd, Act.Ln)
        self.act(rstd, rstd, Act.Exp, scale=-0.5)

    def build(self):
        nc, es = self.nc, self.es
        dram = lambda n, s, k="ExternalInput", d=F32: nc.dram_tensor(
            n, s, d, kind=k).ap()
        self.sbuf = lambda n, s, d=F32: es.enter_context(nc.sbuf_tensor(n, s, d))
        sbuf = self.sbuf

        self.xs = dram("xs", [TPC, D])
        self.xp = dram("xp", [1, D])
        self.a8 = dram("a8", [1, 8])
        self.sel8 = dram("sel8", [8, 1])
        self.pmask = dram("pmask", [1, 1])
        names = ["ln_att_g", "ln_att_b", "ln_ffn_g", "ln_ffn_b", "td", "tf",
                 "mk", "mv", "mr", "fmk", "fmr"]
        self.wvec = {n: dram("w_" + n, [1, D]) for n in names}
        self.w_wk = dram("w_wk", [D, D])
        self.w_wv = dram("w_wv", [D, D])
        self.w_wr = dram("w_wr", [D, D])
        self.w_wo = dram("w_wo", [D, D])
        self.w_fwk = dram("w_fwk", [D, 4 * D])
        self.w_fwv = dram("w_fwv", [4 * D, D])
        self.w_fwr = dram("w_fwr", [D, D])
        self.w_gw = dram("w_gw", [NE, D])
        self.w_gb = dram("w_gb", [1, NE])
        self.w_meg = dram("w_meg", [EPL, DI, D])
        self.w_meu = dram("w_meu", [EPL, DI, D])
        self.w_med = dram("w_med", [EPL, D, DI])
        self.w_sg = dram("w_sg", [D, D])
        self.w_su = dram("w_su", [D, D])
        self.w_sd = dram("w_sd", [D, D])
        self.y = dram("y", [TPC, D], k="ExternalOutput")

        self.s_num = dram("s_num", [NT, 128, NG * TT], k="Internal")
        self.s_den = dram("s_den", [NT, 128, NG * TT], k="Internal")
        self.s_r = dram("s_r", [NT, 128, NG * TT], k="Internal", d=BF16)
        self.s_u0 = dram("s_u0", [TPC, D], k="Internal")
        self.s_xb = dram("s_xb", [TPC, D], k="Internal")
        self.s_x1 = dram("s_x1", [TPC, D], k="Internal")
        self.s_u1 = dram("s_u1", [NT, 128, NG * TT], k="Internal", d=BF16)
        self.s_kf = dram("s_kf", [NT, 128, 32 * TT], k="Internal", d=BF16)
        self.s_rf = dram("s_rf", [NT, 128, NG * TT], k="Internal", d=BF16)
        self.s_shT = dram("s_shT", [3, 128, NG * D], k="Internal", d=BF16)
        self.cc1_in = dram("cc1_in", [1, 2 * D], k="Internal")
        self.cc1_out = dram("cc1_out", [8, 2 * D], k="Internal")
        self.cc2_in = dram("cc2_in", [1, D], k="Internal")
        self.cc2_out = dram("cc2_out", [8, D], k="Internal")
        self.ccw_in = dram("ccw_in", [1, WSH], k="Internal", d=BF16)
        self.ccw_out = dram("ccw_out", [8, WSH], k="Internal", d=BF16)

        if "h2" in self.taps:
            self.s_h2 = dram("s_h2", [NT, 128, NG * TT], k="Internal", d=BF16)
            self.s_xk2 = dram("s_xk2", [NT, 128, NG * TT], k="Internal", d=BF16)
        for t in self.taps:
            src = {"num": self.s_num, "den": self.s_den, "r": self.s_r,
                   "u0": self.s_u0, "xb": self.s_xb, "x1": self.s_x1,
                   "u1": self.s_u1, "kf": self.s_kf, "rf": self.s_rf,
                   "h2": getattr(self, "s_h2", None),
                   "xk2": getattr(self, "s_xk2", None)}[t]
            self.tap_outs[t] = (nc.dram_tensor(
                "tap_" + t, list(src.shape), src.dtype,
                kind="ExternalOutput").ap(), src)

        self.c_id = nc.inline_tensor(np.eye(128, dtype=np.float32), "c_id").ap()
        self.c_ramp = nc.inline_tensor(
            np.arange(TT, dtype=np.float32)[None, :], "c_ramp").ap()
        self.c_ones = nc.inline_tensor(
            np.ones((1, 128), dtype=np.float32), "c_ones").ap()

        self.ident32 = sbuf("ident32", [128, 128])
        self.ident16 = sbuf("ident16", [128, 128], BF16)
        self.ones32 = sbuf("ones32", [1, 128])
        self.ones16 = sbuf("ones16", [1, 128], BF16)
        self.ramp01 = sbuf("ramp01", [128, TT])
        self.rampsb = sbuf("rampsb", [1, TT])
        NCON = 18
        self.cvr = sbuf("cfm", [128, NCON, NG])
        (self.C_MK, self.C_MV, self.C_MR, self.C_FMK, self.C_FMR, self.C_LAG,
         self.C_LAB, self.C_LFG, self.C_LFB, self.C_TD, self.C_TF, self.C_W,
         self.C_EW, self.C_EU, self.C_PIN, self.C_QIN, self.C_WTB,
         self.C_TMP) = range(NCON)
        self.a8sb = sbuf("a8sb", [1, 8])
        self.a8bc = sbuf("a8bc", [128, 8])
        self.pmask_sb = sbuf("pmask_sb", [1, 1])
        self.sel8_sb = sbuf("sel8_sb", [8, 1])
        self.gwT = sbuf("gwT", [128, NG, NE], BF16)
        self.gbias_row = sbuf("gbias_row", [1, NE])
        self.gbias_bc = sbuf("gbias_bc", [128, NE])
        self.bnd_h1 = sbuf("bnd_h1", [128, NG, 1], BF16)
        self.bnd_h2 = sbuf("bnd_h2", [128, NG, 1], BF16)
        self.row_a = sbuf("row_a", [1, D])
        self.row_b = sbuf("row_b", [1, D])
        self.row_gx = sbuf("row_gx", [1, D])
        self.row_bx = sbuf("row_bx", [1, D])
        self.row16 = sbuf("row16", [1, D], BF16)
        self.rst = sbuf("rst", [1, 8])
        self.pq_j = sbuf("pq_j", [128, 8, 16])
        self.coefs = sbuf("coefs", [128, NG, 8])

        self.pb = [es.enter_context(nc.psum_tensor(f"pb{i}", [128, 512], F32))
                   for i in range(8)]
        self.pb16 = [p.ap().bitcast(BF16) for p in self.pb]

        for s in ("dG", "dS", "dA", "cT", "cV", "cS", "cG"):
            setattr(self, s, Sem(nc, es, s))
        self.fence = {e: Sem(nc, es, "f_" + e) for e in ENGS}

        blk = es.enter_context(nc.Block())

        with ExitStack() as stg_scope:
            pst = lambda n, s, d=BF16: stg_scope.enter_context(
                nc.sbuf_tensor(n, s, d))
            self.stg = [pst("stg0", [128, 4096])]
            self.wtb = pst("wtb", [128, 4096])

            self.p0()
            self.p1a_and_exchange()
            self.barrier()
            self.p1b()
            self.barrier()
            self.p2a()
            self.barrier()
            self.p2b()
            self.barrier()
        self.p3()
        self.finish()

        self.emit(blk)
        es.close()
        return nc

    def load_cvec(self, col, src):
        self.dma("gpsimd", self.cvr[:, col, :],
                 src.rearrange("o (g p) -> (p o) g", p=128),
                 allow_slow_non_contiguous=True)

    def p0(self):
        C, cv = self, self.cvr
        self.dma("sync", self.ident32[:], self.c_id)
        self.dma("sync", self.ones32[:], self.c_ones)
        self.dma("sync", self.rampsb[:], self.c_ramp)
        self.dma("sync", self.gbias_row[:], self.w_gb)
        self.dma("gpsimd", self.ident16[:], self.c_id)
        self.dma("gpsimd", self.ones16[:], self.c_ones)
        self.dma("gpsimd", self.a8sb[:], self.a8)
        self.dma("gpsimd", self.pmask_sb[:], self.pmask)
        self.dma("gpsimd", self.sel8_sb[:], self.sel8)
        self.dma("gpsimd", self.row_gx[:], self.wvec["ln_att_g"])
        self.dma("gpsimd", self.row_bx[:], self.wvec["ln_att_b"])
        for col, src in ((C.C_MK, "mk"), (C.C_MV, "mv"), (C.C_MR, "mr"),
                         (C.C_FMK, "fmk"), (C.C_FMR, "fmr"),
                         (C.C_LAG, "ln_att_g"), (C.C_LAB, "ln_att_b"),
                         (C.C_LFG, "ln_ffn_g"), (C.C_LFB, "ln_ffn_b"),
                         (C.C_TD, "td"), (C.C_TF, "tf")):
            self.load_cvec(col, self.wvec[src])
        self.dma("gpsimd", self.row_a[:], self.xp)
        gLD = self.dG.n
        self.gLD = gLD
        self.dS_consts = self.dS.n

        self.wait("scalar", self.dG, gLD)
        self.act(cv[:, C.C_W, :], cv[:, C.C_TD, :], Act.Exp)
        sW = self.cS.n
        self.wait("vector", self.cS, sW)
        self.ts("vector", cv[:, C.C_W, :], cv[:, C.C_W, :], -1.0, None, Alu.mult)
        vW = self.cV.n
        self.wait("scalar", self.cV, vW)
        self.act(cv[:, C.C_EW, :], cv[:, C.C_W, :], Act.Exp)
        self.act(cv[:, C.C_EU, :], cv[:, C.C_TF, :], Act.Exp)

        self.wait("tensor", self.dS)
        self.wait("tensor", self.dG, gLD)
        self.mm(self.pb[6][:, 0:TT], self.ones32[:], self.rampsb[:], True, True)
        self.mm(self.pb[7][:, 0:8], self.ones32[:], self.a8sb[:], True, True)
        self.mm(self.pb[7][:, 8:8 + NE], self.ones32[:], self.gbias_row[:],
                True, True)
        tBC = self.cT.n
        self.wait("vector", self.cT, tBC)
        self.vcopy(self.ramp01[:], self.pb[6][:, 0:TT])
        self.vcopy(self.a8bc[:], self.pb[7][:, 0:8])
        self.vcopy(self.gbias_bc[:], self.pb[7][:, 8:8 + NE])

        for b in (6, 7):
            self.guard_bank(b, "vector")
        self.pb_free_v = self.cV.n

    def build_gwT(self, gw16):
        self.wait("tensor", self.dG)
        for hc in range(NG):
            self.tr(self.pb16[5][:, hc * 64:hc * 64 + NE],
                    gw16[:, hc * 128:(hc + 1) * 128],
                    self.ident16[0:NE, 0:NE])
        tGW = self.cT.n
        self.wait("vector", self.cT, tGW)
        for hc in range(NG):
            self.vcopy(self.gwT[:, hc, :], self.pb16[5][:, hc * 64:hc * 64 + NE])
        self.guard_bank(5, "vector")

    def p0_staging(self):
        mats = []
        for el in range(EPL):
            mats.append((self.w_meg[el], 4, 8,
                         ("ccw", (el * 3 + 0) * EMAT, EMAT)))
            mats.append((self.w_meu[el], 4, 8,
                         ("ccw", (el * 3 + 1) * EMAT, EMAT)))
            mats.append((self.w_med[el], 8, 4,
                         ("ccw", (el * 3 + 2) * EMAT, EMAT)))
        for mi, src in enumerate((self.w_sg, self.w_su, self.w_sd)):
            for half in range(2):
                mats.append((src[:, half * 512:(half + 1) * 512], 8, 4,
                             ("shT", mi, half)))
        store_mark = 0
        for i, (src, nrb, ncb, dest) in enumerate(mats):
            stg = self.stg[0]
            if i >= 2:
                pass
            ldm = self.dma(
                "gpsimd",
                stg.ap().rearrange("p (rb c) -> p rb c", rb=nrb),
                src.rearrange("(rb p) c -> p rb c", p=128))
            self.wait("sync", self.dG, ldm)
            if i >= 1:
                self.wait("sync", self.dG, store_mark)
            for rb in range(nrb):
                for cb in range(ncb):
                    self.sched("sync", lambda e, rb=rb, cb=cb, nrb=nrb, ncb=ncb,
                               stg=stg: e.dma_start_transpose(
                                   out=self.wtb[:, (cb * nrb + rb) * 128:
                                                (cb * nrb + rb) * 128 + 128],
                                   in_=stg[:, (rb * ncb + cb) * 128:
                                           (rb * ncb + cb) * 128 + 128]),
                               inc=self.dS, k=16)
            trm = self.dS.n
            self.wait("gpsimd", self.dS, trm)
            if dest[0] == "ccw":
                _, off, n = dest
                store_mark = self.dma(
                    "gpsimd",
                    self.ccw_in[0, off:off + n].rearrange("(p f) -> p f", p=128),
                    self.wtb[:])
            else:
                _, mi, half = dest
                store_mark = self.dma(
                    "gpsimd", self.s_shT[mi, :, half * 4096:(half + 1) * 4096],
                    self.wtb[:])
        self.wait("gpsimd", self.dG)
        self.sched("gpsimd", lambda e: e.collective_compute(
            "AllGather", Alu.bypass, replica_groups=RG,
            ins=[self.ccw_in], outs=[self.ccw_out]), inc=self.cG)
        self.agw_done = self.cG.n

    def row_ln(self, dst_bnd, g_row, b_row):
        self.act(self.row_b[:], self.row_a[:], Act.Identity,
                 accum=self.rst[:, 0:1])
        self.act(self.row_b[:], self.row_a[:], Act.Square,
                 accum=self.rst[:, 1:2])
        sM = self.cS.n
        self.wait("vector", self.cS, sM)
        self.ln_finish(self.rst[:, 0:1], self.rst[:, 1:2], self.rst[:, 2:3],
                       self.rst[:, 3:4], self.rst[:, 4:5])
        sM2 = self.cS.n
        self.wait("vector", self.cS, sM2)
        self.ts("vector", self.row_b[:], self.row_a[:], self.rst[:, 2:3],
                self.rst[:, 3:4], Alu.subtract, Alu.mult)
        self.tt("vector", self.row_b[:], self.row_b[:], g_row, Alu.mult)
        self.tt("vector", self.row_b[:], self.row_b[:], b_row, Alu.add)
        self.ts("vector", self.row16[:], self.row_b[:],
                self.pmask_sb[0:1, 0:1], None, Alu.mult)
        vD = self.cV.n
        self.use_bank(6)
        self.wait("tensor", self.cV, vD)
        for g in range(NG):
            self.tr(self.pb16[6][:, g * 64:g * 64 + 1],
                    self.row16[0:1, g * 128:(g + 1) * 128],
                    self.ident16[0:1, 0:1])
        tD = self.cT.n
        self.wait("vector", self.cT, tD)
        for g in range(NG):
            self.vcopy(dst_bnd[:, g, 0:1], self.pb16[6][:, g * 64:g * 64 + 1])
        self.guard_bank(6, "vector")

    def p1a_and_exchange(self):
        C, cv = self, self.cvr
        nc = self.nc
        with ExitStack() as ph:
            psb = lambda n, s, d=F32: ph.enter_context(nc.sbuf_tensor(n, s, d))
            wk_sb = psb("wk_sb", [128, NG, D], BF16)
            wv_sb = psb("wv_sb", [128, NG, D], BF16)
            wr_sb = psb("wr_sb", [128, NG, D], BF16)
            x_tm = psb("x_tm", [128, NSUB, D])
            u0_tm = psb("u0_tm", [128, NSUB, D])
            st = psb("st", [128, NSUB, 10])
            h1_fm = psb("h1_fm", [128, NG, TT + 1], BF16)
            xmix = psb("xmix", [128, NG, TT], BF16)
            ek = psb("ek", [128, NG, TT])
            ekv = psb("ekv", [128, NG, TT])
            q_arr = psb("q_arr", [128, NG, TT + 1])
            p_arr = psb("p_arr", [128, NG, TT + 1])
            h1_tm = x_tm.ap().bitcast(BF16)[:, :, 0:D]
            r4 = x_tm.ap().bitcast(BF16)[:, :, D:2 * D].rearrange(
                "p a (g2 t) -> p a g2 t", g2=2)
            r16 = lambda oc: r4[:, oc // 2, oc % 2, :]
            diff = u0_tm.ap().bitcast(BF16).rearrange(
                "p a b -> p (a b)")[:, 0:NG * TT].rearrange(
                "p (g t) -> p g t", g=NG)
            flat = lambda a: a.ap().rearrange("p a b -> p (a b)")

            gw16 = psb("gw16", [NE, D], BF16)
            for dst, src in ((wk_sb, self.w_wk), (wv_sb, self.w_wv),
                             (wr_sb, self.w_wr)):
                self.dma("gpsimd", dst.ap(),
                         src.rearrange("(ic p) o -> p ic o", p=128))
            self.dma("gpsimd", gw16[:], self.w_gw)
            wLD = self.dG.n
            self.build_gwT(gw16)
            self.p0_staging()

            self.wait("scalar", self.dG, wLD)
            self.wait("vector", self.dG, wLD)
            self.wait("vector", self.dS, self.dS_consts)
            self.row_ln(self.bnd_h1, self.row_gx[:], self.row_bx[:])
            self.memset(q_arr[:, :, TT:TT + 1], 0.0)
            self.memset(p_arr[:, :, TT:TT + 1], 0.0)

            u0_spill = [0] * NT
            nd_spill = [0] * NT
            h1T_mark = [0] * NT
            mmr_mark = [0] * NT
            h1evac_mark = [0] * NT

            for t in range(NT):
                rows = self.xs[t * TT:(t + 1) * TT].rearrange(
                    "(s p) h -> p s h", p=128)
                if t > 0:
                    self.wait("scalar", self.cT, h1T_mark[t - 1])
                    self.wait("scalar", self.dA, nd_spill[t - 1])
                xm = self.dma("scalar", x_tm[:], rows)
                self.wait("scalar", self.dA, xm)
                if t > 0:
                    self.wait("scalar", self.dA, u0_spill[t - 1])
                for s in range(NSUB):
                    self.act(u0_tm[:, s, :], x_tm[:, s, :], Act.Identity,
                             accum=st[:, s, 0:1])
                    self.act(u0_tm[:, s, :], x_tm[:, s, :], Act.Square,
                             accum=st[:, s, 1:2])
                sStat = self.cS.n
                self.wait("vector", self.cS, sStat)
                self.ln_finish(st[:, :, 0], st[:, :, 1], st[:, :, 2],
                               st[:, :, 3], st[:, :, 4])
                sFin = self.cS.n
                self.wait("vector", self.cS, sFin)
                for s in range(NSUB):
                    self.ts("vector", u0_tm[:, s, :], x_tm[:, s, :],
                            st[:, s, 2:3], st[:, s, 3:4], Alu.subtract, Alu.mult)
                vU0 = self.cV.n
                self.wait("scalar", self.cV, vU0)
                u0_spill[t] = self.dma(
                    "scalar", self.s_u0[t * TT:(t + 1) * TT].rearrange(
                        "(s p) h -> p s h", p=128), u0_tm[:])
                for s in range(NSUB):
                    self.act(x_tm[:, s, :], u0_tm[:, s, :], Act.Identity,
                             accum=st[:, s, 5:6])
                    self.act(x_tm[:, s, :], u0_tm[:, s, :], Act.Square,
                             accum=st[:, s, 6:7])
                sStat2 = self.cS.n
                self.wait("vector", self.cS, sStat2)
                self.ln_finish(st[:, :, 5], st[:, :, 6], st[:, :, 7],
                               st[:, :, 8], st[:, :, 9])
                sFin2 = self.cS.n
                self.wait("vector", self.cS, sFin2)
                for s in range(NSUB):
                    self.ts("vector", h1_tm[:, s, :], u0_tm[:, s, :],
                            st[:, s, 7:8], st[:, s, 8:9], Alu.subtract, Alu.mult)
                vH1 = self.cV.n

                if t == 0:
                    self.vcopy(h1_fm[:, :, 0:1], self.bnd_h1[:])
                else:
                    self.wait("vector", self.cS, h1evac_mark[t - 1])
                    self.vcopy(h1_fm[:, :, 0:1], h1_fm[:, :, TT:TT + 1])
                self.wait("tensor", self.cV, self.cV.n)
                for s in range(NSUB):
                    self.use_bank(4 + s)
                for s in range(NSUB):
                    for g in range(NG):
                        self.tr(self.pb16[4 + s][:, g * 128:(g + 1) * 128],
                                h1_tm[:, s, g * 128:(g + 1) * 128],
                                self.ident16[:])
                h1T_mark[t] = self.cT.n
                self.wait("scalar", self.cT, h1T_mark[t])
                for s in range(NSUB):
                    for g in range(NG):
                        self.act(h1_fm[:, g, 1 + s * 128:1 + (s + 1) * 128],
                                 self.pb16[4 + s][:, g * 128:(g + 1) * 128],
                                 Act.Identity, scale=cv[:, C.C_LAG, g:g + 1],
                                 bias=cv[:, C.C_LAB, g:g + 1])
                h1evac_mark[t] = self.cS.n
                for s in range(NSUB):
                    self.guard_bank(4 + s, "scalar")

                if t > 0:
                    self.wait("vector", self.dA, nd_spill[t - 1])
                self.wait("vector", self.dA, u0_spill[t])
                self.wait("vector", self.cS, h1evac_mark[t])
                self.tt("vector", diff, h1_fm[:, :, 1:TT + 1],
                        h1_fm[:, :, 0:TT], Alu.subtract)
                if t == 0:
                    self.wait("tensor", self.dG, wLD)
                ek_marks = []
                if t > 0:
                    self.wait("scalar", self.dA, nd_spill[t - 1])
                for mat, (wsb, col) in enumerate(((wk_sb, C.C_MK),
                                                  (wv_sb, C.C_MV),
                                                  (wr_sb, C.C_MR))):
                    if not (t == 0 and mat == 0):
                        self.wait("vector", self.cT,
                                  self._xmix_use if mat == 0 else self.cT.n)
                    for g in range(NG):
                        self.stt("vector", xmix[:, g, :], diff[:, g, :],
                                 cv[:, col, g:g + 1], h1_fm[:, g, 0:TT],
                                 Alu.mult, Alu.add)
                    vMix = self.cV.n
                    self.wait("tensor", self.cV, vMix)
                    for oc in range(NG):
                        bank = (mat * NG + oc) % 3
                        self.use_bank(bank)
                        for ic in range(NG):
                            self.mm(self.pb[bank][:, 0:TT],
                                    wsb[:, ic, oc * 128:(oc + 1) * 128],
                                    xmix[:, ic, :], ic == 0, ic == NG - 1)
                        tMM = self.cT.n
                        self._xmix_use = tMM
                        if mat == 0:
                            self.wait("scalar", self.cT, tMM)
                            self.act(ek[:, oc, :], self.pb[bank][:, 0:TT],
                                     Act.Exp)
                            self.guard_bank(bank, "scalar")
                            ek_marks.append(self.cS.n)
                        elif mat == 1:
                            self.wait("vector", self.cT, tMM)
                            self.wait("vector", self.cS, ek_marks[oc])
                            self.tt("vector", ekv[:, oc, :],
                                    self.pb[bank][:, 0:TT], ek[:, oc, :],
                                    Alu.mult)
                            self.guard_bank(bank, "vector")
                        else:
                            self.wait("scalar", self.cT, tMM)
                            self.act(r16(oc), self.pb[bank][:, 0:TT],
                                     Act.Tanh, scale=0.5)
                            self.guard_bank(bank, "scalar")
                mmr_mark[t] = self.cT.n
                sR = self.cS.n
                self.wait("vector", self.cS, sR)
                self.ts("vector", r4, r4, 0.5, 0.5, Alu.mult, Alu.add)
                for g in range(NG):
                    self.vcopy(q_arr[:, g, 0:1], q_arr[:, g, TT:TT + 1])
                    self.vcopy(p_arr[:, g, 0:1], p_arr[:, g, TT:TT + 1])
                for g in range(NG):
                    self.sched("vector", lambda e, g=g: e.tensor_tensor_scan(
                        out=q_arr[:, g, 1:TT + 1],
                        data0=cv[:, C.C_EW, g:g + 1].to_broadcast([128, TT]),
                        data1=ek[:, g, :], initial=q_arr[:, g, 0:1],
                        op0=Alu.mult, op1=Alu.add), inc=self.cV)
                    self.sched("vector", lambda e, g=g: e.tensor_tensor_scan(
                        out=p_arr[:, g, 1:TT + 1],
                        data0=cv[:, C.C_EW, g:g + 1].to_broadcast([128, TT]),
                        data1=ekv[:, g, :], initial=p_arr[:, g, 0:1],
                        op0=Alu.mult, op1=Alu.add), inc=self.cV)
                for g in range(NG):
                    self.stt("vector", ekv[:, g, :], ekv[:, g, :],
                             cv[:, C.C_EU, g:g + 1], p_arr[:, g, 0:TT],
                             Alu.mult, Alu.add)
                    self.stt("vector", ek[:, g, :], ek[:, g, :],
                             cv[:, C.C_EU, g:g + 1], q_arr[:, g, 0:TT],
                             Alu.mult, Alu.add)
                vND = self.cV.n
                self.wait("scalar", self.cV, vND)
                self.dma("scalar", self.s_num[t], flat(ekv))
                self.dma("scalar", self.s_den[t], flat(ek))
                nd_spill[t] = self.dma("scalar", self.s_r[t], r4)

            self.vcopy(self.pq_j[:, 0, 0:NG], p_arr[:, :, TT])
            self.vcopy(self.pq_j[:, 0, NG:16], q_arr[:, :, TT])
            vPack = self.cV.n
            self.wait("gpsimd", self.cV, vPack)
            self.dma("gpsimd", self.cc1_in[0].rearrange("(p f) -> p f", p=128),
                     self.pq_j[:, 0, :])
            self.wait("gpsimd", self.dG)
            self.sched("gpsimd", lambda e: e.collective_compute(
                "AllGather", Alu.bypass, replica_groups=RG,
                ins=[self.cc1_in], outs=[self.cc1_out]), inc=self.cG)
            ag1 = self.cG.n
            self.wait("gpsimd", self.cG, ag1)
            for j in range(8):
                self.dma("gpsimd", self.pq_j[:, j, :],
                         self.cc1_out[j].rearrange("(p f) -> p f", p=128))
            ccLD = self.dG.n
            self.wait("scalar", self.dG, ccLD)
            for j in range(8):
                self.act(self.coefs[:, :, j], cv[:, C.C_W, :], Act.Exp,
                         scale=self.a8bc[:, j:j + 1])
            sCoef = self.cS.n
            self.wait("vector", self.cS, sCoef)
            self.wait("vector", self.dG, ccLD)
            self.memset(cv[:, C.C_PIN, :], 0.0)
            self.memset(cv[:, C.C_QIN, :], 0.0)
            self.vsync()
            for j in range(8):
                self.tt("vector", cv[:, C.C_TMP, :], self.coefs[:, :, j],
                        self.pq_j[:, j, 0:NG], Alu.mult)
                self.vsync()
                self.tt("vector", cv[:, C.C_PIN, :], cv[:, C.C_PIN, :],
                        cv[:, C.C_TMP, :], Alu.add)
                self.vsync()
                self.tt("vector", cv[:, C.C_TMP, :], self.coefs[:, :, j],
                        self.pq_j[:, j, NG:16], Alu.mult)
                self.vsync()
                self.tt("vector", cv[:, C.C_QIN, :], cv[:, C.C_QIN, :],
                        cv[:, C.C_TMP, :], Alu.add)
                self.vsync()

    def p1b(self):
        C, cv = self, self.cvr
        nc = self.nc
        with ExitStack() as ph:
            psb = lambda n, s, d=F32: ph.enter_context(nc.sbuf_tensor(n, s, d))
            wo_sb = psb("wo_sb", [128, NG, D], BF16)
            nbufs = [psb("nbuf0", [128, NG, TT])] * 2
            dbufs = [psb("dbuf0", [128, NG, TT])] * 2
            rbufs = [psb("rbuf0", [128, NG, TT], BF16)] * 2
            u0_tms = [psb("u0b_tm0", [128, NSUB, D])] * 2
            Dfm = psb("Dfm", [128, NG, TT])
            a_fm = psb("a_fm", [128, NG, TT])
            wkv16 = Dfm.ap().bitcast(BF16)[:, :, 0:TT]
            xb_tm = psb("xb_tm", [128, NSUB, D])
            flat = lambda a: a.ap().rearrange("p a b -> p (a b)")

            self.dma("gpsimd", wo_sb.ap(),
                     self.w_wo.rearrange("(ic p) o -> p ic o", p=128))
            wLD = self.dG.n

            consume_mark = [0] * NT
            aT_mark = [0] * NT
            wo_mm_mark = [0] * NT
            for t in range(NT):
                nbuf, dbuf = nbufs[t % 2], dbufs[t % 2]
                rbuf, u0_tm = rbufs[t % 2], u0_tms[t % 2]
                if t > 0:
                    self.wait("scalar", self.cT, wo_mm_mark[t - 1])
                if t > 0:
                    self.wait("scalar", self.cV, consume_mark[t - 1])
                m1 = self.dma("scalar", flat(nbuf), self.s_num[t])
                m2 = self.dma("scalar", flat(dbuf), self.s_den[t])
                if t > 0:
                    self.wait("scalar", self.cV, consume_mark[t - 1])
                m3 = self.dma("scalar", flat(rbuf), self.s_r[t])
                if t > 0:
                    self.wait("scalar", self.cV, aT_mark[t - 1])
                m4 = self.dma("scalar", u0_tm[:],
                              self.s_u0[t * TT:(t + 1) * TT].rearrange(
                                  "(s p) h -> p s h", p=128))
                self.ts("vector", cv[:, C.C_WTB, :], cv[:, C.C_W, :],
                        float(t * TT), None, Alu.mult)
                vWT = self.cV.n
                self.wait("scalar", self.cV, vWT)
                for g in range(NG):
                    self.act(Dfm[:, g, :], self.ramp01[:], Act.Exp,
                             scale=cv[:, C.C_W, g:g + 1],
                             bias=cv[:, C.C_WTB, g:g + 1])
                sD = self.cS.n
                self.wait("vector", self.cS, sD)
                self.wait("vector", self.dA, m2)
                for g in range(NG):
                    self.stt("vector", nbuf[:, g, :], Dfm[:, g, :],
                             cv[:, C.C_PIN, g:g + 1], nbuf[:, g, :],
                             Alu.mult, Alu.add)
                    self.stt("vector", dbuf[:, g, :], Dfm[:, g, :],
                             cv[:, C.C_QIN, g:g + 1], dbuf[:, g, :],
                             Alu.mult, Alu.add)
                self.sched("vector", lambda e: e.reciprocal(
                    out=flat(dbuf), in_=flat(dbuf)), inc=self.cV)
                self.tt("vector", wkv16, nbuf.ap(), dbuf.ap(), Alu.mult)
                self.wait("vector", self.dA, m3)
                self.tt("vector", wkv16, wkv16, rbuf.ap(), Alu.mult)
                consume_mark[t] = self.cV.n

                self.wait("tensor", self.cV, consume_mark[t])
                if t == 0:
                    self.wait("tensor", self.dG, wLD)
                for oc in range(NG):
                    bank = oc % 3
                    self.use_bank(bank)
                    for ic in range(NG):
                        self.mm(self.pb[bank][:, 0:TT],
                                wo_sb[:, ic, oc * 128:(oc + 1) * 128],
                                wkv16[:, ic, :], ic == 0, ic == NG - 1)
                    tMM = self.cT.n
                    self.wait("vector", self.cT, tMM)
                    self.vcopy(a_fm[:, oc, :], self.pb[bank][:, 0:TT])
                    self.guard_bank(bank, "vector")
                vA = self.cV.n
                wo_mm_mark[t] = self.cT.n

                self.wait("tensor", self.cV, vA)
                self.wait("scalar", self.dA, m4)
                for s in range(NSUB):
                    b0, b1 = 4 + 2 * (s % 2), 5 + 2 * (s % 2)
                    self.use_bank(b0)
                    self.use_bank(b1)
                    for g in range(NG):
                        bank = b0 if g < 4 else b1
                        self.tr(self.pb[bank].ap()[:, (g % 4) * 128:
                                                   (g % 4) * 128 + 128],
                                a_fm[:, g, s * 128:(s + 1) * 128],
                                self.ident32[:])
                    tS = self.cT.n
                    self.wait("vector", self.cT, tS)
                    self.tt("vector", xb_tm[:, s, 0:512], self.pb[b0][:],
                            u0_tm[:, s, 0:512], Alu.add)
                    self.tt("vector", xb_tm[:, s, 512:1024], self.pb[b1][:],
                            u0_tm[:, s, 512:1024], Alu.add)
                    self.guard_bank(b0, "vector")
                    self.guard_bank(b1, "vector")
                aT_mark[t] = self.cV.n
                self.wait("scalar", self.cV, aT_mark[t])
                self.dma("scalar", self.s_xb[t * TT:(t + 1) * TT].rearrange(
                    "(s p) h -> p s h", p=128), xb_tm[:])
                if t == NT - 1:
                    self.wait("gpsimd", self.cV, aT_mark[t])
                    self.dma("gpsimd", self.cc2_in, xb_tm[127:128, 3, :])
                    self.wait("gpsimd", self.dG)
                    self.sched("gpsimd", lambda e: e.collective_compute(
                        "AllGather", Alu.bypass, replica_groups=RG,
                        ins=[self.cc2_in], outs=[self.cc2_out]), inc=self.cG)
                    self.ag2_done = self.cG.n

    def p2a(self):
        C, cv = self, self.cvr
        nc = self.nc
        with ExitStack() as ph:
            psb = lambda n, s, d=F32: ph.enter_context(nc.sbuf_tensor(n, s, d))
            fwk_sb = psb("fwk_sb", [128, NG, 4 * D], BF16)
            fwr_sb = psb("fwr_sb", [128, NG, D], BF16)
            xb_tm = psb("xb2_tm", [128, NSUB, D])
            st = psb("st2", [128, NSUB, 6])
            h2_fm = psb("h2_fm", [128, NG, TT + 1], BF16)
            kf16 = psb("kf16", [128, 32, TT], BF16)
            xk2 = psb("xk2", [128, NG, TT], BF16)
            xr2 = psb("xr2", [128, NG, TT], BF16)
            ktmp = psb("ktmp", [128, 2, TT])
            cc2_sb = psb("cc2_sb", [8, D])
            h2_tm = kf16.ap()[:, 8:16, :].rearrange(
                "p (a two) t -> p a (two t)", two=2)
            rf4 = xb_tm.ap().bitcast(BF16)[:, :, D:2 * D].rearrange(
                "p a (g2 t) -> p a g2 t", g2=2)
            rf16 = lambda oc: rf4[:, oc // 2, oc % 2, :]
            diff = kf16.ap()[:, 0:NG, 0:TT]
            flat = lambda a: a.ap().rearrange("p a b -> p (a b)")

            self.dma("gpsimd", fwk_sb.ap(),
                     self.w_fwk.rearrange("(ic p) o -> p ic o", p=128))
            self.dma("gpsimd", fwr_sb.ap(),
                     self.w_fwr.rearrange("(ic p) o -> p ic o", p=128))
            wLD = self.dG.n
            self.dma("gpsimd", self.row_gx[:], self.wvec["ln_ffn_g"])
            self.dma("gpsimd", self.row_bx[:], self.wvec["ln_ffn_b"])
            rowLD = self.dG.n

            self.wait("gpsimd", self.cG, self.ag2_done)
            ccm = self.dma("gpsimd", cc2_sb[:], self.cc2_out)
            self.wait("tensor", self.dG, ccm)
            self.use_bank(6)
            self.use_bank(7)
            self.mm(self.pb[6][0:1, 0:512], self.sel8_sb[:],
                    cc2_sb[:, 0:512], True, True)
            self.mm(self.pb[7][0:1, 0:512], self.sel8_sb[:],
                    cc2_sb[:, 512:1024], True, True)
            tSel = self.cT.n
            self.wait("vector", self.cT, tSel)
            self.vcopy(self.row_a[0:1, 0:512], self.pb[6][0:1, 0:512])
            self.vcopy(self.row_a[0:1, 512:1024], self.pb[7][0:1, 0:512])
            self.guard_bank(6, "vector")
            self.guard_bank(7, "vector")
            self.wait("vector", self.dG, rowLD)
            self.row_ln(self.bnd_h2, self.row_gx[:], self.row_bx[:])

            kf_spill = [0] * NT
            h2T_mark = [0] * NT
            h2evac_marks = [0] * NT
            mm_mark = [0] * NT
            for t in range(NT):
                if t > 0:
                    self.wait("scalar", self.cT, h2T_mark[t - 1])
                    self.wait("scalar", self.dA, kf_spill[t - 1])
                xm = self.dma("scalar", xb_tm[:],
                              self.s_xb[t * TT:(t + 1) * TT].rearrange(
                                  "(s p) h -> p s h", p=128))
                self.wait("scalar", self.dA, xm)
                kdump = ktmp.ap().rearrange("p a b -> p (a b)")
                for s in range(NSUB):
                    self.act(kdump, xb_tm[:, s, :],
                             Act.Identity, accum=st[:, s, 0:1])
                    self.act(kdump, xb_tm[:, s, :],
                             Act.Square, accum=st[:, s, 1:2])
                sStat = self.cS.n
                self.wait("vector", self.cS, sStat)
                self.ln_finish(st[:, :, 0], st[:, :, 1], st[:, :, 2],
                               st[:, :, 3], st[:, :, 4])
                sFin = self.cS.n
                self.wait("vector", self.cS, sFin)
                if t > 0:
                    self.wait("vector", self.dA, kf_spill[t - 1])
                for s in range(NSUB):
                    self.ts("vector", h2_tm[:, s, :], xb_tm[:, s, :],
                            st[:, s, 2:3], st[:, s, 3:4], Alu.subtract, Alu.mult)
                vH2 = self.cV.n
                if t == 0:
                    self.vcopy(h2_fm[:, :, 0:1], self.bnd_h2[:])
                else:
                    self.wait("vector", self.cS, h2evac_marks[t - 1])
                    self.vcopy(h2_fm[:, :, 0:1], h2_fm[:, :, TT:TT + 1])
                self.wait("tensor", self.cV, self.cV.n)
                for s in range(NSUB):
                    self.use_bank(4 + s)
                for s in range(NSUB):
                    for g in range(NG):
                        self.tr(self.pb16[4 + s][:, g * 128:(g + 1) * 128],
                                h2_tm[:, s, g * 128:(g + 1) * 128],
                                self.ident16[:])
                h2T_mark[t] = self.cT.n
                self.wait("scalar", self.cT, h2T_mark[t])
                for s in range(NSUB):
                    for g in range(NG):
                        self.act(h2_fm[:, g, 1 + s * 128:1 + (s + 1) * 128],
                                 self.pb16[4 + s][:, g * 128:(g + 1) * 128],
                                 Act.Identity, scale=cv[:, C.C_LFG, g:g + 1],
                                 bias=cv[:, C.C_LFB, g:g + 1])
                h2evac_marks[t] = self.cS.n
                for s in range(NSUB):
                    self.guard_bank(4 + s, "scalar")

                if t > 0:
                    self.wait("vector", self.dA, kf_spill[t - 1])
                self.wait("vector", self.cS, h2evac_marks[t])
                self.tt("vector", diff, h2_fm[:, :, 1:TT + 1],
                        h2_fm[:, :, 0:TT], Alu.subtract)
                if t > 0:
                    self.wait("vector", self.cT, mm_mark[t - 1])
                for dst, col in ((xk2, C.C_FMK), (xr2, C.C_FMR)):
                    for g in range(NG):
                        self.stt("vector", dst[:, g, :], diff[:, g, :],
                                 cv[:, col, g:g + 1], h2_fm[:, g, 0:TT],
                                 Alu.mult, Alu.add)
                vMix = self.cV.n
                if "h2" in self.taps:
                    self.wait("scalar", self.cV, vMix)
                    self.dma("scalar",
                             self.s_h2[t].rearrange("p (g c) -> p g c", g=NG),
                             h2_fm[:, :, 1:TT + 1])
                    self.dma("scalar", self.s_xk2[t], flat(xk2))

                self.wait("tensor", self.cV, vMix)
                if t == 0:
                    self.wait("tensor", self.dG, wLD)
                kf_sq_mark = [0, 0]
                for oc in range(32):
                    bank = oc % 3
                    self.use_bank(bank)
                    for ic in range(NG):
                        self.mm(self.pb[bank][:, 0:TT],
                                fwk_sb[:, ic, oc * 128:(oc + 1) * 128],
                                xk2[:, ic, :], ic == 0, ic == NG - 1)
                    tMM = self.cT.n
                    self.wait("scalar", self.cT, tMM)
                    slot = oc % 2
                    self.act(ktmp[:, slot, :], self.pb[bank][:, 0:TT], Act.Relu)
                    self.guard_bank(bank, "scalar")
                    self.act(kf16[:, oc, :], ktmp[:, slot, :], Act.Square)
                    kf_sq_mark[slot] = self.cS.n
                for oc in range(NG):
                    bank = oc % 3
                    self.use_bank(bank)
                    for ic in range(NG):
                        self.mm(self.pb[bank][:, 0:TT],
                                fwr_sb[:, ic, oc * 128:(oc + 1) * 128],
                                xr2[:, ic, :], ic == 0, ic == NG - 1)
                    tMM = self.cT.n
                    self.wait("scalar", self.cT, tMM)
                    self.act(rf16(oc), self.pb[bank][:, 0:TT],
                             Act.Tanh, scale=0.5)
                    self.guard_bank(bank, "scalar")
                mm_mark[t] = self.cT.n
                sRF = self.cS.n
                self.wait("vector", self.cS, sRF)
                self.ts("vector", rf4, rf4, 0.5, 0.5, Alu.mult, Alu.add)
                vRF = self.cV.n
                self.wait("scalar", self.cV, vRF)
                self.dma("scalar", self.s_kf[t], flat(kf16))
                kf_spill[t] = self.dma("scalar", self.s_rf[t], rf4)

    def p2b(self):
        C, cv = self, self.cvr
        nc = self.nc
        with ExitStack() as ph:
            psb = lambda n, s, d=F32: ph.enter_context(nc.sbuf_tensor(n, s, d))
            fwv_sb = psb("fwv_sb", [128, 32, D], BF16)
            kf16 = psb("kf16b", [128, 32, TT], BF16)
            xb_tm = psb("xb3_tm", [128, NSUB, D])
            x0_tm = psb("x0_tm", [128, NSUB, D])
            f_fm = psb("f_fm", [128, NG, TT])
            x1_tm = psb("x1_tm", [128, NSUB, D])
            st = psb("st3", [128, NSUB, 6])
            u1_tm = x0_tm.ap().bitcast(BF16)[:, :, 0:D]
            rf4 = x1_tm.ap().bitcast(BF16)[:, :, 0:D].rearrange(
                "p a (g2 t) -> p a g2 t", g2=2)
            rf16 = lambda oc: rf4[:, oc // 2, oc % 2, :]
            u1buf = f_fm.ap().bitcast(BF16)[:, 4:8, 0:2 * TT].rearrange(
                "p a (g2 t) -> p (a g2) t", g2=2)
            flat = lambda a: a.ap().rearrange("p a b -> p (a b)")

            self.dma("gpsimd", fwv_sb.ap(),
                     self.w_fwv.rearrange("(ic p) o -> p ic o", p=128))
            wLD = self.dG.n

            ld_mark = [0] * NT
            u1T_mark = [0] * NT
            u1_spill = [0] * NT
            mm_mark = [0] * NT
            asm_mark = [0] * NT
            for t in range(NT):
                if t > 0:
                    self.wait("scalar", self.cT, mm_mark[t - 1])
                m1 = self.dma("scalar", flat(kf16), self.s_kf[t])
                if t > 0:
                    self.wait("scalar", self.cV, ld_mark[t - 1])
                m2 = self.dma("scalar", rf4, self.s_rf[t])
                if t > 0:
                    self.wait("scalar", self.cV, asm_mark[t - 1])
                m3 = self.dma("scalar", xb_tm[:],
                              self.s_xb[t * TT:(t + 1) * TT].rearrange(
                                  "(s p) h -> p s h", p=128))
                if t > 0:
                    self.wait("scalar", self.cT, u1T_mark[t - 1])
                m4 = self.dma("scalar", x0_tm[:],
                              self.xs[t * TT:(t + 1) * TT].rearrange(
                                  "(s p) h -> p s h", p=128))
                self.wait("tensor", self.dA, m1)
                if t == 0:
                    self.wait("tensor", self.dG, wLD)
                if t > 0:
                    self.wait("vector", self.dA, u1_spill[t - 1])
                for oc in range(NG):
                    bank = oc % 3
                    self.use_bank(bank)
                    for ic in range(32):
                        self.mm(self.pb[bank][:, 0:TT],
                                fwv_sb[:, ic, oc * 128:(oc + 1) * 128],
                                kf16[:, ic, :], ic == 0, ic == 31)
                    tMM = self.cT.n
                    self.wait("vector", self.cT, tMM)
                    if oc == 0:
                        self.wait("vector", self.dA, m2)
                    self.tt("vector", f_fm[:, oc, :], self.pb[bank][:, 0:TT],
                            rf16(oc), Alu.mult)
                    self.guard_bank(bank, "vector")
                vF = self.cV.n
                mm_mark[t] = self.cT.n

                self.wait("tensor", self.cV, vF)
                self.wait("vector", self.dA, m3)
                for s in range(NSUB):
                    b0, b1 = 4 + 2 * (s % 2), 5 + 2 * (s % 2)
                    self.use_bank(b0)
                    self.use_bank(b1)
                    for g in range(NG):
                        bank = b0 if g < 4 else b1
                        self.tr(self.pb[bank].ap()[:, (g % 4) * 128:
                                                   (g % 4) * 128 + 128],
                                f_fm[:, g, s * 128:(s + 1) * 128],
                                self.ident32[:])
                    tS = self.cT.n
                    self.wait("vector", self.cT, tS)
                    self.tt("vector", x1_tm[:, s, 0:512], self.pb[b0][:],
                            xb_tm[:, s, 0:512], Alu.add)
                    self.tt("vector", x1_tm[:, s, 512:1024], self.pb[b1][:],
                            xb_tm[:, s, 512:1024], Alu.add)
                    self.guard_bank(b0, "vector")
                    self.guard_bank(b1, "vector")
                self.wait("vector", self.dA, m4)
                self.tt("vector", flat(x1_tm), flat(x1_tm), flat(x0_tm), Alu.add)
                vX1 = self.cV.n
                asm_mark[t] = vX1
                self.wait("scalar", self.cV, vX1)
                self.dma("scalar", self.s_x1[t * TT:(t + 1) * TT].rearrange(
                    "(s p) h -> p s h", p=128), x1_tm[:])

                fdump = f_fm.ap()[:, 0:2, :].rearrange("p a b -> p (a b)")
                for s in range(NSUB):
                    self.act(fdump, x1_tm[:, s, :], Act.Identity,
                             accum=st[:, s, 0:1])
                    self.act(fdump, x1_tm[:, s, :], Act.Square,
                             accum=st[:, s, 1:2])
                sStat = self.cS.n
                self.wait("vector", self.cS, sStat)
                self.ln_finish(st[:, :, 0], st[:, :, 1], st[:, :, 2],
                               st[:, :, 3], st[:, :, 4])
                sFin = self.cS.n
                self.wait("vector", self.cS, sFin)
                for s in range(NSUB):
                    self.ts("vector", u1_tm[:, s, :], x1_tm[:, s, :],
                            st[:, s, 2:3], st[:, s, 3:4], Alu.subtract, Alu.mult)
                vU1 = self.cV.n
                ld_mark[t] = vU1
                self.wait("tensor", self.cV, vU1)
                for s in range(NSUB):
                    self.use_bank(4 + s)
                for s in range(NSUB):
                    for g in range(NG):
                        self.tr(self.pb16[4 + s][:, g * 128:(g + 1) * 128],
                                u1_tm[:, s, g * 128:(g + 1) * 128],
                                self.ident16[:])
                u1T_mark[t] = self.cT.n
                self.wait("vector", self.cT, u1T_mark[t])
                for s in range(NSUB):
                    self.vcopy(
                        u1buf[:, :, s * 128:(s + 1) * 128],
                        self.pb16[4 + s].rearrange("p (g c) -> p g c", g=NG))
                    self.guard_bank(4 + s, "vector")
                vU1e = self.cV.n
                self.wait("scalar", self.cV, vU1e)
                u1_spill[t] = self.dma("scalar", self.s_u1[t],
                                       u1buf.rearrange("p g t -> p (g t)"))

    def p3(self):
        C, cv = self, self.cvr
        nc = self.nc
        with ExitStack() as ph:
            psb = lambda n, s, d=F32: ph.enter_context(nc.sbuf_tensor(n, s, d))
            shg = psb("shg", [128, NG, D], BF16)
            shu = psb("shu", [128, NG, D], BF16)
            shd = shg
            u1q = psb("u1q", [128, NG, QTR], BF16)
            acc = psb("acc", [128, NG, QTR])
            gssh = psb("gssh", [128, NG, QTR], BF16)
            wge = psb("wge0", [128, NG, DI], BF16)
            wue = psb("wue0", [128, NG, DI], BF16)
            wde = psb("wde0", [128, 4, D], BF16)
            gs16 = psb("gs16", [128, 4, QTR], BF16)
            wb_sb = psb("wb_sb", [128, QTR])
            sc_all = psb("sc_all", [128, 8, NE])
            wmat = psb("wmat", [128, 8, NE])
            wfm = psb("wfm", [32, QTR])
            wfm16 = psb("wfm16", [32, QTR], BF16)
            wrow = psb("wrow", [1, QTR], BF16)
            gsum8 = psb("gsum8", [128, 8])
            m8 = psb("m8", [128, 8])
            msk = psb("msk", [128, NE])
            stmp = psb("stmp", [128, 8, TT])
            utmp = gssh.ap().bitcast(F32)
            dtmp = psb("dtmp", [128, 2, TT])
            x1_tm = psb("x1b_tm", [128, NSUB, D])
            flat = lambda a: a.ap().rearrange("p a b -> p (a b)")
            self.memset(gsum8[:, 4:8], -1e30)
            self.wait("sync", self.cG, self.agw_done)
            self.wait("gpsimd", self.cG, self.agw_done)

            self._a_done = 0
            self._d_done = 0
            self._d_mms = 0
            self._gs_read = [0, 0]
            self._shg_use = 0
            self._wrow_use = 0
            u1q_use = 0
            ylast = 0
            for q in range(NQ):
                self.wait("sync", self.cT, u1q_use)
                uq = self.dma("sync", u1q[:, :, 0:TT], self.s_u1[2 * q])
                uq = self.dma("sync", u1q[:, :, TT:QTR], self.s_u1[2 * q + 1])
                self.wait("tensor", self.dS, uq)

                for i in range(8):
                    self.use_bank(3)
                    for ic in range(NG):
                        self.mm(self.pb[3][:, i * 64:i * 64 + NE],
                                u1q[:, ic, i * 128:(i + 1) * 128],
                                self.gwT[:, ic, :], ic == 0, ic == NG - 1)
                tLG = self.cT.n
                self.wait("scalar", self.cT, tLG)
                for i in range(8):
                    self.act(sc_all[:, i, :], self.pb[3][:, i * 64:i * 64 + NE],
                             Act.Tanh, scale=0.5)
                self.guard_bank(3, "scalar")
                sLG = self.cS.n
                self.wait("vector", self.cS, sLG)
                self.ts("vector", flat(sc_all), flat(sc_all), 0.5, 0.5,
                        Alu.mult, Alu.add)
                for i in range(8):
                    sc = sc_all[:, i, :]
                    wm = wmat[:, i, :]
                    self.tt("vector", wm, sc, self.gbias_bc[:], Alu.add)
                    self.vsync()
                    for grp in range(4):
                        self.sched("vector", lambda e, grp=grp, wm=wm: e.max(
                            out=m8[:], in_=wm[:, grp * 8:(grp + 1) * 8]),
                            inc=self.cV)
                        self.vsync()
                        self.tt("vector", gsum8[:, grp:grp + 1], m8[:, 0:1],
                                m8[:, 1:2], Alu.add)
                        self.vsync()
                    self.sched("vector", lambda e: e.max(
                        out=m8[:], in_=gsum8[:]), inc=self.cV)
                    self.vsync()
                    self.sched("vector", lambda e: e.tensor_single_scalar(
                        out=gsum8[:, 0:4], in_=gsum8[:, 0:4],
                        scalar=m8[:, 1:2], op=Alu.is_ge), inc=self.cV)
                    self.vsync()
                    self.tt("vector",
                            msk.ap().rearrange("p (g e) -> p g e", g=4),
                            gsum8.ap()[:, 0:4].unsqueeze(2).to_broadcast(
                                [128, 4, 8]),
                            wm.rearrange("p (g e) -> p g e", g=4), Alu.mult)
                    self.vsync()
                    self.sched("vector", lambda e: e.max(
                        out=m8[:], in_=msk[:]), inc=self.cV)
                    self.vsync()
                    self.sched("vector", lambda e: e.tensor_single_scalar(
                        out=msk[:], in_=msk[:], scalar=m8[:, 7:8],
                        op=Alu.is_ge), inc=self.cV)
                    self.vsync()
                    self.tt("vector", wm, sc, msk[:], Alu.mult)
                    self.vsync()
                    self.sched("vector", lambda e, wm=wm: e.reduce_sum(
                        out=m8[:, 0:1], in_=wm, axis=mybir.AxisListType.X),
                        inc=self.cV)
                    self.vsync()
                    self.ts("vector", m8[:, 0:1], m8[:, 0:1], 1e-20, None,
                            Alu.add)
                    self.vsync()
                    self.sched("vector", lambda e: e.reciprocal(
                        out=m8[:, 1:2], in_=m8[:, 0:1]), inc=self.cV)
                    self.vsync()
                    self.ts("vector", m8[:, 1:2], m8[:, 1:2], 2.5, None,
                            Alu.mult)
                    self.vsync()
                    self.sched("vector", lambda e, wm=wm: e.tensor_single_scalar(
                        out=wm, in_=wm, scalar=m8[:, 1:2], op=Alu.mult),
                        inc=self.cV)
                    self.vsync()
                vGate = self.cV.n
                self.wait("sync", self.cT, self._shg_use)
                self.dma("sync", flat(shg), self.s_shT[0])
                shLD = self.dma("sync", flat(shu), self.s_shT[1])
                self.wait("tensor", self.dS, shLD)
                sh_mark = [0, 0, 0, 0]
                for tk in range(2):
                    for oc in range(NG):
                        b0 = (oc % 2) * 2
                        b1 = b0 + 1
                        self.use_bank(b0)
                        self.use_bank(b1)
                        for ic in range(NG):
                            self.mm(self.pb[b0][:, 0:TT],
                                    shg[:, ic, oc * 128:(oc + 1) * 128],
                                    u1q[:, ic, tk * TT:(tk + 1) * TT],
                                    ic == 0, ic == NG - 1)
                        for ic in range(NG):
                            self.mm(self.pb[b1][:, 0:TT],
                                    shu[:, ic, oc * 128:(oc + 1) * 128],
                                    u1q[:, ic, tk * TT:(tk + 1) * TT],
                                    ic == 0, ic == NG - 1)
                        tMM = self.cT.n
                        self.wait("scalar", self.cT, tMM)
                        self.wait("scalar", self.cV, sh_mark[oc % 4])
                        self.act(stmp[:, oc % 4, :], self.pb[b0][:, 0:TT],
                                 Act.Silu)
                        self.guard_bank(b0, "scalar")
                        sSi = self.cS.n
                        self.wait("vector", self.cS, sSi)
                        self.tt("vector", stmp[:, 4 + oc % 4, :],
                                self.pb[b1][:, 0:TT], stmp[:, oc % 4, :],
                                Alu.mult)
                        self.guard_bank(b1, "vector")
                        self.vcopy(gssh[:, oc, tk * TT:(tk + 1) * TT],
                                   stmp[:, 4 + oc % 4, :])
                        sh_mark[oc % 4] = self.cV.n
                vSh = self.cV.n
                guDone = self.cT.n
                self.wait("sync", self.cT, guDone)
                shdLD = self.dma("sync", flat(shd), self.s_shT[2])
                self.wait("tensor", self.cV, vGate)
                self.use_bank(2)
                self.use_bank(3)
                for i in range(8):
                    bk = 2 + i // 4
                    self.tr(self.pb[bk][0:NE, (i % 4) * 128:(i % 4) * 128 + 128],
                            wmat[:, i, :], self.ident32[:])
                tWM = self.cT.n
                self.wait("vector", self.cT, tWM)
                for i in range(8):
                    bk = 2 + i // 4
                    self.vcopy(wfm[:, i * 128:(i + 1) * 128],
                               self.pb[bk][0:NE, (i % 4) * 128:(i % 4) * 128 + 128])
                self.guard_bank(2, "vector")
                self.guard_bank(3, "vector")
                self.vcopy(wfm16[:], wfm[:])
                vWF = self.cV.n
                self.wait("tensor", self.cV, vSh)
                self.wait("tensor", self.dS, shdLD)
                for tk in range(2):
                    for oc in range(NG):
                        bank = oc % 2
                        self.use_bank(bank)
                        for ic in range(NG):
                            self.mm(self.pb[bank][:, 0:TT],
                                    shd[:, ic, oc * 128:(oc + 1) * 128],
                                    gssh[:, ic, tk * TT:(tk + 1) * TT],
                                    ic == 0, ic == NG - 1)
                        tMM = self.cT.n
                        self.wait("vector", self.cT, tMM)
                        self.vcopy(acc[:, oc, tk * TT:(tk + 1) * TT],
                                   self.pb[bank][:, 0:TT])
                        self.guard_bank(bank, "vector")
                self._shg_use = self.cT.n

                dn_marks = {}
                vShD = self.cV.n
                self.wait("gpsimd", self.cV, vShD)
                for e in range(NE):
                    self.wait("sync", self.cT, self._a_done)
                    for part, dst in ((0, wge), (1, wue)):
                        src = self.ccw_out[e // EPL,
                                           ((e % EPL) * 3 + part) * EMAT:
                                           ((e % EPL) * 3 + part + 1) * EMAT]
                        ldGU = self.dma("sync", flat(dst),
                                        src.rearrange("(p f) -> p f", p=128))
                    self.wait("sync", self.cT, self._d_done)
                    srcd = self.ccw_out[e // EPL,
                                        ((e % EPL) * 3 + 2) * EMAT:
                                        ((e % EPL) * 3 + 3) * EMAT]
                    ldD = self.dma("sync", flat(wde),
                                   srcd.rearrange("(p f) -> p f", p=128))
                    self.wait("scalar", self.cV, vWF)
                    self.wait("scalar", self.cT, self._wrow_use)
                    wrm = self.dma("scalar", wrow[:], wfm16[e:e + 1, :])
                    self.wait("tensor", self.cV, vWF)
                    self.wait("tensor", self.dA, wrm)
                    self.use_bank(6)
                    self.use_bank(7)
                    self.mm(self.pb[6][:, 0:TT], self.ones16[:],
                            wrow[0:1, 0:TT], True, True)
                    self.mm(self.pb[7][:, 0:TT], self.ones16[:],
                            wrow[0:1, TT:QTR], True, True)
                    self._wrow_use = self.cT.n
                    tWB = self.cT.n
                    self.wait("vector", self.cT, tWB)
                    self.vcopy(wb_sb[:, 0:TT], self.pb[6][:, 0:TT])
                    self.vcopy(wb_sb[:, TT:QTR], self.pb[7][:, 0:TT])
                    self.guard_bank(6, "vector")
                    self.guard_bank(7, "vector")
                    vWB = self.cV.n

                    self.wait("tensor", self.dS, ldGU)
                    gu_marks = {}
                    for tk in range(2):
                        for oc in range(4):
                            self.use_bank(4 * tk + oc)
                            for ic in range(NG):
                                self.mm(self.pb[4 * tk + oc][:, 0:TT],
                                        wge[:, ic, oc * 128:(oc + 1) * 128],
                                        u1q[:, ic, tk * TT:(tk + 1) * TT],
                                        ic == 0, ic == NG - 1)
                        gu_marks[("g", tk)] = self.cT.n
                        self.wait("scalar", self.cT, gu_marks[("g", tk)])
                        self.wait("scalar", self.cV, self._gs_read[tk])
                        for oc in range(4):
                            self.act(stmp[:, 4 * tk + oc, :],
                                     self.pb[4 * tk + oc][:, 0:TT], Act.Silu)
                            self.guard_bank(4 * tk + oc, "scalar")
                        gu_marks[("s", tk)] = self.cS.n
                    for tk in range(2):
                        for oc in range(4):
                            self.use_bank(4 * tk + oc)
                            for ic in range(NG):
                                self.mm(self.pb[4 * tk + oc][:, 0:TT],
                                        wue[:, ic, oc * 128:(oc + 1) * 128],
                                        u1q[:, ic, tk * TT:(tk + 1) * TT],
                                        ic == 0, ic == NG - 1)
                        gu_marks[("u", tk)] = self.cT.n
                        self.wait("vector", self.cT, gu_marks[("u", tk)])
                        self.wait("vector", self.cS, gu_marks[("s", tk)])
                        if tk == 0:
                            self.wait("vector", self.cT, self._d_mms)
                        for oc in range(4):
                            self.tt("vector", utmp[:, 4 * tk + oc, :],
                                    self.pb[4 * tk + oc][:, 0:TT],
                                    wb_sb[:, tk * TT:(tk + 1) * TT], Alu.mult)
                            self.guard_bank(4 * tk + oc, "vector")
                        for oc in range(4):
                            self.tt("vector",
                                    gs16[:, oc, tk * TT:(tk + 1) * TT],
                                    stmp[:, 4 * tk + oc, :],
                                    utmp[:, 4 * tk + oc, :], Alu.mult)
                        gu_marks[("gs", tk)] = self.cV.n
                        self._gs_read[tk] = self.cV.n
                    self._a_done = self.cT.n
                    self.wait("tensor", self.dS, ldD)
                    for tk in range(2):
                        self.wait("tensor", self.cV, gu_marks[("gs", tk)])
                        for oc in range(NG):
                            bank = 2 * tk + oc % 2
                            self.use_bank(bank)
                            for ic in range(4):
                                self.mm(self.pb[bank][:, 0:TT],
                                        wde[:, ic, oc * 128:(oc + 1) * 128],
                                        gs16[:, ic, tk * TT:(tk + 1) * TT],
                                        ic == 0, ic == 3)
                            tDN = self.cT.n
                            self.wait("scalar", self.cT, tDN)
                            dslot = oc % 2
                            self.wait("scalar", self.dG,
                                      dn_marks.get(dslot, 0))
                            self.scopy(dtmp[:, dslot, :], self.pb[bank][:, 0:TT])
                            self.guard_bank(bank, "scalar")
                            sCp = self.cS.n
                            self.wait("gpsimd", self.cS, sCp)
                            dn_marks[dslot] = self.dma(
                                "gpsimd", acc[:, oc, tk * TT:(tk + 1) * TT],
                                dtmp[:, dslot, :], accum_op=Alu.add)
                    self._d_done = self.cT.n
                    self._d_mms = self.cT.n

                self.wait("vector", self.dG)
                u1q_use = self.cT.n
                for tk in range(2):
                    self.wait("scalar", self.dG, ylast)
                    xm = self.dma("scalar", x1_tm[:],
                                  self.s_x1[q * QTR + tk * TT:
                                            q * QTR + (tk + 1) * TT].rearrange(
                                      "(s p) h -> p s h", p=128))
                    self.wait("vector", self.dA, xm)
                    self.wait("tensor", self.dA, xm)
                    for s in range(NSUB):
                        b0, b1 = 4 + 2 * (s % 2), 5 + 2 * (s % 2)
                        self.use_bank(b0)
                        self.use_bank(b1)
                        for g in range(NG):
                            bank = b0 if g < 4 else b1
                            self.tr(self.pb[bank].ap()[:, (g % 4) * 128:
                                                       (g % 4) * 128 + 128],
                                    acc[:, g, tk * TT + s * 128:
                                        tk * TT + (s + 1) * 128],
                                    self.ident32[:])
                        tS = self.cT.n
                        self.wait("vector", self.cT, tS)
                        self.tt("vector", x1_tm[:, s, 0:512], self.pb[b0][:],
                                x1_tm[:, s, 0:512], Alu.add)
                        self.tt("vector", x1_tm[:, s, 512:1024], self.pb[b1][:],
                                x1_tm[:, s, 512:1024], Alu.add)
                        self.guard_bank(b0, "vector")
                        self.guard_bank(b1, "vector")
                    vOut = self.cV.n
                    self.wait("gpsimd", self.cV, vOut)
                    ylast = self.dma(
                        "gpsimd",
                        self.y[q * QTR + tk * TT:q * QTR + (tk + 1) * TT].rearrange(
                            "(s p) h -> p s h", p=128), x1_tm[:])

    def finish(self):
        for name, (tap, src) in self.tap_outs.items():
            self.wait("gpsimd", self.dA)
            self.wait("gpsimd", self.dS)
            self.dma("gpsimd", tap, src)
        self.wait("gpsimd", self.dG)
        self.wait("sync", self.dG)


_BUILT = {}


def _get_nc(taps=()):
    key = tuple(sorted(taps))
    if key not in _BUILT:
        mk = MK(taps)
        nc = mk.build()
        _BUILT[key] = (nc, mk.tap_outs)
    return _BUILT[key]


def make_in_maps(inputs):
    x = np.ascontiguousarray(np.asarray(inputs["L0_attn_x"],
                                        dtype=np.float32).reshape(-1, D))
    g = lambda n: np.ascontiguousarray(np.asarray(inputs[n], np.float32))
    vec = lambda n: g(n).reshape(1, D)
    common = {
        "w_ln_att_g": vec("L0_attn_ln_att_g"),
        "w_ln_att_b": vec("L0_attn_ln_att_b"),
        "w_ln_ffn_g": vec("L0_attn_ln_ffn_g"),
        "w_ln_ffn_b": vec("L0_attn_ln_ffn_b"),
        "w_td": vec("L0_attn_att_time_decay"),
        "w_tf": vec("L0_attn_att_time_first"),
        "w_mk": vec("L0_attn_att_mix_k"),
        "w_mv": vec("L0_attn_att_mix_v"),
        "w_mr": vec("L0_attn_att_mix_r"),
        "w_fmk": vec("L0_attn_ffn_mix_k"),
        "w_fmr": vec("L0_attn_ffn_mix_r"),
        "w_wk": g("L0_attn_att_wk"),
        "w_wv": g("L0_attn_att_wv"),
        "w_wr": g("L0_attn_att_wr"),
        "w_wo": g("L0_attn_att_wo"),
        "w_fwk": g("L0_attn_ffn_wk"),
        "w_fwv": g("L0_attn_ffn_wv"),
        "w_fwr": g("L0_attn_ffn_wr"),
        "w_gw": g("L0_ffn_gate_w"),
        "w_gb": g("L0_ffn_gate_bias").reshape(1, NE),
        "w_sg": g("L0_ffn_shared_gate"),
        "w_su": g("L0_ffn_shared_up"),
        "w_sd": g("L0_ffn_shared_down"),
    }
    eg = g("L0_ffn_expert_gate")
    eu = g("L0_ffn_expert_up")
    ed = g("L0_ffn_expert_down")
    maps = []
    for c in range(NCORES):
        seq, pos = c // 4, c % 4
        a8 = np.full(8, 1e9, np.float32)
        sel8 = np.zeros((8, 1), np.float32)
        for j in range(NCORES):
            if j // 4 == seq and j < c:
                a8[j] = float(TPC) * (c - 1 - j)
        if pos > 0:
            sel8[c - 1, 0] = 1.0
            xp = x[c * TPC - 1:c * TPC]
            pm = 1.0
        else:
            xp = np.zeros((1, D), np.float32)
            pm = 0.0
        m = dict(common)
        m["xs"] = x[c * TPC:(c + 1) * TPC]
        m["xp"] = np.ascontiguousarray(xp)
        m["a8"] = a8.reshape(1, 8)
        m["sel8"] = sel8
        m["pmask"] = np.array([[pm]], np.float32)
        m["w_meg"] = np.ascontiguousarray(eg[c * EPL:(c + 1) * EPL])
        m["w_meu"] = np.ascontiguousarray(eu[c * EPL:(c + 1) * EPL])
        m["w_med"] = np.ascontiguousarray(ed[c * EPL:(c + 1) * EPL])
        maps.append(m)
    return maps


def kernel(**inputs):
    nc, _ = _get_nc()
    maps = make_in_maps(inputs)
    res = run_bass_kernel_spmd(nc, maps, core_ids=list(range(NCORES)))
    out = np.concatenate([res.results[c]["y"] for c in range(NCORES)], axis=0)
    return out.reshape(B, L, D)


# version 07
# speedup vs baseline: 1.1059x; 1.0922x over previous
import numpy as np
from contextlib import ExitStack

import concourse.bass as bass
import concourse.mybir as mybir
from concourse.bass_utils import run_bass_kernel_spmd

F32 = mybir.dt.float32
BF16 = mybir.dt.bfloat16
Alu = mybir.AluOpType
Act = mybir.ActivationFunctionType

NCORES = 8
B, L, D = 2, 16384, 1024
TPC = 4096
NT = 8
TT = 512
NSUB = 4
NG = 8
NE = 32
EPL = 4
DI = 512
EPS = 1e-5
RG = [[0, 1, 2, 3, 4, 5, 6, 7]]
EMAT = DI * D
WSH = EPL * 3 * EMAT
QTR = 1024
NQ = TPC // QTR
ENGS = ("tensor", "vector", "scalar", "sync", "gpsimd")


class Sem:
    def __init__(self, nc, es, name):
        self.h = es.enter_context(nc.semaphore(name))
        self.n = 0


class MK:
    def __init__(self, taps=()):
        self.taps = taps
        self.nc = bass.Bass("TRN2", target_bir_lowering=False, debug=False)
        self.es = ExitStack()
        self.ops = {e: [] for e in ENGS}
        self.tap_outs = {}
        self.bank_guard = {}

    def sched(self, eng, fn, inc=None, k=1):
        if inc is not None:
            inc.n += k

            def thunk(e, fn=fn, inc=inc, k=k):
                fn(e).then_inc(inc.h, k)
        else:
            def thunk(e, fn=fn):
                fn(e)
        self.ops[eng].append(thunk)

    def wait(self, eng, sem, val=None):
        v = sem.n if val is None else val
        self.ops[eng].append(lambda e, h=sem.h, v=v: e.wait_ge(h, v))

    def dma(self, eng, out, in_, **kw):
        sem = {"sync": self.dS, "scalar": self.dA, "gpsimd": self.dG}[eng]
        self.sched(eng, lambda e, o=out, i=in_, kw=kw: e.dma_start(out=o, in_=i, **kw),
                   inc=sem, k=16)
        return sem.n

    def barrier(self):
        self.wait("sync", self.dS)
        self.wait("scalar", self.dA)
        self.wait("gpsimd", self.dG)
        marks = {}
        for e in ENGS:
            s = self.fence[e]
            self.sched(e, lambda en: en.nop(), inc=s)
            marks[e] = s.n
        for e in ENGS:
            for o in ENGS:
                if o != e:
                    self.wait(e, self.fence[o], marks[o])

    def emit(self, blk):
        for e in ENGS:
            getattr(blk, e)(lambda eng, e=e: [f(eng) for f in self.ops[e]])

    def csem(self, eng):
        return {"vector": self.cV, "scalar": self.cS, "gpsimd": self.cG}[eng]

    def ts(self, eng, out, in0, s1, s2, op0, op1=Alu.bypass, accum=None):
        self.sched(eng, lambda e: e.tensor_scalar(
            out=out, in0=in0, scalar1=s1, scalar2=s2, op0=op0, op1=op1,
            accum_out=accum), inc=self.csem(eng))

    def tt(self, eng, out, in0, in1, op):
        self.sched(eng, lambda e: e.tensor_tensor(out=out, in0=in0, in1=in1, op=op),
                   inc=self.csem(eng))

    def stt(self, eng, out, in0, scalar, in1, op0, op1):
        self.sched(eng, lambda e: e.scalar_tensor_tensor(
            out=out, in0=in0, scalar=scalar, in1=in1, op0=op0, op1=op1),
            inc=self.csem(eng))

    def act(self, out, in_, func, bias=0.0, scale=1.0, accum=None):
        self.sched("scalar", lambda e: e.activation(
            out=out, in_=in_, func=func, bias=bias, scale=scale,
            accum_out=accum), inc=self.cS)

    def scopy(self, out, in_):
        self.sched("scalar", lambda e: e.copy(out=out, in_=in_), inc=self.cS)

    def vcopy(self, out, in_):
        self.sched("vector", lambda e: e.tensor_copy(out=out, in_=in_), inc=self.cV)

    def memset(self, ap, val):
        self.sched("vector", lambda e: e.memset(ap, val), inc=self.cV)

    def mm(self, out, lhsT, rhs, start, stop):
        self.sched("tensor", lambda e: e.matmul(
            out=out, lhsT=lhsT, rhs=rhs, start=start, stop=stop),
            inc=self.cT if stop else None)

    def tr(self, out, in_, ident):
        self.sched("tensor", lambda e: e.transpose(
            out=out, in_=in_, identity=ident), inc=self.cT)

    def use_bank(self, bank):
        g = self.bank_guard.get(bank)
        if g is not None:
            sem, mark = g
            self.wait("tensor", sem, mark)

    def guard_bank(self, bank, eng):
        self.bank_guard[bank] = (self.csem(eng), self.csem(eng).n)

    def vsync(self):
        self.wait("vector", self.cV)

    def ln_finish(self, s_sum, s_sq, mean, rstd, tmp, nelem=D):
        inv = 1.0 / nelem
        self.ts("vector", mean, s_sum, inv, None, Alu.mult)
        self.ts("vector", rstd, s_sq, inv, None, Alu.mult)
        self.vsync()
        self.tt("vector", tmp, mean, mean, Alu.mult)
        self.vsync()
        self.tt("vector", rstd, rstd, tmp, Alu.subtract)
        self.vsync()
        self.ts("vector", rstd, rstd, EPS, None, Alu.add)
        vm = self.cV.n
        self.wait("scalar", self.cV, vm)
        self.act(rstd, rstd, Act.Ln)
        self.act(rstd, rstd, Act.Exp, scale=-0.5)

    def build(self):
        nc, es = self.nc, self.es
        dram = lambda n, s, k="ExternalInput", d=F32: nc.dram_tensor(
            n, s, d, kind=k).ap()
        self.sbuf = lambda n, s, d=F32: es.enter_context(nc.sbuf_tensor(n, s, d))
        sbuf = self.sbuf

        self.xs = dram("xs", [TPC, D])
        self.xp = dram("xp", [1, D])
        self.a8 = dram("a8", [1, 8])
        self.sel8 = dram("sel8", [8, 1])
        self.pmask = dram("pmask", [1, 1])
        names = ["ln_att_g", "ln_att_b", "ln_ffn_g", "ln_ffn_b", "td", "tf",
                 "mk", "mv", "mr", "fmk", "fmr"]
        self.wvec = {n: dram("w_" + n, [1, D]) for n in names}
        self.w_wk = dram("w_wk", [D, D])
        self.w_wv = dram("w_wv", [D, D])
        self.w_wr = dram("w_wr", [D, D])
        self.w_wo = dram("w_wo", [D, D])
        self.w_fwk = dram("w_fwk", [D, 4 * D])
        self.w_fwv = dram("w_fwv", [4 * D, D])
        self.w_fwr = dram("w_fwr", [D, D])
        self.w_gw = dram("w_gw", [NE, D])
        self.w_gb = dram("w_gb", [1, NE])
        self.w_meg = dram("w_meg", [EPL, DI, D])
        self.w_meu = dram("w_meu", [EPL, DI, D])
        self.w_med = dram("w_med", [EPL, D, DI])
        self.w_sg = dram("w_sg", [D, D])
        self.w_su = dram("w_su", [D, D])
        self.w_sd = dram("w_sd", [D, D])
        self.y = dram("y", [TPC, D], k="ExternalOutput")

        self.s_num = dram("s_num", [NT, 128, NG * TT], k="Internal")
        self.s_den = dram("s_den", [NT, 128, NG * TT], k="Internal")
        self.s_r = dram("s_r", [NT, 128, NG * TT], k="Internal", d=BF16)
        self.s_u0 = dram("s_u0", [TPC, D], k="Internal")
        self.s_xb = dram("s_xb", [TPC, D], k="Internal")
        self.s_x1 = dram("s_x1", [TPC, D], k="Internal")
        self.s_u1 = dram("s_u1", [NT, 128, NG * TT], k="Internal", d=BF16)
        self.s_kf = dram("s_kf", [NT, 128, 32 * TT], k="Internal", d=BF16)
        self.s_rf = dram("s_rf", [NT, 128, NG * TT], k="Internal", d=BF16)
        self.s_shT = dram("s_shT", [3, 128, NG * D], k="Internal", d=BF16)
        self.cc1_in = dram("cc1_in", [1, 2 * D], k="Internal")
        self.cc1_out = dram("cc1_out", [8, 2 * D], k="Internal")
        self.cc2_in = dram("cc2_in", [1, D], k="Internal")
        self.cc2_out = dram("cc2_out", [8, D], k="Internal")
        self.ccw_in = dram("ccw_in", [1, WSH], k="Internal", d=BF16)
        self.ccw_out = dram("ccw_out", [8, WSH], k="Internal", d=BF16)

        if "h2" in self.taps:
            self.s_h2 = dram("s_h2", [NT, 128, NG * TT], k="Internal", d=BF16)
            self.s_xk2 = dram("s_xk2", [NT, 128, NG * TT], k="Internal", d=BF16)
        for t in self.taps:
            src = {"num": self.s_num, "den": self.s_den, "r": self.s_r,
                   "u0": self.s_u0, "xb": self.s_xb, "x1": self.s_x1,
                   "u1": self.s_u1, "kf": self.s_kf, "rf": self.s_rf,
                   "h2": getattr(self, "s_h2", None),
                   "xk2": getattr(self, "s_xk2", None)}[t]
            self.tap_outs[t] = (nc.dram_tensor(
                "tap_" + t, list(src.shape), src.dtype,
                kind="ExternalOutput").ap(), src)

        self.c_id = nc.inline_tensor(np.eye(128, dtype=np.float32), "c_id").ap()
        self.c_ramp = nc.inline_tensor(
            np.arange(TT, dtype=np.float32)[None, :], "c_ramp").ap()
        self.c_ones = nc.inline_tensor(
            np.ones((1, 128), dtype=np.float32), "c_ones").ap()

        self.ident32 = sbuf("ident32", [128, 128])
        self.ident16 = sbuf("ident16", [128, 128], BF16)
        self.ones32 = sbuf("ones32", [1, 128])
        self.ones16 = sbuf("ones16", [1, 128], BF16)
        self.ramp01 = sbuf("ramp01", [128, TT])
        self.rampsb = sbuf("rampsb", [1, TT])
        NCON = 18
        self.cvr = sbuf("cfm", [128, NCON, NG])
        (self.C_MK, self.C_MV, self.C_MR, self.C_FMK, self.C_FMR, self.C_LAG,
         self.C_LAB, self.C_LFG, self.C_LFB, self.C_TD, self.C_TF, self.C_W,
         self.C_EW, self.C_EU, self.C_PIN, self.C_QIN, self.C_WTB,
         self.C_TMP) = range(NCON)
        self.a8sb = sbuf("a8sb", [1, 8])
        self.a8bc = sbuf("a8bc", [128, 8])
        self.pmask_sb = sbuf("pmask_sb", [1, 1])
        self.sel8_sb = sbuf("sel8_sb", [8, 1])
        self.gwT = sbuf("gwT", [128, NG, NE], BF16)
        self.gbias_row = sbuf("gbias_row", [1, NE])
        self.gbias_bc = sbuf("gbias_bc", [128, NE])
        self.bnd_h1 = sbuf("bnd_h1", [128, NG, 1], BF16)
        self.bnd_h2 = sbuf("bnd_h2", [128, NG, 1], BF16)
        self.row_a = sbuf("row_a", [1, D])
        self.row_b = sbuf("row_b", [1, D])
        self.row_gx = sbuf("row_gx", [1, D])
        self.row_bx = sbuf("row_bx", [1, D])
        self.row16 = sbuf("row16", [1, D], BF16)
        self.rst = sbuf("rst", [1, 8])
        self.pq_j = sbuf("pq_j", [128, 8, 16])
        self.coefs = sbuf("coefs", [128, NG, 8])

        self.pb = [es.enter_context(nc.psum_tensor(f"pb{i}", [128, 512], F32))
                   for i in range(8)]
        self.pb16 = [p.ap().bitcast(BF16) for p in self.pb]

        for s in ("dG", "dS", "dA", "cT", "cV", "cS", "cG"):
            setattr(self, s, Sem(nc, es, s))
        self.fence = {e: Sem(nc, es, "f_" + e) for e in ENGS}

        blk = es.enter_context(nc.Block())

        with ExitStack() as stg_scope:
            pst = lambda n, s, d=BF16: stg_scope.enter_context(
                nc.sbuf_tensor(n, s, d))
            self.stg = [pst("stg0", [128, 4096])]
            self.wtb = pst("wtb", [128, 4096])

            self.p0()
            self.p1a_and_exchange()
            self.barrier()
            self.p1b()
            self.barrier()
            self.p2a()
            self.barrier()
            self.p2b()
            self.barrier()
        self.p3()
        self.finish()

        self.emit(blk)
        es.close()
        return nc

    def load_cvec(self, col, src):
        self.dma("gpsimd", self.cvr[:, col, :],
                 src.rearrange("o (g p) -> (p o) g", p=128),
                 allow_slow_non_contiguous=True)

    def p0(self):
        C, cv = self, self.cvr
        self.dma("sync", self.ident32[:], self.c_id)
        self.dma("sync", self.ones32[:], self.c_ones)
        self.dma("sync", self.rampsb[:], self.c_ramp)
        self.dma("sync", self.gbias_row[:], self.w_gb)
        self.dma("gpsimd", self.ident16[:], self.c_id)
        self.dma("gpsimd", self.ones16[:], self.c_ones)
        self.dma("gpsimd", self.a8sb[:], self.a8)
        self.dma("gpsimd", self.pmask_sb[:], self.pmask)
        self.dma("gpsimd", self.sel8_sb[:], self.sel8)
        self.dma("gpsimd", self.row_gx[:], self.wvec["ln_att_g"])
        self.dma("gpsimd", self.row_bx[:], self.wvec["ln_att_b"])
        for col, src in ((C.C_MK, "mk"), (C.C_MV, "mv"), (C.C_MR, "mr"),
                         (C.C_FMK, "fmk"), (C.C_FMR, "fmr"),
                         (C.C_LAG, "ln_att_g"), (C.C_LAB, "ln_att_b"),
                         (C.C_LFG, "ln_ffn_g"), (C.C_LFB, "ln_ffn_b"),
                         (C.C_TD, "td"), (C.C_TF, "tf")):
            self.load_cvec(col, self.wvec[src])
        self.dma("gpsimd", self.row_a[:], self.xp)
        gLD = self.dG.n
        self.gLD = gLD
        self.dS_consts = self.dS.n

        self.wait("scalar", self.dG, gLD)
        self.act(cv[:, C.C_W, :], cv[:, C.C_TD, :], Act.Exp)
        sW = self.cS.n
        self.wait("vector", self.cS, sW)
        self.ts("vector", cv[:, C.C_W, :], cv[:, C.C_W, :], -1.0, None, Alu.mult)
        vW = self.cV.n
        self.wait("scalar", self.cV, vW)
        self.act(cv[:, C.C_EW, :], cv[:, C.C_W, :], Act.Exp)
        self.act(cv[:, C.C_EU, :], cv[:, C.C_TF, :], Act.Exp)

        self.wait("tensor", self.dS)
        self.wait("tensor", self.dG, gLD)
        self.mm(self.pb[6][:, 0:TT], self.ones32[:], self.rampsb[:], True, True)
        self.mm(self.pb[7][:, 0:8], self.ones32[:], self.a8sb[:], True, True)
        self.mm(self.pb[7][:, 8:8 + NE], self.ones32[:], self.gbias_row[:],
                True, True)
        tBC = self.cT.n
        self.wait("vector", self.cT, tBC)
        self.vcopy(self.ramp01[:], self.pb[6][:, 0:TT])
        self.vcopy(self.a8bc[:], self.pb[7][:, 0:8])
        self.vcopy(self.gbias_bc[:], self.pb[7][:, 8:8 + NE])

        for b in (6, 7):
            self.guard_bank(b, "vector")
        self.pb_free_v = self.cV.n

    def build_gwT(self, gw16):
        self.wait("tensor", self.dG)
        for hc in range(NG):
            self.tr(self.pb16[5][:, hc * 64:hc * 64 + NE],
                    gw16[:, hc * 128:(hc + 1) * 128],
                    self.ident16[0:NE, 0:NE])
        tGW = self.cT.n
        self.wait("vector", self.cT, tGW)
        for hc in range(NG):
            self.vcopy(self.gwT[:, hc, :], self.pb16[5][:, hc * 64:hc * 64 + NE])
        self.guard_bank(5, "vector")

    def p0_staging(self):
        mats = []
        for el in range(EPL):
            mats.append((self.w_meg[el], 4, 8,
                         ("ccw", (el * 3 + 0) * EMAT, EMAT)))
            mats.append((self.w_meu[el], 4, 8,
                         ("ccw", (el * 3 + 1) * EMAT, EMAT)))
            mats.append((self.w_med[el], 8, 4,
                         ("ccw", (el * 3 + 2) * EMAT, EMAT)))
        for mi, src in enumerate((self.w_sg, self.w_su, self.w_sd)):
            for half in range(2):
                mats.append((src[:, half * 512:(half + 1) * 512], 8, 4,
                             ("shT", mi, half)))
        store_mark = 0
        for i, (src, nrb, ncb, dest) in enumerate(mats):
            stg = self.stg[0]
            if i >= 2:
                pass
            ldm = self.dma(
                "gpsimd",
                stg.ap().rearrange("p (rb c) -> p rb c", rb=nrb),
                src.rearrange("(rb p) c -> p rb c", p=128))
            self.wait("sync", self.dG, ldm)
            if i >= 1:
                self.wait("sync", self.dG, store_mark)
            for rb in range(nrb):
                for cb in range(ncb):
                    self.sched("sync", lambda e, rb=rb, cb=cb, nrb=nrb, ncb=ncb,
                               stg=stg: e.dma_start_transpose(
                                   out=self.wtb[:, (cb * nrb + rb) * 128:
                                                (cb * nrb + rb) * 128 + 128],
                                   in_=stg[:, (rb * ncb + cb) * 128:
                                           (rb * ncb + cb) * 128 + 128]),
                               inc=self.dS, k=16)
            trm = self.dS.n
            self.wait("gpsimd", self.dS, trm)
            if dest[0] == "ccw":
                _, off, n = dest
                store_mark = self.dma(
                    "gpsimd",
                    self.ccw_in[0, off:off + n].rearrange("(p f) -> p f", p=128),
                    self.wtb[:])
            else:
                _, mi, half = dest
                store_mark = self.dma(
                    "gpsimd", self.s_shT[mi, :, half * 4096:(half + 1) * 4096],
                    self.wtb[:])
        self.wait("gpsimd", self.dG)
        self.sched("gpsimd", lambda e: e.collective_compute(
            "AllGather", Alu.bypass, replica_groups=RG,
            ins=[self.ccw_in], outs=[self.ccw_out]), inc=self.cG)
        self.agw_done = self.cG.n

    def row_ln(self, dst_bnd, g_row, b_row):
        self.act(self.row_b[:], self.row_a[:], Act.Identity,
                 accum=self.rst[:, 0:1])
        self.act(self.row_b[:], self.row_a[:], Act.Square,
                 accum=self.rst[:, 1:2])
        sM = self.cS.n
        self.wait("vector", self.cS, sM)
        self.ln_finish(self.rst[:, 0:1], self.rst[:, 1:2], self.rst[:, 2:3],
                       self.rst[:, 3:4], self.rst[:, 4:5])
        sM2 = self.cS.n
        self.wait("vector", self.cS, sM2)
        self.ts("vector", self.row_b[:], self.row_a[:], self.rst[:, 2:3],
                self.rst[:, 3:4], Alu.subtract, Alu.mult)
        self.tt("vector", self.row_b[:], self.row_b[:], g_row, Alu.mult)
        self.tt("vector", self.row_b[:], self.row_b[:], b_row, Alu.add)
        self.ts("vector", self.row16[:], self.row_b[:],
                self.pmask_sb[0:1, 0:1], None, Alu.mult)
        vD = self.cV.n
        self.use_bank(6)
        self.wait("tensor", self.cV, vD)
        for g in range(NG):
            self.tr(self.pb16[6][:, g * 64:g * 64 + 1],
                    self.row16[0:1, g * 128:(g + 1) * 128],
                    self.ident16[0:1, 0:1])
        tD = self.cT.n
        self.wait("vector", self.cT, tD)
        for g in range(NG):
            self.vcopy(dst_bnd[:, g, 0:1], self.pb16[6][:, g * 64:g * 64 + 1])
        self.guard_bank(6, "vector")

    def p1a_and_exchange(self):
        C, cv = self, self.cvr
        nc = self.nc
        with ExitStack() as ph:
            psb = lambda n, s, d=F32: ph.enter_context(nc.sbuf_tensor(n, s, d))
            wk_sb = psb("wk_sb", [128, NG, D], BF16)
            wv_sb = psb("wv_sb", [128, NG, D], BF16)
            wr_sb = psb("wr_sb", [128, NG, D], BF16)
            x_tm = psb("x_tm", [128, NSUB, D])
            u0_tm = psb("u0_tm", [128, NSUB, D])
            st = psb("st", [128, NSUB, 10])
            h1_fm = psb("h1_fm", [128, NG, TT + 1], BF16)
            xmix = psb("xmix", [128, NG, TT], BF16)
            ek = psb("ek", [128, NG, TT])
            ekv = psb("ekv", [128, NG, TT])
            q_arr = psb("q_arr", [128, NG, TT + 1])
            p_arr = psb("p_arr", [128, NG, TT + 1])
            h1_tm = x_tm.ap().bitcast(BF16)[:, :, 0:D]
            r4 = x_tm.ap().bitcast(BF16)[:, :, D:2 * D].rearrange(
                "p a (g2 t) -> p a g2 t", g2=2)
            r16 = lambda oc: r4[:, oc // 2, oc % 2, :]
            diff = u0_tm.ap().bitcast(BF16).rearrange(
                "p a b -> p (a b)")[:, 0:NG * TT].rearrange(
                "p (g t) -> p g t", g=NG)
            flat = lambda a: a.ap().rearrange("p a b -> p (a b)")

            gw16 = psb("gw16", [NE, D], BF16)
            for dst, src in ((wk_sb, self.w_wk), (wv_sb, self.w_wv),
                             (wr_sb, self.w_wr)):
                self.dma("gpsimd", dst.ap(),
                         src.rearrange("(ic p) o -> p ic o", p=128))
            self.dma("gpsimd", gw16[:], self.w_gw)
            wLD = self.dG.n
            self.build_gwT(gw16)
            self.p0_staging()

            self.wait("scalar", self.dG, wLD)
            self.wait("vector", self.dG, wLD)
            self.wait("vector", self.dS, self.dS_consts)
            self.row_ln(self.bnd_h1, self.row_gx[:], self.row_bx[:])
            self.memset(q_arr[:, :, TT:TT + 1], 0.0)
            self.memset(p_arr[:, :, TT:TT + 1], 0.0)

            u0_spill = [0] * NT
            nd_spill = [0] * NT
            h1T_mark = [0] * NT
            mmr_mark = [0] * NT
            h1evac_mark = [0] * NT

            for t in range(NT):
                rows = self.xs[t * TT:(t + 1) * TT].rearrange(
                    "(s p) h -> p s h", p=128)
                if t > 0:
                    self.wait("scalar", self.cT, h1T_mark[t - 1])
                    self.wait("scalar", self.dA, nd_spill[t - 1])
                xm = self.dma("scalar", x_tm[:], rows)
                self.wait("scalar", self.dA, xm)
                if t > 0:
                    self.wait("scalar", self.dA, u0_spill[t - 1])
                for s in range(NSUB):
                    self.act(u0_tm[:, s, :], x_tm[:, s, :], Act.Identity,
                             accum=st[:, s, 0:1])
                    self.act(u0_tm[:, s, :], x_tm[:, s, :], Act.Square,
                             accum=st[:, s, 1:2])
                sStat = self.cS.n
                self.wait("vector", self.cS, sStat)
                self.ln_finish(st[:, :, 0], st[:, :, 1], st[:, :, 2],
                               st[:, :, 3], st[:, :, 4])
                sFin = self.cS.n
                self.wait("vector", self.cS, sFin)
                for s in range(NSUB):
                    self.ts("vector", u0_tm[:, s, :], x_tm[:, s, :],
                            st[:, s, 2:3], st[:, s, 3:4], Alu.subtract, Alu.mult)
                vU0 = self.cV.n
                self.wait("scalar", self.cV, vU0)
                u0_spill[t] = self.dma(
                    "scalar", self.s_u0[t * TT:(t + 1) * TT].rearrange(
                        "(s p) h -> p s h", p=128), u0_tm[:])
                for s in range(NSUB):
                    self.act(x_tm[:, s, :], u0_tm[:, s, :], Act.Identity,
                             accum=st[:, s, 5:6])
                    self.act(x_tm[:, s, :], u0_tm[:, s, :], Act.Square,
                             accum=st[:, s, 6:7])
                sStat2 = self.cS.n
                self.wait("vector", self.cS, sStat2)
                self.ln_finish(st[:, :, 5], st[:, :, 6], st[:, :, 7],
                               st[:, :, 8], st[:, :, 9])
                sFin2 = self.cS.n
                self.wait("vector", self.cS, sFin2)
                for s in range(NSUB):
                    self.ts("vector", h1_tm[:, s, :], u0_tm[:, s, :],
                            st[:, s, 7:8], st[:, s, 8:9], Alu.subtract, Alu.mult)
                vH1 = self.cV.n

                if t == 0:
                    self.vcopy(h1_fm[:, :, 0:1], self.bnd_h1[:])
                else:
                    self.wait("vector", self.cS, h1evac_mark[t - 1])
                    self.vcopy(h1_fm[:, :, 0:1], h1_fm[:, :, TT:TT + 1])
                self.wait("tensor", self.cV, self.cV.n)
                for s in range(NSUB):
                    self.use_bank(4 + s)
                for s in range(NSUB):
                    for g in range(NG):
                        self.tr(self.pb16[4 + s][:, g * 128:(g + 1) * 128],
                                h1_tm[:, s, g * 128:(g + 1) * 128],
                                self.ident16[:])
                h1T_mark[t] = self.cT.n
                self.wait("scalar", self.cT, h1T_mark[t])
                for s in range(NSUB):
                    for g in range(NG):
                        self.act(h1_fm[:, g, 1 + s * 128:1 + (s + 1) * 128],
                                 self.pb16[4 + s][:, g * 128:(g + 1) * 128],
                                 Act.Identity, scale=cv[:, C.C_LAG, g:g + 1],
                                 bias=cv[:, C.C_LAB, g:g + 1])
                h1evac_mark[t] = self.cS.n
                for s in range(NSUB):
                    self.guard_bank(4 + s, "scalar")

                if t > 0:
                    self.wait("vector", self.dA, nd_spill[t - 1])
                self.wait("vector", self.dA, u0_spill[t])
                self.wait("vector", self.cS, h1evac_mark[t])
                self.tt("vector", diff, h1_fm[:, :, 1:TT + 1],
                        h1_fm[:, :, 0:TT], Alu.subtract)
                if t == 0:
                    self.wait("tensor", self.dG, wLD)
                ek_marks = []
                if t > 0:
                    self.wait("scalar", self.dA, nd_spill[t - 1])
                for mat, (wsb, col) in enumerate(((wk_sb, C.C_MK),
                                                  (wv_sb, C.C_MV),
                                                  (wr_sb, C.C_MR))):
                    if not (t == 0 and mat == 0):
                        self.wait("vector", self.cT,
                                  self._xmix_use if mat == 0 else self.cT.n)
                    for g in range(NG):
                        self.stt("vector", xmix[:, g, :], diff[:, g, :],
                                 cv[:, col, g:g + 1], h1_fm[:, g, 0:TT],
                                 Alu.mult, Alu.add)
                    vMix = self.cV.n
                    self.wait("tensor", self.cV, vMix)
                    for oc in range(NG):
                        bank = (mat * NG + oc) % 3
                        self.use_bank(bank)
                        for ic in range(NG):
                            self.mm(self.pb[bank][:, 0:TT],
                                    wsb[:, ic, oc * 128:(oc + 1) * 128],
                                    xmix[:, ic, :], ic == 0, ic == NG - 1)
                        tMM = self.cT.n
                        self._xmix_use = tMM
                        if mat == 0:
                            self.wait("scalar", self.cT, tMM)
                            self.act(ek[:, oc, :], self.pb[bank][:, 0:TT],
                                     Act.Exp)
                            self.guard_bank(bank, "scalar")
                            ek_marks.append(self.cS.n)
                        elif mat == 1:
                            self.wait("vector", self.cT, tMM)
                            self.wait("vector", self.cS, ek_marks[oc])
                            self.tt("vector", ekv[:, oc, :],
                                    self.pb[bank][:, 0:TT], ek[:, oc, :],
                                    Alu.mult)
                            self.guard_bank(bank, "vector")
                        else:
                            self.wait("scalar", self.cT, tMM)
                            self.act(r16(oc), self.pb[bank][:, 0:TT],
                                     Act.Tanh, scale=0.5)
                            self.guard_bank(bank, "scalar")
                mmr_mark[t] = self.cT.n
                sR = self.cS.n
                self.wait("vector", self.cS, sR)
                self.ts("vector", r4, r4, 0.5, 0.5, Alu.mult, Alu.add)
                for g in range(NG):
                    self.vcopy(q_arr[:, g, 0:1], q_arr[:, g, TT:TT + 1])
                    self.vcopy(p_arr[:, g, 0:1], p_arr[:, g, TT:TT + 1])
                for g in range(NG):
                    self.sched("vector", lambda e, g=g: e.tensor_tensor_scan(
                        out=q_arr[:, g, 1:TT + 1],
                        data0=cv[:, C.C_EW, g:g + 1].to_broadcast([128, TT]),
                        data1=ek[:, g, :], initial=q_arr[:, g, 0:1],
                        op0=Alu.mult, op1=Alu.add), inc=self.cV)
                    self.sched("vector", lambda e, g=g: e.tensor_tensor_scan(
                        out=p_arr[:, g, 1:TT + 1],
                        data0=cv[:, C.C_EW, g:g + 1].to_broadcast([128, TT]),
                        data1=ekv[:, g, :], initial=p_arr[:, g, 0:1],
                        op0=Alu.mult, op1=Alu.add), inc=self.cV)
                for g in range(NG):
                    self.stt("vector", ekv[:, g, :], ekv[:, g, :],
                             cv[:, C.C_EU, g:g + 1], p_arr[:, g, 0:TT],
                             Alu.mult, Alu.add)
                    self.stt("vector", ek[:, g, :], ek[:, g, :],
                             cv[:, C.C_EU, g:g + 1], q_arr[:, g, 0:TT],
                             Alu.mult, Alu.add)
                vND = self.cV.n
                self.wait("scalar", self.cV, vND)
                self.dma("scalar", self.s_num[t], flat(ekv))
                self.dma("scalar", self.s_den[t], flat(ek))
                nd_spill[t] = self.dma("scalar", self.s_r[t], r4)

            self.vcopy(self.pq_j[:, 0, 0:NG], p_arr[:, :, TT])
            self.vcopy(self.pq_j[:, 0, NG:16], q_arr[:, :, TT])
            vPack = self.cV.n
            self.wait("gpsimd", self.cV, vPack)
            self.dma("gpsimd", self.cc1_in[0].rearrange("(p f) -> p f", p=128),
                     self.pq_j[:, 0, :])
            self.wait("gpsimd", self.dG)
            self.sched("gpsimd", lambda e: e.collective_compute(
                "AllGather", Alu.bypass, replica_groups=RG,
                ins=[self.cc1_in], outs=[self.cc1_out]), inc=self.cG)
            self.ag1_done = self.cG.n

    def exchange1_consume(self):
        C, cv = self, self.cvr
        self.wait("gpsimd", self.cG, self.ag1_done)
        for j in range(8):
            self.dma("gpsimd", self.pq_j[:, j, :],
                     self.cc1_out[j].rearrange("(p f) -> p f", p=128))
        ccLD = self.dG.n
        self.wait("scalar", self.dG, ccLD)
        for j in range(8):
            self.act(self.coefs[:, :, j], cv[:, C.C_W, :], Act.Exp,
                     scale=self.a8bc[:, j:j + 1])
        sCoef = self.cS.n
        self.wait("vector", self.cS, sCoef)
        self.wait("vector", self.dG, ccLD)
        self.memset(cv[:, C.C_PIN, :], 0.0)
        self.memset(cv[:, C.C_QIN, :], 0.0)
        self.vsync()
        for j in range(8):
            self.tt("vector", cv[:, C.C_TMP, :], self.coefs[:, :, j],
                    self.pq_j[:, j, 0:NG], Alu.mult)
            self.vsync()
            self.tt("vector", cv[:, C.C_PIN, :], cv[:, C.C_PIN, :],
                    cv[:, C.C_TMP, :], Alu.add)
            self.vsync()
            self.tt("vector", cv[:, C.C_TMP, :], self.coefs[:, :, j],
                    self.pq_j[:, j, NG:16], Alu.mult)
            self.vsync()
            self.tt("vector", cv[:, C.C_QIN, :], cv[:, C.C_QIN, :],
                    cv[:, C.C_TMP, :], Alu.add)
            self.vsync()

    def p1b(self):
        C, cv = self, self.cvr
        nc = self.nc
        with ExitStack() as ph:
            psb = lambda n, s, d=F32: ph.enter_context(nc.sbuf_tensor(n, s, d))
            wo_sb = psb("wo_sb", [128, NG, D], BF16)
            nbufs = [psb("nbuf0", [128, NG, TT])] * 2
            dbufs = [psb("dbuf0", [128, NG, TT])] * 2
            rbufs = [psb("rbuf0", [128, NG, TT], BF16)] * 2
            u0_tms = [psb("u0b_tm0", [128, NSUB, D])] * 2
            Dfm = psb("Dfm", [128, NG, TT])
            a_fm = psb("a_fm", [128, NG, TT])
            wkv16 = Dfm.ap().bitcast(BF16)[:, :, 0:TT]
            xb_tm = psb("xb_tm", [128, NSUB, D])
            flat = lambda a: a.ap().rearrange("p a b -> p (a b)")

            self.dma("gpsimd", wo_sb.ap(),
                     self.w_wo.rearrange("(ic p) o -> p ic o", p=128))
            wLD = self.dG.n

            consume_mark = [0] * NT
            aT_mark = [0] * NT
            wo_mm_mark = [0] * NT
            order = [2, 3, 4, 5, 6, 7, 0, 1]
            prev = -1
            for t in order:
                if t == 0:
                    self.exchange1_consume()
                nbuf, dbuf = nbufs[t % 2], dbufs[t % 2]
                rbuf, u0_tm = rbufs[t % 2], u0_tms[t % 2]
                if prev >= 0:
                    self.wait("scalar", self.cT, wo_mm_mark[prev])
                    self.wait("scalar", self.cV, consume_mark[prev])
                m1 = self.dma("scalar", flat(nbuf), self.s_num[t])
                m2 = self.dma("scalar", flat(dbuf), self.s_den[t])
                if prev >= 0:
                    self.wait("scalar", self.cV, consume_mark[prev])
                m3 = self.dma("scalar", flat(rbuf), self.s_r[t])
                if prev >= 0:
                    self.wait("scalar", self.cV, aT_mark[prev])
                m4 = self.dma("scalar", u0_tm[:],
                              self.s_u0[t * TT:(t + 1) * TT].rearrange(
                                  "(s p) h -> p s h", p=128))
                self.wait("vector", self.dA, m2)
                if t < 2:
                    self.ts("vector", cv[:, C.C_WTB, :], cv[:, C.C_W, :],
                            float(t * TT), None, Alu.mult)
                    vWT = self.cV.n
                    self.wait("scalar", self.cV, vWT)
                    for g in range(NG):
                        self.act(Dfm[:, g, :], self.ramp01[:], Act.Exp,
                                 scale=cv[:, C.C_W, g:g + 1],
                                 bias=cv[:, C.C_WTB, g:g + 1])
                    sD = self.cS.n
                    self.wait("vector", self.cS, sD)
                    for g in range(NG):
                        self.stt("vector", nbuf[:, g, :], Dfm[:, g, :],
                                 cv[:, C.C_PIN, g:g + 1], nbuf[:, g, :],
                                 Alu.mult, Alu.add)
                        self.stt("vector", dbuf[:, g, :], Dfm[:, g, :],
                                 cv[:, C.C_QIN, g:g + 1], dbuf[:, g, :],
                                 Alu.mult, Alu.add)
                self.sched("vector", lambda e: e.reciprocal(
                    out=flat(dbuf), in_=flat(dbuf)), inc=self.cV)
                self.tt("vector", wkv16, nbuf.ap(), dbuf.ap(), Alu.mult)
                self.wait("vector", self.dA, m3)
                self.tt("vector", wkv16, wkv16, rbuf.ap(), Alu.mult)
                consume_mark[t] = self.cV.n

                self.wait("tensor", self.cV, consume_mark[t])
                if t == 0:
                    self.wait("tensor", self.dG, wLD)
                for oc in range(NG):
                    bank = oc % 3
                    self.use_bank(bank)
                    for ic in range(NG):
                        self.mm(self.pb[bank][:, 0:TT],
                                wo_sb[:, ic, oc * 128:(oc + 1) * 128],
                                wkv16[:, ic, :], ic == 0, ic == NG - 1)
                    tMM = self.cT.n
                    self.wait("vector", self.cT, tMM)
                    self.vcopy(a_fm[:, oc, :], self.pb[bank][:, 0:TT])
                    self.guard_bank(bank, "vector")
                vA = self.cV.n
                wo_mm_mark[t] = self.cT.n

                self.wait("tensor", self.cV, vA)
                self.wait("scalar", self.dA, m4)
                for s in range(NSUB):
                    b0, b1 = 4 + 2 * (s % 2), 5 + 2 * (s % 2)
                    self.use_bank(b0)
                    self.use_bank(b1)
                    for g in range(NG):
                        bank = b0 if g < 4 else b1
                        self.tr(self.pb[bank].ap()[:, (g % 4) * 128:
                                                   (g % 4) * 128 + 128],
                                a_fm[:, g, s * 128:(s + 1) * 128],
                                self.ident32[:])
                    tS = self.cT.n
                    self.wait("vector", self.cT, tS)
                    self.tt("vector", xb_tm[:, s, 0:512], self.pb[b0][:],
                            u0_tm[:, s, 0:512], Alu.add)
                    self.tt("vector", xb_tm[:, s, 512:1024], self.pb[b1][:],
                            u0_tm[:, s, 512:1024], Alu.add)
                    self.guard_bank(b0, "vector")
                    self.guard_bank(b1, "vector")
                aT_mark[t] = self.cV.n
                self.wait("scalar", self.cV, aT_mark[t])
                self.dma("scalar", self.s_xb[t * TT:(t + 1) * TT].rearrange(
                    "(s p) h -> p s h", p=128), xb_tm[:])
                if t == NT - 1:
                    self.wait("gpsimd", self.cV, aT_mark[t])
                    self.dma("gpsimd", self.cc2_in, xb_tm[127:128, 3, :])
                    self.wait("gpsimd", self.dG)
                    self.sched("gpsimd", lambda e: e.collective_compute(
                        "AllGather", Alu.bypass, replica_groups=RG,
                        ins=[self.cc2_in], outs=[self.cc2_out]), inc=self.cG)
                    self.ag2_done = self.cG.n
                prev = t

    def p2a(self):
        C, cv = self, self.cvr
        nc = self.nc
        with ExitStack() as ph:
            psb = lambda n, s, d=F32: ph.enter_context(nc.sbuf_tensor(n, s, d))
            fwk_sb = psb("fwk_sb", [128, NG, 4 * D], BF16)
            fwr_sb = psb("fwr_sb", [128, NG, D], BF16)
            xb_tm = psb("xb2_tm", [128, NSUB, D])
            st = psb("st2", [128, NSUB, 6])
            h2_fm = psb("h2_fm", [128, NG, TT + 1], BF16)
            kf16 = psb("kf16", [128, 32, TT], BF16)
            xk2 = psb("xk2", [128, NG, TT], BF16)
            xr2 = psb("xr2", [128, NG, TT], BF16)
            ktmp = psb("ktmp", [128, 2, TT])
            cc2_sb = psb("cc2_sb", [8, D])
            h2_tm = kf16.ap()[:, 8:16, :].rearrange(
                "p (a two) t -> p a (two t)", two=2)
            rf4 = xb_tm.ap().bitcast(BF16)[:, :, D:2 * D].rearrange(
                "p a (g2 t) -> p a g2 t", g2=2)
            rf16 = lambda oc: rf4[:, oc // 2, oc % 2, :]
            diff = kf16.ap()[:, 0:NG, 0:TT]
            flat = lambda a: a.ap().rearrange("p a b -> p (a b)")

            self.dma("gpsimd", fwk_sb.ap(),
                     self.w_fwk.rearrange("(ic p) o -> p ic o", p=128))
            self.dma("gpsimd", fwr_sb.ap(),
                     self.w_fwr.rearrange("(ic p) o -> p ic o", p=128))
            wLD = self.dG.n
            self.dma("gpsimd", self.row_gx[:], self.wvec["ln_ffn_g"])
            self.dma("gpsimd", self.row_bx[:], self.wvec["ln_ffn_b"])
            rowLD = self.dG.n

            self.wait("gpsimd", self.cG, self.ag2_done)
            ccm = self.dma("gpsimd", cc2_sb[:], self.cc2_out)
            self.wait("tensor", self.dG, ccm)
            self.use_bank(6)
            self.use_bank(7)
            self.mm(self.pb[6][0:1, 0:512], self.sel8_sb[:],
                    cc2_sb[:, 0:512], True, True)
            self.mm(self.pb[7][0:1, 0:512], self.sel8_sb[:],
                    cc2_sb[:, 512:1024], True, True)
            tSel = self.cT.n
            self.wait("vector", self.cT, tSel)
            self.vcopy(self.row_a[0:1, 0:512], self.pb[6][0:1, 0:512])
            self.vcopy(self.row_a[0:1, 512:1024], self.pb[7][0:1, 0:512])
            self.guard_bank(6, "vector")
            self.guard_bank(7, "vector")
            self.wait("vector", self.dG, rowLD)
            self.row_ln(self.bnd_h2, self.row_gx[:], self.row_bx[:])

            kf_spill = [0] * NT
            h2T_mark = [0] * NT
            h2evac_marks = [0] * NT
            mm_mark = [0] * NT
            for t in range(NT):
                if t > 0:
                    self.wait("scalar", self.cT, h2T_mark[t - 1])
                    self.wait("scalar", self.dA, kf_spill[t - 1])
                xm = self.dma("scalar", xb_tm[:],
                              self.s_xb[t * TT:(t + 1) * TT].rearrange(
                                  "(s p) h -> p s h", p=128))
                self.wait("scalar", self.dA, xm)
                kdump = ktmp.ap().rearrange("p a b -> p (a b)")
                for s in range(NSUB):
                    self.act(kdump, xb_tm[:, s, :],
                             Act.Identity, accum=st[:, s, 0:1])
                    self.act(kdump, xb_tm[:, s, :],
                             Act.Square, accum=st[:, s, 1:2])
                sStat = self.cS.n
                self.wait("vector", self.cS, sStat)
                self.ln_finish(st[:, :, 0], st[:, :, 1], st[:, :, 2],
                               st[:, :, 3], st[:, :, 4])
                sFin = self.cS.n
                self.wait("vector", self.cS, sFin)
                if t > 0:
                    self.wait("vector", self.dA, kf_spill[t - 1])
                for s in range(NSUB):
                    self.ts("vector", h2_tm[:, s, :], xb_tm[:, s, :],
                            st[:, s, 2:3], st[:, s, 3:4], Alu.subtract, Alu.mult)
                vH2 = self.cV.n
                if t == 0:
                    self.vcopy(h2_fm[:, :, 0:1], self.bnd_h2[:])
                else:
                    self.wait("vector", self.cS, h2evac_marks[t - 1])
                    self.vcopy(h2_fm[:, :, 0:1], h2_fm[:, :, TT:TT + 1])
                self.wait("tensor", self.cV, self.cV.n)
                for s in range(NSUB):
                    self.use_bank(4 + s)
                for s in range(NSUB):
                    for g in range(NG):
                        self.tr(self.pb16[4 + s][:, g * 128:(g + 1) * 128],
                                h2_tm[:, s, g * 128:(g + 1) * 128],
                                self.ident16[:])
                h2T_mark[t] = self.cT.n
                self.wait("scalar", self.cT, h2T_mark[t])
                for s in range(NSUB):
                    for g in range(NG):
                        self.act(h2_fm[:, g, 1 + s * 128:1 + (s + 1) * 128],
                                 self.pb16[4 + s][:, g * 128:(g + 1) * 128],
                                 Act.Identity, scale=cv[:, C.C_LFG, g:g + 1],
                                 bias=cv[:, C.C_LFB, g:g + 1])
                h2evac_marks[t] = self.cS.n
                for s in range(NSUB):
                    self.guard_bank(4 + s, "scalar")

                if t > 0:
                    self.wait("vector", self.dA, kf_spill[t - 1])
                self.wait("vector", self.cS, h2evac_marks[t])
                self.tt("vector", diff, h2_fm[:, :, 1:TT + 1],
                        h2_fm[:, :, 0:TT], Alu.subtract)
                if t > 0:
                    self.wait("vector", self.cT, mm_mark[t - 1])
                for dst, col in ((xk2, C.C_FMK), (xr2, C.C_FMR)):
                    for g in range(NG):
                        self.stt("vector", dst[:, g, :], diff[:, g, :],
                                 cv[:, col, g:g + 1], h2_fm[:, g, 0:TT],
                                 Alu.mult, Alu.add)
                vMix = self.cV.n
                if "h2" in self.taps:
                    self.wait("scalar", self.cV, vMix)
                    self.dma("scalar",
                             self.s_h2[t].rearrange("p (g c) -> p g c", g=NG),
                             h2_fm[:, :, 1:TT + 1])
                    self.dma("scalar", self.s_xk2[t], flat(xk2))

                self.wait("tensor", self.cV, vMix)
                if t == 0:
                    self.wait("tensor", self.dG, wLD)
                kf_sq_mark = [0, 0]
                for oc in range(32):
                    bank = oc % 3
                    self.use_bank(bank)
                    for ic in range(NG):
                        self.mm(self.pb[bank][:, 0:TT],
                                fwk_sb[:, ic, oc * 128:(oc + 1) * 128],
                                xk2[:, ic, :], ic == 0, ic == NG - 1)
                    tMM = self.cT.n
                    self.wait("scalar", self.cT, tMM)
                    slot = oc % 2
                    self.act(ktmp[:, slot, :], self.pb[bank][:, 0:TT], Act.Relu)
                    self.guard_bank(bank, "scalar")
                    self.act(kf16[:, oc, :], ktmp[:, slot, :], Act.Square)
                    kf_sq_mark[slot] = self.cS.n
                for oc in range(NG):
                    bank = oc % 3
                    self.use_bank(bank)
                    for ic in range(NG):
                        self.mm(self.pb[bank][:, 0:TT],
                                fwr_sb[:, ic, oc * 128:(oc + 1) * 128],
                                xr2[:, ic, :], ic == 0, ic == NG - 1)
                    tMM = self.cT.n
                    self.wait("scalar", self.cT, tMM)
                    self.act(rf16(oc), self.pb[bank][:, 0:TT],
                             Act.Tanh, scale=0.5)
                    self.guard_bank(bank, "scalar")
                mm_mark[t] = self.cT.n
                sRF = self.cS.n
                self.wait("vector", self.cS, sRF)
                self.ts("vector", rf4, rf4, 0.5, 0.5, Alu.mult, Alu.add)
                vRF = self.cV.n
                self.wait("scalar", self.cV, vRF)
                self.dma("scalar", self.s_kf[t], flat(kf16))
                kf_spill[t] = self.dma("scalar", self.s_rf[t], rf4)

    def p2b(self):
        C, cv = self, self.cvr
        nc = self.nc
        with ExitStack() as ph:
            psb = lambda n, s, d=F32: ph.enter_context(nc.sbuf_tensor(n, s, d))
            fwv_sb = psb("fwv_sb", [128, 32, D], BF16)
            kf16 = psb("kf16b", [128, 32, TT], BF16)
            xb_tm = psb("xb3_tm", [128, NSUB, D])
            x0_tm = psb("x0_tm", [128, NSUB, D])
            f_fm = psb("f_fm", [128, NG, TT])
            x1_tm = psb("x1_tm", [128, NSUB, D])
            st = psb("st3", [128, NSUB, 6])
            u1_tm = x0_tm.ap().bitcast(BF16)[:, :, 0:D]
            rf4 = x1_tm.ap().bitcast(BF16)[:, :, 0:D].rearrange(
                "p a (g2 t) -> p a g2 t", g2=2)
            rf16 = lambda oc: rf4[:, oc // 2, oc % 2, :]
            u1buf = f_fm.ap().bitcast(BF16)[:, 4:8, 0:2 * TT].rearrange(
                "p a (g2 t) -> p (a g2) t", g2=2)
            flat = lambda a: a.ap().rearrange("p a b -> p (a b)")

            self.dma("gpsimd", fwv_sb.ap(),
                     self.w_fwv.rearrange("(ic p) o -> p ic o", p=128))
            wLD = self.dG.n

            ld_mark = [0] * NT
            u1T_mark = [0] * NT
            u1_spill = [0] * NT
            mm_mark = [0] * NT
            asm_mark = [0] * NT
            for t in range(NT):
                if t > 0:
                    self.wait("scalar", self.cT, mm_mark[t - 1])
                m1 = self.dma("scalar", flat(kf16), self.s_kf[t])
                if t > 0:
                    self.wait("scalar", self.cV, ld_mark[t - 1])
                m2 = self.dma("scalar", rf4, self.s_rf[t])
                if t > 0:
                    self.wait("scalar", self.cV, asm_mark[t - 1])
                m3 = self.dma("scalar", xb_tm[:],
                              self.s_xb[t * TT:(t + 1) * TT].rearrange(
                                  "(s p) h -> p s h", p=128))
                if t > 0:
                    self.wait("scalar", self.cT, u1T_mark[t - 1])
                m4 = self.dma("scalar", x0_tm[:],
                              self.xs[t * TT:(t + 1) * TT].rearrange(
                                  "(s p) h -> p s h", p=128))
                self.wait("tensor", self.dA, m1)
                if t == 0:
                    self.wait("tensor", self.dG, wLD)
                if t > 0:
                    self.wait("vector", self.dA, u1_spill[t - 1])
                for oc in range(NG):
                    bank = oc % 3
                    self.use_bank(bank)
                    for ic in range(32):
                        self.mm(self.pb[bank][:, 0:TT],
                                fwv_sb[:, ic, oc * 128:(oc + 1) * 128],
                                kf16[:, ic, :], ic == 0, ic == 31)
                    tMM = self.cT.n
                    self.wait("vector", self.cT, tMM)
                    if oc == 0:
                        self.wait("vector", self.dA, m2)
                    self.tt("vector", f_fm[:, oc, :], self.pb[bank][:, 0:TT],
                            rf16(oc), Alu.mult)
                    self.guard_bank(bank, "vector")
                vF = self.cV.n
                mm_mark[t] = self.cT.n

                self.wait("tensor", self.cV, vF)
                self.wait("vector", self.dA, m3)
                for s in range(NSUB):
                    b0, b1 = 4 + 2 * (s % 2), 5 + 2 * (s % 2)
                    self.use_bank(b0)
                    self.use_bank(b1)
                    for g in range(NG):
                        bank = b0 if g < 4 else b1
                        self.tr(self.pb[bank].ap()[:, (g % 4) * 128:
                                                   (g % 4) * 128 + 128],
                                f_fm[:, g, s * 128:(s + 1) * 128],
                                self.ident32[:])
                    tS = self.cT.n
                    self.wait("vector", self.cT, tS)
                    self.tt("vector", x1_tm[:, s, 0:512], self.pb[b0][:],
                            xb_tm[:, s, 0:512], Alu.add)
                    self.tt("vector", x1_tm[:, s, 512:1024], self.pb[b1][:],
                            xb_tm[:, s, 512:1024], Alu.add)
                    self.guard_bank(b0, "vector")
                    self.guard_bank(b1, "vector")
                self.wait("vector", self.dA, m4)
                self.tt("vector", flat(x1_tm), flat(x1_tm), flat(x0_tm), Alu.add)
                vX1 = self.cV.n
                asm_mark[t] = vX1
                self.wait("scalar", self.cV, vX1)
                self.dma("scalar", self.s_x1[t * TT:(t + 1) * TT].rearrange(
                    "(s p) h -> p s h", p=128), x1_tm[:])

                fdump = f_fm.ap()[:, 0:2, :].rearrange("p a b -> p (a b)")
                for s in range(NSUB):
                    self.act(fdump, x1_tm[:, s, :], Act.Identity,
                             accum=st[:, s, 0:1])
                    self.act(fdump, x1_tm[:, s, :], Act.Square,
                             accum=st[:, s, 1:2])
                sStat = self.cS.n
                self.wait("vector", self.cS, sStat)
                self.ln_finish(st[:, :, 0], st[:, :, 1], st[:, :, 2],
                               st[:, :, 3], st[:, :, 4])
                sFin = self.cS.n
                self.wait("vector", self.cS, sFin)
                for s in range(NSUB):
                    self.ts("vector", u1_tm[:, s, :], x1_tm[:, s, :],
                            st[:, s, 2:3], st[:, s, 3:4], Alu.subtract, Alu.mult)
                vU1 = self.cV.n
                ld_mark[t] = vU1
                self.wait("tensor", self.cV, vU1)
                for s in range(NSUB):
                    self.use_bank(4 + s)
                for s in range(NSUB):
                    for g in range(NG):
                        self.tr(self.pb16[4 + s][:, g * 128:(g + 1) * 128],
                                u1_tm[:, s, g * 128:(g + 1) * 128],
                                self.ident16[:])
                u1T_mark[t] = self.cT.n
                self.wait("vector", self.cT, u1T_mark[t])
                for s in range(NSUB):
                    self.vcopy(
                        u1buf[:, :, s * 128:(s + 1) * 128],
                        self.pb16[4 + s].rearrange("p (g c) -> p g c", g=NG))
                    self.guard_bank(4 + s, "vector")
                vU1e = self.cV.n
                self.wait("scalar", self.cV, vU1e)
                u1_spill[t] = self.dma("scalar", self.s_u1[t],
                                       u1buf.rearrange("p g t -> p (g t)"))

    def p3(self):
        C, cv = self, self.cvr
        nc = self.nc
        with ExitStack() as ph:
            psb = lambda n, s, d=F32: ph.enter_context(nc.sbuf_tensor(n, s, d))
            shg = psb("shg", [128, NG, D], BF16)
            shu = psb("shu", [128, NG, D], BF16)
            shd = shg
            u1q = psb("u1q", [128, NG, QTR], BF16)
            acc = psb("acc", [128, NG, QTR])
            gssh = psb("gssh", [128, NG, QTR], BF16)
            wge = psb("wge0", [128, NG, DI], BF16)
            wue = psb("wue0", [128, NG, DI], BF16)
            wde = psb("wde0", [128, 4, D], BF16)
            gs16 = psb("gs16", [128, 4, QTR], BF16)
            wb_sb = psb("wb_sb", [128, QTR])
            sc_all = psb("sc_all", [128, 8, NE])
            wmat = psb("wmat", [128, 8, NE])
            wfm = psb("wfm", [32, QTR])
            wfm16 = psb("wfm16", [32, QTR], BF16)
            wrow = psb("wrow", [1, 2, QTR], BF16)
            gsum8 = psb("gsum8", [128, 8])
            m8 = psb("m8", [128, 8])
            msk = psb("msk", [128, NE])
            stmp = psb("stmp", [128, 8, TT])
            utmp = gssh.ap().bitcast(F32)
            dtmp = psb("dtmp", [128, 2, TT])
            x1_tm = psb("x1b_tm", [128, NSUB, D])
            flat = lambda a: a.ap().rearrange("p a b -> p (a b)")
            self.memset(gsum8[:, 4:8], -1e30)
            self.wait("sync", self.cG, self.agw_done)
            self.wait("gpsimd", self.cG, self.agw_done)

            self._a_done = 0
            self._d_done = 0
            self._d_mms = 0
            self._gs_read = [0, 0]
            self._shg_use = 0
            self._wrow_use = 0
            u1q_use = 0
            ylast = 0
            for q in range(NQ):
                self.wait("sync", self.cT, u1q_use)
                uq = self.dma("sync", u1q[:, :, 0:TT], self.s_u1[2 * q])
                uq = self.dma("sync", u1q[:, :, TT:QTR], self.s_u1[2 * q + 1])
                self.wait("tensor", self.dS, uq)

                for i in range(8):
                    self.use_bank(3)
                    for ic in range(NG):
                        self.mm(self.pb[3][:, i * 64:i * 64 + NE],
                                u1q[:, ic, i * 128:(i + 1) * 128],
                                self.gwT[:, ic, :], ic == 0, ic == NG - 1)
                tLG = self.cT.n
                self.wait("scalar", self.cT, tLG)
                for i in range(8):
                    self.act(sc_all[:, i, :], self.pb[3][:, i * 64:i * 64 + NE],
                             Act.Tanh, scale=0.5)
                self.guard_bank(3, "scalar")
                sLG = self.cS.n
                self.wait("vector", self.cS, sLG)
                self.ts("vector", flat(sc_all), flat(sc_all), 0.5, 0.5,
                        Alu.mult, Alu.add)
                for i in range(8):
                    sc = sc_all[:, i, :]
                    wm = wmat[:, i, :]
                    self.tt("vector", wm, sc, self.gbias_bc[:], Alu.add)
                    self.vsync()
                    for grp in range(4):
                        self.sched("vector", lambda e, grp=grp, wm=wm: e.max(
                            out=m8[:], in_=wm[:, grp * 8:(grp + 1) * 8]),
                            inc=self.cV)
                        self.vsync()
                        self.tt("vector", gsum8[:, grp:grp + 1], m8[:, 0:1],
                                m8[:, 1:2], Alu.add)
                        self.vsync()
                    self.sched("vector", lambda e: e.max(
                        out=m8[:], in_=gsum8[:]), inc=self.cV)
                    self.vsync()
                    self.sched("vector", lambda e: e.tensor_single_scalar(
                        out=gsum8[:, 0:4], in_=gsum8[:, 0:4],
                        scalar=m8[:, 1:2], op=Alu.is_ge), inc=self.cV)
                    self.vsync()
                    self.tt("vector",
                            msk.ap().rearrange("p (g e) -> p g e", g=4),
                            gsum8.ap()[:, 0:4].unsqueeze(2).to_broadcast(
                                [128, 4, 8]),
                            wm.rearrange("p (g e) -> p g e", g=4), Alu.mult)
                    self.vsync()
                    self.sched("vector", lambda e: e.max(
                        out=m8[:], in_=msk[:]), inc=self.cV)
                    self.vsync()
                    self.sched("vector", lambda e: e.tensor_single_scalar(
                        out=msk[:], in_=msk[:], scalar=m8[:, 7:8],
                        op=Alu.is_ge), inc=self.cV)
                    self.vsync()
                    self.tt("vector", wm, sc, msk[:], Alu.mult)
                    self.vsync()
                    self.sched("vector", lambda e, wm=wm: e.reduce_sum(
                        out=m8[:, 0:1], in_=wm, axis=mybir.AxisListType.X),
                        inc=self.cV)
                    self.vsync()
                    self.ts("vector", m8[:, 0:1], m8[:, 0:1], 1e-20, None,
                            Alu.add)
                    self.vsync()
                    self.sched("vector", lambda e: e.reciprocal(
                        out=m8[:, 1:2], in_=m8[:, 0:1]), inc=self.cV)
                    self.vsync()
                    self.ts("vector", m8[:, 1:2], m8[:, 1:2], 2.5, None,
                            Alu.mult)
                    self.vsync()
                    self.sched("vector", lambda e, wm=wm: e.tensor_single_scalar(
                        out=wm, in_=wm, scalar=m8[:, 1:2], op=Alu.mult),
                        inc=self.cV)
                    self.vsync()
                vGate = self.cV.n
                self.wait("sync", self.cT, self._shg_use)
                self.dma("sync", flat(shg), self.s_shT[0])
                shLD = self.dma("sync", flat(shu), self.s_shT[1])
                self.wait("tensor", self.dS, shLD)
                sh_mark = [0, 0, 0, 0]
                for tk in range(2):
                    for oc in range(NG):
                        b0 = (oc % 2) * 2
                        b1 = b0 + 1
                        self.use_bank(b0)
                        self.use_bank(b1)
                        for ic in range(NG):
                            self.mm(self.pb[b0][:, 0:TT],
                                    shg[:, ic, oc * 128:(oc + 1) * 128],
                                    u1q[:, ic, tk * TT:(tk + 1) * TT],
                                    ic == 0, ic == NG - 1)
                        for ic in range(NG):
                            self.mm(self.pb[b1][:, 0:TT],
                                    shu[:, ic, oc * 128:(oc + 1) * 128],
                                    u1q[:, ic, tk * TT:(tk + 1) * TT],
                                    ic == 0, ic == NG - 1)
                        tMM = self.cT.n
                        self.wait("scalar", self.cT, tMM)
                        self.wait("scalar", self.cV, sh_mark[oc % 4])
                        self.act(stmp[:, oc % 4, :], self.pb[b0][:, 0:TT],
                                 Act.Silu)
                        self.guard_bank(b0, "scalar")
                        sSi = self.cS.n
                        self.wait("vector", self.cS, sSi)
                        self.tt("vector", stmp[:, 4 + oc % 4, :],
                                self.pb[b1][:, 0:TT], stmp[:, oc % 4, :],
                                Alu.mult)
                        self.guard_bank(b1, "vector")
                        self.vcopy(gssh[:, oc, tk * TT:(tk + 1) * TT],
                                   stmp[:, 4 + oc % 4, :])
                        sh_mark[oc % 4] = self.cV.n
                vSh = self.cV.n
                guDone = self.cT.n
                self.wait("sync", self.cT, guDone)
                shdLD = self.dma("sync", flat(shd), self.s_shT[2])
                self.wait("tensor", self.cV, vGate)
                self.use_bank(2)
                self.use_bank(3)
                for i in range(8):
                    bk = 2 + i // 4
                    self.tr(self.pb[bk][0:NE, (i % 4) * 128:(i % 4) * 128 + 128],
                            wmat[:, i, :], self.ident32[:])
                tWM = self.cT.n
                self.wait("vector", self.cT, tWM)
                for i in range(8):
                    bk = 2 + i // 4
                    self.vcopy(wfm[:, i * 128:(i + 1) * 128],
                               self.pb[bk][0:NE, (i % 4) * 128:(i % 4) * 128 + 128])
                self.guard_bank(2, "vector")
                self.guard_bank(3, "vector")
                self.vcopy(wfm16[:], wfm[:])
                vWF = self.cV.n
                self.wait("tensor", self.cV, vSh)
                self.wait("tensor", self.dS, shdLD)
                for tk in range(2):
                    for oc in range(NG):
                        bank = oc % 2
                        self.use_bank(bank)
                        for ic in range(NG):
                            self.mm(self.pb[bank][:, 0:TT],
                                    shd[:, ic, oc * 128:(oc + 1) * 128],
                                    gssh[:, ic, tk * TT:(tk + 1) * TT],
                                    ic == 0, ic == NG - 1)
                        tMM = self.cT.n
                        self.wait("vector", self.cT, tMM)
                        self.vcopy(acc[:, oc, tk * TT:(tk + 1) * TT],
                                   self.pb[bank][:, 0:TT])
                        self.guard_bank(bank, "vector")
                self._shg_use = self.cT.n

                dn_marks = {}
                vShD = self.cV.n
                self.wait("gpsimd", self.cV, vShD)
                self.wait("scalar", self.cV, vWF)
                self.wait("scalar", self.cT, self._wrow_use)
                wrow_mark = [self.dma("scalar", wrow[:, 0, :], wfm16[0:1, :]), 0]
                wrow_mark_use = [0, 0]
                for e in range(NE):
                    self.wait("sync", self.cT, self._a_done)
                    for part, dst in ((0, wge), (1, wue)):
                        src = self.ccw_out[e // EPL,
                                           ((e % EPL) * 3 + part) * EMAT:
                                           ((e % EPL) * 3 + part + 1) * EMAT]
                        ldGU = self.dma("sync", flat(dst),
                                        src.rearrange("(p f) -> p f", p=128))
                    self.wait("sync", self.cT, self._d_done)
                    srcd = self.ccw_out[e // EPL,
                                        ((e % EPL) * 3 + 2) * EMAT:
                                        ((e % EPL) * 3 + 3) * EMAT]
                    ldD = self.dma("sync", flat(wde),
                                   srcd.rearrange("(p f) -> p f", p=128))
                    self.wait("tensor", self.cV, vWF)
                    self.wait("tensor", self.dA, wrow_mark[e % 2])
                    self.use_bank(6)
                    self.use_bank(7)
                    self.mm(self.pb[6][:, 0:TT], self.ones16[:],
                            wrow[:, e % 2, 0:TT], True, True)
                    self.mm(self.pb[7][:, 0:TT], self.ones16[:],
                            wrow[:, e % 2, TT:QTR], True, True)
                    self._wrow_use = self.cT.n
                    tWB = self.cT.n
                    if e + 1 < NE:
                        self.wait("scalar", self.cT, wrow_mark_use[(e + 1) % 2])
                        wrow_mark[(e + 1) % 2] = self.dma(
                            "scalar", wrow[:, (e + 1) % 2, :],
                            wfm16[e + 1:e + 2, :])
                    wrow_mark_use[e % 2] = self.cT.n
                    self.wait("vector", self.cT, tWB)
                    self.vcopy(wb_sb[:, 0:TT], self.pb[6][:, 0:TT])
                    self.vcopy(wb_sb[:, TT:QTR], self.pb[7][:, 0:TT])
                    self.guard_bank(6, "vector")
                    self.guard_bank(7, "vector")
                    vWB = self.cV.n

                    self.wait("tensor", self.dS, ldGU)
                    gu_marks = {}
                    for tk in range(2):
                        for oc in range(4):
                            self.use_bank(4 * tk + oc)
                            for ic in range(NG):
                                self.mm(self.pb[4 * tk + oc][:, 0:TT],
                                        wge[:, ic, oc * 128:(oc + 1) * 128],
                                        u1q[:, ic, tk * TT:(tk + 1) * TT],
                                        ic == 0, ic == NG - 1)
                        gu_marks[("g", tk)] = self.cT.n
                        self.wait("scalar", self.cT, gu_marks[("g", tk)])
                        self.wait("scalar", self.cV, self._gs_read[tk])
                        for oc in range(4):
                            self.act(stmp[:, 4 * tk + oc, :],
                                     self.pb[4 * tk + oc][:, 0:TT], Act.Silu)
                            self.guard_bank(4 * tk + oc, "scalar")
                        gu_marks[("s", tk)] = self.cS.n
                    for tk in range(2):
                        for oc in range(4):
                            self.use_bank(4 * tk + oc)
                            for ic in range(NG):
                                self.mm(self.pb[4 * tk + oc][:, 0:TT],
                                        wue[:, ic, oc * 128:(oc + 1) * 128],
                                        u1q[:, ic, tk * TT:(tk + 1) * TT],
                                        ic == 0, ic == NG - 1)
                        gu_marks[("u", tk)] = self.cT.n
                        self.wait("vector", self.cT, gu_marks[("u", tk)])
                        self.wait("vector", self.cS, gu_marks[("s", tk)])
                        if tk == 0:
                            self.wait("vector", self.cT, self._d_mms)
                        for oc in range(4):
                            self.tt("vector", utmp[:, 4 * tk + oc, :],
                                    self.pb[4 * tk + oc][:, 0:TT],
                                    wb_sb[:, tk * TT:(tk + 1) * TT], Alu.mult)
                            self.guard_bank(4 * tk + oc, "vector")
                        for oc in range(4):
                            self.tt("vector",
                                    gs16[:, oc, tk * TT:(tk + 1) * TT],
                                    stmp[:, 4 * tk + oc, :],
                                    utmp[:, 4 * tk + oc, :], Alu.mult)
                        gu_marks[("gs", tk)] = self.cV.n
                        self._gs_read[tk] = self.cV.n
                    self._a_done = self.cT.n
                    self.wait("tensor", self.dS, ldD)
                    for tk in range(2):
                        self.wait("tensor", self.cV, gu_marks[("gs", tk)])
                        for oc in range(NG):
                            bank = 2 * tk + oc % 2
                            self.use_bank(bank)
                            for ic in range(4):
                                self.mm(self.pb[bank][:, 0:TT],
                                        wde[:, ic, oc * 128:(oc + 1) * 128],
                                        gs16[:, ic, tk * TT:(tk + 1) * TT],
                                        ic == 0, ic == 3)
                            tDN = self.cT.n
                            self.wait("scalar", self.cT, tDN)
                            dslot = oc % 2
                            self.wait("scalar", self.dG,
                                      dn_marks.get(dslot, 0))
                            self.scopy(dtmp[:, dslot, :], self.pb[bank][:, 0:TT])
                            self.guard_bank(bank, "scalar")
                            sCp = self.cS.n
                            self.wait("gpsimd", self.cS, sCp)
                            dn_marks[dslot] = self.dma(
                                "gpsimd", acc[:, oc, tk * TT:(tk + 1) * TT],
                                dtmp[:, dslot, :], accum_op=Alu.add)
                    self._d_done = self.cT.n
                    self._d_mms = self.cT.n

                self.wait("vector", self.dG)
                u1q_use = self.cT.n
                for tk in range(2):
                    self.wait("scalar", self.dG, ylast)
                    xm = self.dma("scalar", x1_tm[:],
                                  self.s_x1[q * QTR + tk * TT:
                                            q * QTR + (tk + 1) * TT].rearrange(
                                      "(s p) h -> p s h", p=128))
                    self.wait("vector", self.dA, xm)
                    self.wait("tensor", self.dA, xm)
                    for s in range(NSUB):
                        b0, b1 = 4 + 2 * (s % 2), 5 + 2 * (s % 2)
                        self.use_bank(b0)
                        self.use_bank(b1)
                        for g in range(NG):
                            bank = b0 if g < 4 else b1
                            self.tr(self.pb[bank].ap()[:, (g % 4) * 128:
                                                       (g % 4) * 128 + 128],
                                    acc[:, g, tk * TT + s * 128:
                                        tk * TT + (s + 1) * 128],
                                    self.ident32[:])
                        tS = self.cT.n
                        self.wait("vector", self.cT, tS)
                        self.tt("vector", x1_tm[:, s, 0:512], self.pb[b0][:],
                                x1_tm[:, s, 0:512], Alu.add)
                        self.tt("vector", x1_tm[:, s, 512:1024], self.pb[b1][:],
                                x1_tm[:, s, 512:1024], Alu.add)
                        self.guard_bank(b0, "vector")
                        self.guard_bank(b1, "vector")
                    vOut = self.cV.n
                    self.wait("gpsimd", self.cV, vOut)
                    ylast = self.dma(
                        "gpsimd",
                        self.y[q * QTR + tk * TT:q * QTR + (tk + 1) * TT].rearrange(
                            "(s p) h -> p s h", p=128), x1_tm[:])

    def finish(self):
        for name, (tap, src) in self.tap_outs.items():
            self.wait("gpsimd", self.dA)
            self.wait("gpsimd", self.dS)
            self.dma("gpsimd", tap, src)
        self.wait("gpsimd", self.dG)
        self.wait("sync", self.dG)


_BUILT = {}


def _get_nc(taps=()):
    key = tuple(sorted(taps))
    if key not in _BUILT:
        mk = MK(taps)
        nc = mk.build()
        _BUILT[key] = (nc, mk.tap_outs)
    return _BUILT[key]


def make_in_maps(inputs):
    x = np.ascontiguousarray(np.asarray(inputs["L0_attn_x"],
                                        dtype=np.float32).reshape(-1, D))
    g = lambda n: np.ascontiguousarray(np.asarray(inputs[n], np.float32))
    vec = lambda n: g(n).reshape(1, D)
    common = {
        "w_ln_att_g": vec("L0_attn_ln_att_g"),
        "w_ln_att_b": vec("L0_attn_ln_att_b"),
        "w_ln_ffn_g": vec("L0_attn_ln_ffn_g"),
        "w_ln_ffn_b": vec("L0_attn_ln_ffn_b"),
        "w_td": vec("L0_attn_att_time_decay"),
        "w_tf": vec("L0_attn_att_time_first"),
        "w_mk": vec("L0_attn_att_mix_k"),
        "w_mv": vec("L0_attn_att_mix_v"),
        "w_mr": vec("L0_attn_att_mix_r"),
        "w_fmk": vec("L0_attn_ffn_mix_k"),
        "w_fmr": vec("L0_attn_ffn_mix_r"),
        "w_wk": g("L0_attn_att_wk"),
        "w_wv": g("L0_attn_att_wv"),
        "w_wr": g("L0_attn_att_wr"),
        "w_wo": g("L0_attn_att_wo"),
        "w_fwk": g("L0_attn_ffn_wk"),
        "w_fwv": g("L0_attn_ffn_wv"),
        "w_fwr": g("L0_attn_ffn_wr"),
        "w_gw": g("L0_ffn_gate_w"),
        "w_gb": g("L0_ffn_gate_bias").reshape(1, NE),
        "w_sg": g("L0_ffn_shared_gate"),
        "w_su": g("L0_ffn_shared_up"),
        "w_sd": g("L0_ffn_shared_down"),
    }
    eg = g("L0_ffn_expert_gate")
    eu = g("L0_ffn_expert_up")
    ed = g("L0_ffn_expert_down")
    maps = []
    for c in range(NCORES):
        seq, pos = c // 4, c % 4
        a8 = np.full(8, 1e9, np.float32)
        sel8 = np.zeros((8, 1), np.float32)
        for j in range(NCORES):
            if j // 4 == seq and j < c:
                a8[j] = float(TPC) * (c - 1 - j)
        if pos > 0:
            sel8[c - 1, 0] = 1.0
            xp = x[c * TPC - 1:c * TPC]
            pm = 1.0
        else:
            xp = np.zeros((1, D), np.float32)
            pm = 0.0
        m = dict(common)
        m["xs"] = x[c * TPC:(c + 1) * TPC]
        m["xp"] = np.ascontiguousarray(xp)
        m["a8"] = a8.reshape(1, 8)
        m["sel8"] = sel8
        m["pmask"] = np.array([[pm]], np.float32)
        m["w_meg"] = np.ascontiguousarray(eg[c * EPL:(c + 1) * EPL])
        m["w_meu"] = np.ascontiguousarray(eu[c * EPL:(c + 1) * EPL])
        m["w_med"] = np.ascontiguousarray(ed[c * EPL:(c + 1) * EPL])
        maps.append(m)
    return maps


def kernel(**inputs):
    nc, _ = _get_nc()
    maps = make_in_maps(inputs)
    res = run_bass_kernel_spmd(nc, maps, core_ids=list(range(NCORES)))
    out = np.concatenate([res.results[c]["y"] for c in range(NCORES)], axis=0)
    return out.reshape(B, L, D)


# version 09
# speedup vs baseline: 1.1187x; 1.0116x over previous
import numpy as np
from contextlib import ExitStack

import concourse.bass as bass
import concourse.mybir as mybir
from concourse.bass_utils import run_bass_kernel_spmd

F32 = mybir.dt.float32
BF16 = mybir.dt.bfloat16
Alu = mybir.AluOpType
Act = mybir.ActivationFunctionType

NCORES = 8
B, L, D = 2, 16384, 1024
TPC = 4096
NT = 8
TT = 512
NSUB = 4
NG = 8
NE = 32
EPL = 4
DI = 512
EPS = 1e-5
RG = [[0, 1, 2, 3, 4, 5, 6, 7]]
EMAT = DI * D
WSH = EPL * 3 * EMAT
QTR = 1024
NQ = TPC // QTR
ENGS = ("tensor", "vector", "scalar", "sync", "gpsimd")


class Sem:
    def __init__(self, nc, es, name):
        self.h = es.enter_context(nc.semaphore(name))
        self.n = 0


class MK:
    def __init__(self, taps=()):
        self.taps = taps
        self.nc = bass.Bass("TRN2", target_bir_lowering=False, debug=False)
        self.es = ExitStack()
        self.ops = {e: [] for e in ENGS}
        self.tap_outs = {}
        self.bank_guard = {}

    def sched(self, eng, fn, inc=None, k=1):
        if inc is not None:
            inc.n += k

            def thunk(e, fn=fn, inc=inc, k=k):
                fn(e).then_inc(inc.h, k)
        else:
            def thunk(e, fn=fn):
                fn(e)
        self.ops[eng].append(thunk)

    def wait(self, eng, sem, val=None):
        v = sem.n if val is None else val
        self.ops[eng].append(lambda e, h=sem.h, v=v: e.wait_ge(h, v))

    def dma(self, eng, out, in_, **kw):
        sem = {"sync": self.dS, "scalar": self.dA, "gpsimd": self.dG}[eng]
        self.sched(eng, lambda e, o=out, i=in_, kw=kw: e.dma_start(out=o, in_=i, **kw),
                   inc=sem, k=16)
        return sem.n

    def barrier(self, engs=ENGS):
        if "sync" in engs:
            self.wait("sync", self.dS)
        if "scalar" in engs:
            self.wait("scalar", self.dA)
        if "gpsimd" in engs:
            self.wait("gpsimd", self.dG)
        marks = {}
        for e in engs:
            s = self.fence[e]
            self.sched(e, lambda en: en.nop(), inc=s)
            marks[e] = s.n
        for e in engs:
            for o in engs:
                if o != e:
                    self.wait(e, self.fence[o], marks[o])

    def emit(self, blk):
        for e in ENGS:
            getattr(blk, e)(lambda eng, e=e: [f(eng) for f in self.ops[e]])

    def csem(self, eng):
        return {"vector": self.cV, "scalar": self.cS, "gpsimd": self.cG}[eng]

    def ts(self, eng, out, in0, s1, s2, op0, op1=Alu.bypass, accum=None):
        self.sched(eng, lambda e: e.tensor_scalar(
            out=out, in0=in0, scalar1=s1, scalar2=s2, op0=op0, op1=op1,
            accum_out=accum), inc=self.csem(eng))

    def tt(self, eng, out, in0, in1, op):
        self.sched(eng, lambda e: e.tensor_tensor(out=out, in0=in0, in1=in1, op=op),
                   inc=self.csem(eng))

    def stt(self, eng, out, in0, scalar, in1, op0, op1):
        self.sched(eng, lambda e: e.scalar_tensor_tensor(
            out=out, in0=in0, scalar=scalar, in1=in1, op0=op0, op1=op1),
            inc=self.csem(eng))

    def act(self, out, in_, func, bias=0.0, scale=1.0, accum=None):
        self.sched("scalar", lambda e: e.activation(
            out=out, in_=in_, func=func, bias=bias, scale=scale,
            accum_out=accum), inc=self.cS)

    def scopy(self, out, in_):
        self.sched("scalar", lambda e: e.copy(out=out, in_=in_), inc=self.cS)

    def vcopy(self, out, in_):
        self.sched("vector", lambda e: e.tensor_copy(out=out, in_=in_), inc=self.cV)

    def memset(self, ap, val):
        self.sched("vector", lambda e: e.memset(ap, val), inc=self.cV)

    def mm(self, out, lhsT, rhs, start, stop):
        self.sched("tensor", lambda e: e.matmul(
            out=out, lhsT=lhsT, rhs=rhs, start=start, stop=stop),
            inc=self.cT if stop else None)

    def tr(self, out, in_, ident):
        self.sched("tensor", lambda e: e.transpose(
            out=out, in_=in_, identity=ident), inc=self.cT)

    def use_bank(self, bank):
        g = self.bank_guard.get(bank)
        if g is not None:
            sem, mark = g
            self.wait("tensor", sem, mark)

    def guard_bank(self, bank, eng):
        self.bank_guard[bank] = (self.csem(eng), self.csem(eng).n)

    def vsync(self):
        self.wait("vector", self.cV)

    def ln_finish(self, s_sum, s_sq, mean, rstd, tmp, nelem=D):
        inv = 1.0 / nelem
        self.ts("vector", mean, s_sum, inv, None, Alu.mult)
        self.ts("vector", rstd, s_sq, inv, None, Alu.mult)
        self.vsync()
        self.tt("vector", tmp, mean, mean, Alu.mult)
        self.vsync()
        self.tt("vector", rstd, rstd, tmp, Alu.subtract)
        self.vsync()
        self.ts("vector", rstd, rstd, EPS, None, Alu.add)
        vm = self.cV.n
        self.wait("scalar", self.cV, vm)
        self.act(rstd, rstd, Act.Ln)
        self.act(rstd, rstd, Act.Exp, scale=-0.5)

    def build(self):
        nc, es = self.nc, self.es
        dram = lambda n, s, k="ExternalInput", d=F32: nc.dram_tensor(
            n, s, d, kind=k).ap()
        self.sbuf = lambda n, s, d=F32: es.enter_context(nc.sbuf_tensor(n, s, d))
        sbuf = self.sbuf

        self.xs = dram("xs", [TPC, D])
        self.xp = dram("xp", [1, D])
        self.a8 = dram("a8", [1, 8])
        self.sel8 = dram("sel8", [8, 1])
        self.pmask = dram("pmask", [1, 1])
        names = ["ln_att_g", "ln_att_b", "ln_ffn_g", "ln_ffn_b", "td", "tf",
                 "mk", "mv", "mr", "fmk", "fmr"]
        self.wvec = {n: dram("w_" + n, [1, D]) for n in names}
        self.w_wk = dram("w_wk", [D, D])
        self.w_wv = dram("w_wv", [D, D])
        self.w_wr = dram("w_wr", [D, D])
        self.w_wo = dram("w_wo", [D, D])
        self.w_fwk = dram("w_fwk", [D, 4 * D])
        self.w_fwv = dram("w_fwv", [4 * D, D])
        self.w_fwr = dram("w_fwr", [D, D])
        self.w_gw = dram("w_gw", [NE, D])
        self.w_gb = dram("w_gb", [1, NE])
        self.w_meg = dram("w_meg", [EPL, DI, D])
        self.w_meu = dram("w_meu", [EPL, DI, D])
        self.w_med = dram("w_med", [EPL, D, DI])
        self.w_sg = dram("w_sg", [D, D])
        self.w_su = dram("w_su", [D, D])
        self.w_sd = dram("w_sd", [D, D])
        self.y = dram("y", [TPC, D], k="ExternalOutput")

        self.s_num = dram("s_num", [NT, 128, NG * TT], k="Internal")
        self.s_den = dram("s_den", [NT, 128, NG * TT], k="Internal")
        self.s_r = dram("s_r", [NT, 128, NG * TT], k="Internal", d=BF16)
        self.s_u0 = dram("s_u0", [TPC, D], k="Internal")
        self.s_xb = dram("s_xb", [TPC, D], k="Internal")
        self.s_x1 = dram("s_x1", [TPC, D], k="Internal")
        self.s_u1 = dram("s_u1", [NT, 128, NG * TT], k="Internal", d=BF16)
        self.s_kf = dram("s_kf", [NT, 128, 32 * TT], k="Internal", d=BF16)
        self.s_rf = dram("s_rf", [NT, 128, NG * TT], k="Internal", d=BF16)
        self.s_shT = dram("s_shT", [3, 128, NG * D], k="Internal", d=BF16)
        self.cc1_in = dram("cc1_in", [1, 2 * D], k="Internal")
        self.cc1_out = dram("cc1_out", [8, 2 * D], k="Internal")
        self.cc2_in = dram("cc2_in", [1, D], k="Internal")
        self.cc2_out = dram("cc2_out", [8, D], k="Internal")
        self.ccw_in = dram("ccw_in", [1, WSH], k="Internal", d=BF16)
        self.ccw_out = dram("ccw_out", [8, WSH], k="Internal", d=BF16)

        if "h2" in self.taps:
            self.s_h2 = dram("s_h2", [NT, 128, NG * TT], k="Internal", d=BF16)
            self.s_xk2 = dram("s_xk2", [NT, 128, NG * TT], k="Internal", d=BF16)
        for t in self.taps:
            src = {"num": self.s_num, "den": self.s_den, "r": self.s_r,
                   "u0": self.s_u0, "xb": self.s_xb, "x1": self.s_x1,
                   "u1": self.s_u1, "kf": self.s_kf, "rf": self.s_rf,
                   "h2": getattr(self, "s_h2", None),
                   "xk2": getattr(self, "s_xk2", None)}[t]
            self.tap_outs[t] = (nc.dram_tensor(
                "tap_" + t, list(src.shape), src.dtype,
                kind="ExternalOutput").ap(), src)

        self.c_id = nc.inline_tensor(np.eye(128, dtype=np.float32), "c_id").ap()
        self.c_ramp = nc.inline_tensor(
            np.arange(TT, dtype=np.float32)[None, :], "c_ramp").ap()
        self.c_ones = nc.inline_tensor(
            np.ones((1, 128), dtype=np.float32), "c_ones").ap()

        self.ident32 = sbuf("ident32", [128, 128])
        self.ident16 = sbuf("ident16", [128, 128], BF16)
        self.ones32 = sbuf("ones32", [1, 128])
        self.ones16 = sbuf("ones16", [1, 128], BF16)
        self.ramp01 = sbuf("ramp01", [128, TT])
        self.rampsb = sbuf("rampsb", [1, TT])
        NCON = 18
        self.cvr = sbuf("cfm", [128, NCON, NG])
        (self.C_MK, self.C_MV, self.C_MR, self.C_FMK, self.C_FMR, self.C_LAG,
         self.C_LAB, self.C_LFG, self.C_LFB, self.C_TD, self.C_TF, self.C_W,
         self.C_EW, self.C_EU, self.C_PIN, self.C_QIN, self.C_WTB,
         self.C_TMP) = range(NCON)
        self.a8sb = sbuf("a8sb", [1, 8])
        self.a8bc = sbuf("a8bc", [128, 8])
        self.pmask_sb = sbuf("pmask_sb", [1, 1])
        self.sel8_sb = sbuf("sel8_sb", [8, 1])
        self.gwT = sbuf("gwT", [128, NG, NE], BF16)
        self.gbias_row = sbuf("gbias_row", [1, NE])
        self.gbias_bc = sbuf("gbias_bc", [128, NE])
        self.bnd_h1 = sbuf("bnd_h1", [128, NG, 1], BF16)
        self.bnd_h2 = sbuf("bnd_h2", [128, NG, 1], BF16)
        self.row_a = sbuf("row_a", [1, D])
        self.row_b = sbuf("row_b", [1, D])
        self.row_gx = sbuf("row_gx", [1, D])
        self.row_bx = sbuf("row_bx", [1, D])
        self.row16 = sbuf("row16", [1, D], BF16)
        self.rst = sbuf("rst", [1, 8])
        self.pq_j = sbuf("pq_j", [128, 8, 16])
        self.coefs = sbuf("coefs", [128, NG, 8])

        self.pb = [es.enter_context(nc.psum_tensor(f"pb{i}", [128, 512], F32))
                   for i in range(8)]
        self.pb16 = [p.ap().bitcast(BF16) for p in self.pb]

        for s in ("dG", "dS", "dA", "cT", "cV", "cS", "cG"):
            setattr(self, s, Sem(nc, es, s))
        self.fence = {e: Sem(nc, es, "f_" + e) for e in ENGS}

        blk = es.enter_context(nc.Block())

        with ExitStack() as stg_scope:
            pst = lambda n, s, d=BF16: stg_scope.enter_context(
                nc.sbuf_tensor(n, s, d))
            self.stg = [pst("stg0", [128, 4096])]
            self.wtb = pst("wtb", [128, 4096])

            self.p0()
            self.p1a_and_exchange()
            self.barrier(engs=("tensor", "vector", "scalar"))
            self.p1b()
            self.barrier()
            self.p2a()
            self.barrier()
            self.p2b()
            self.barrier()
        self.p3()
        self.finish()

        self.emit(blk)
        es.close()
        return nc

    def load_cvec(self, col, src):
        self.dma("gpsimd", self.cvr[:, col, :],
                 src.rearrange("o (g p) -> (p o) g", p=128),
                 allow_slow_non_contiguous=True)

    def p0(self):
        C, cv = self, self.cvr
        self.dma("sync", self.ident32[:], self.c_id)
        self.dma("sync", self.ones32[:], self.c_ones)
        self.dma("sync", self.rampsb[:], self.c_ramp)
        self.dma("sync", self.gbias_row[:], self.w_gb)
        self.dma("gpsimd", self.ident16[:], self.c_id)
        self.dma("gpsimd", self.ones16[:], self.c_ones)
        self.dma("gpsimd", self.a8sb[:], self.a8)
        self.dma("gpsimd", self.pmask_sb[:], self.pmask)
        self.dma("gpsimd", self.sel8_sb[:], self.sel8)
        self.dma("gpsimd", self.row_gx[:], self.wvec["ln_att_g"])
        self.dma("gpsimd", self.row_bx[:], self.wvec["ln_att_b"])
        for col, src in ((C.C_MK, "mk"), (C.C_MV, "mv"), (C.C_MR, "mr"),
                         (C.C_FMK, "fmk"), (C.C_FMR, "fmr"),
                         (C.C_LAG, "ln_att_g"), (C.C_LAB, "ln_att_b"),
                         (C.C_LFG, "ln_ffn_g"), (C.C_LFB, "ln_ffn_b"),
                         (C.C_TD, "td"), (C.C_TF, "tf")):
            self.load_cvec(col, self.wvec[src])
        self.dma("gpsimd", self.row_a[:], self.xp)
        gLD = self.dG.n
        self.gLD = gLD
        self.dS_consts = self.dS.n

        self.wait("scalar", self.dG, gLD)
        self.act(cv[:, C.C_W, :], cv[:, C.C_TD, :], Act.Exp)
        sW = self.cS.n
        self.wait("vector", self.cS, sW)
        self.ts("vector", cv[:, C.C_W, :], cv[:, C.C_W, :], -1.0, None, Alu.mult)
        vW = self.cV.n
        self.wait("scalar", self.cV, vW)
        self.act(cv[:, C.C_EW, :], cv[:, C.C_W, :], Act.Exp)
        self.act(cv[:, C.C_EU, :], cv[:, C.C_TF, :], Act.Exp)

        self.wait("tensor", self.dS)
        self.wait("tensor", self.dG, gLD)
        self.mm(self.pb[6][:, 0:TT], self.ones32[:], self.rampsb[:], True, True)
        self.mm(self.pb[7][:, 0:8], self.ones32[:], self.a8sb[:], True, True)
        self.mm(self.pb[7][:, 8:8 + NE], self.ones32[:], self.gbias_row[:],
                True, True)
        tBC = self.cT.n
        self.wait("vector", self.cT, tBC)
        self.vcopy(self.ramp01[:], self.pb[6][:, 0:TT])
        self.vcopy(self.a8bc[:], self.pb[7][:, 0:8])
        self.vcopy(self.gbias_bc[:], self.pb[7][:, 8:8 + NE])

        for b in (6, 7):
            self.guard_bank(b, "vector")
        self.pb_free_v = self.cV.n

    def build_gwT(self, gw16):
        self.wait("tensor", self.dG)
        for hc in range(NG):
            self.tr(self.pb16[5][:, hc * 64:hc * 64 + NE],
                    gw16[:, hc * 128:(hc + 1) * 128],
                    self.ident16[0:NE, 0:NE])
        tGW = self.cT.n
        self.wait("vector", self.cT, tGW)
        for hc in range(NG):
            self.vcopy(self.gwT[:, hc, :], self.pb16[5][:, hc * 64:hc * 64 + NE])
        self.guard_bank(5, "vector")

    def p0_staging(self):
        mats = []
        for el in range(EPL):
            mats.append((self.w_meg[el], 4, 8,
                         ("ccw", (el * 3 + 0) * EMAT, EMAT)))
            mats.append((self.w_meu[el], 4, 8,
                         ("ccw", (el * 3 + 1) * EMAT, EMAT)))
            mats.append((self.w_med[el], 8, 4,
                         ("ccw", (el * 3 + 2) * EMAT, EMAT)))
        for mi, src in enumerate((self.w_sg, self.w_su, self.w_sd)):
            for half in range(2):
                mats.append((src[:, half * 512:(half + 1) * 512], 8, 4,
                             ("shT", mi, half)))
        store_mark = 0
        for i, (src, nrb, ncb, dest) in enumerate(mats):
            stg = self.stg[0]
            if i >= 2:
                pass
            ldm = self.dma(
                "gpsimd",
                stg.ap().rearrange("p (rb c) -> p rb c", rb=nrb),
                src.rearrange("(rb p) c -> p rb c", p=128))
            self.wait("sync", self.dG, ldm)
            if i >= 1:
                self.wait("sync", self.dG, store_mark)
            for rb in range(nrb):
                for cb in range(ncb):
                    self.sched("sync", lambda e, rb=rb, cb=cb, nrb=nrb, ncb=ncb,
                               stg=stg: e.dma_start_transpose(
                                   out=self.wtb[:, (cb * nrb + rb) * 128:
                                                (cb * nrb + rb) * 128 + 128],
                                   in_=stg[:, (rb * ncb + cb) * 128:
                                           (rb * ncb + cb) * 128 + 128]),
                               inc=self.dS, k=16)
            trm = self.dS.n
            self.wait("gpsimd", self.dS, trm)
            if dest[0] == "ccw":
                _, off, n = dest
                store_mark = self.dma(
                    "gpsimd",
                    self.ccw_in[0, off:off + n].rearrange("(p f) -> p f", p=128),
                    self.wtb[:])
                if i == EPL * 3 - 1:
                    self.wait("gpsimd", self.dG, store_mark)
                    self.sched("gpsimd", lambda e: e.collective_compute(
                        "AllGather", Alu.bypass, replica_groups=RG,
                        ins=[self.ccw_in], outs=[self.ccw_out]), inc=self.cG)
                    self.agw_done = self.cG.n
            else:
                _, mi, half = dest
                store_mark = self.dma(
                    "gpsimd", self.s_shT[mi, :, half * 4096:(half + 1) * 4096],
                    self.wtb[:])


    def row_ln(self, dst_bnd, g_row, b_row):
        self.act(self.row_b[:], self.row_a[:], Act.Identity,
                 accum=self.rst[:, 0:1])
        self.act(self.row_b[:], self.row_a[:], Act.Square,
                 accum=self.rst[:, 1:2])
        sM = self.cS.n
        self.wait("vector", self.cS, sM)
        self.ln_finish(self.rst[:, 0:1], self.rst[:, 1:2], self.rst[:, 2:3],
                       self.rst[:, 3:4], self.rst[:, 4:5])
        sM2 = self.cS.n
        self.wait("vector", self.cS, sM2)
        self.ts("vector", self.row_b[:], self.row_a[:], self.rst[:, 2:3],
                self.rst[:, 3:4], Alu.subtract, Alu.mult)
        self.tt("vector", self.row_b[:], self.row_b[:], g_row, Alu.mult)
        self.tt("vector", self.row_b[:], self.row_b[:], b_row, Alu.add)
        self.ts("vector", self.row16[:], self.row_b[:],
                self.pmask_sb[0:1, 0:1], None, Alu.mult)
        vD = self.cV.n
        self.use_bank(6)
        self.wait("tensor", self.cV, vD)
        for g in range(NG):
            self.tr(self.pb16[6][:, g * 64:g * 64 + 1],
                    self.row16[0:1, g * 128:(g + 1) * 128],
                    self.ident16[0:1, 0:1])
        tD = self.cT.n
        self.wait("vector", self.cT, tD)
        for g in range(NG):
            self.vcopy(dst_bnd[:, g, 0:1], self.pb16[6][:, g * 64:g * 64 + 1])
        self.guard_bank(6, "vector")

    def p1a_and_exchange(self):
        C, cv = self, self.cvr
        nc = self.nc
        with ExitStack() as ph:
            psb = lambda n, s, d=F32: ph.enter_context(nc.sbuf_tensor(n, s, d))
            wk_sb = psb("wk_sb", [128, NG, D], BF16)
            wv_sb = psb("wv_sb", [128, NG, D], BF16)
            wr_sb = psb("wr_sb", [128, NG, D], BF16)
            x_tm = psb("x_tm", [128, NSUB, D])
            u0_tm = psb("u0_tm", [128, NSUB, D])
            st = psb("st", [128, NSUB, 10])
            h1_fm = psb("h1_fm", [128, NG, TT + 1], BF16)
            xmix = psb("xmix", [128, NG, TT], BF16)
            ek = psb("ek", [128, NG, TT])
            ekv = psb("ekv", [128, NG, TT])
            q_arr = psb("q_arr", [128, NG, TT + 1])
            p_arr = psb("p_arr", [128, NG, TT + 1])
            h1_tm = x_tm.ap().bitcast(BF16)[:, :, 0:D]
            r4 = x_tm.ap().bitcast(BF16)[:, :, D:2 * D].rearrange(
                "p a (g2 t) -> p a g2 t", g2=2)
            r16 = lambda oc: r4[:, oc // 2, oc % 2, :]
            diff = u0_tm.ap().bitcast(BF16).rearrange(
                "p a b -> p (a b)")[:, 0:NG * TT].rearrange(
                "p (g t) -> p g t", g=NG)
            flat = lambda a: a.ap().rearrange("p a b -> p (a b)")

            gw16 = psb("gw16", [NE, D], BF16)
            for dst, src in ((wk_sb, self.w_wk), (wv_sb, self.w_wv),
                             (wr_sb, self.w_wr)):
                self.dma("gpsimd", dst.ap(),
                         src.rearrange("(ic p) o -> p ic o", p=128))
            self.dma("gpsimd", gw16[:], self.w_gw)
            wLD = self.dG.n
            self.build_gwT(gw16)
            self.p0_staging()

            self.wait("scalar", self.dG, wLD)
            self.wait("vector", self.dG, wLD)
            self.wait("vector", self.dS, self.dS_consts)
            self.row_ln(self.bnd_h1, self.row_gx[:], self.row_bx[:])
            self.memset(q_arr[:, :, TT:TT + 1], 0.0)
            self.memset(p_arr[:, :, TT:TT + 1], 0.0)

            u0_spill = [0] * NT
            nd_spill = [0] * NT
            h1T_mark = [0] * NT
            mmr_mark = [0] * NT
            h1evac_mark = [0] * NT

            for t in range(NT):
                rows = self.xs[t * TT:(t + 1) * TT].rearrange(
                    "(s p) h -> p s h", p=128)
                if t > 0:
                    self.wait("scalar", self.cT, h1T_mark[t - 1])
                    self.wait("scalar", self.dA, nd_spill[t - 1])
                xm = self.dma("scalar", x_tm[:], rows)
                self.wait("scalar", self.dA, xm)
                if t > 0:
                    self.wait("scalar", self.dA, u0_spill[t - 1])
                for s in range(NSUB):
                    self.act(u0_tm[:, s, :], x_tm[:, s, :], Act.Identity,
                             accum=st[:, s, 0:1])
                    self.act(u0_tm[:, s, :], x_tm[:, s, :], Act.Square,
                             accum=st[:, s, 1:2])
                sStat = self.cS.n
                self.wait("vector", self.cS, sStat)
                self.ln_finish(st[:, :, 0], st[:, :, 1], st[:, :, 2],
                               st[:, :, 3], st[:, :, 4])
                sFin = self.cS.n
                self.wait("vector", self.cS, sFin)
                for s in range(NSUB):
                    self.ts("vector", u0_tm[:, s, :], x_tm[:, s, :],
                            st[:, s, 2:3], st[:, s, 3:4], Alu.subtract, Alu.mult)
                vU0 = self.cV.n
                self.wait("scalar", self.cV, vU0)
                u0_spill[t] = self.dma(
                    "scalar", self.s_u0[t * TT:(t + 1) * TT].rearrange(
                        "(s p) h -> p s h", p=128), u0_tm[:])
                for s in range(NSUB):
                    self.act(x_tm[:, s, :], u0_tm[:, s, :], Act.Identity,
                             accum=st[:, s, 5:6])
                    self.act(x_tm[:, s, :], u0_tm[:, s, :], Act.Square,
                             accum=st[:, s, 6:7])
                sStat2 = self.cS.n
                self.wait("vector", self.cS, sStat2)
                self.ln_finish(st[:, :, 5], st[:, :, 6], st[:, :, 7],
                               st[:, :, 8], st[:, :, 9])
                sFin2 = self.cS.n
                self.wait("vector", self.cS, sFin2)
                for s in range(NSUB):
                    self.ts("vector", h1_tm[:, s, :], u0_tm[:, s, :],
                            st[:, s, 7:8], st[:, s, 8:9], Alu.subtract, Alu.mult)
                vH1 = self.cV.n

                if t == 0:
                    self.vcopy(h1_fm[:, :, 0:1], self.bnd_h1[:])
                else:
                    self.wait("vector", self.cS, h1evac_mark[t - 1])
                    self.vcopy(h1_fm[:, :, 0:1], h1_fm[:, :, TT:TT + 1])
                self.wait("tensor", self.cV, self.cV.n)
                for s in range(NSUB):
                    self.use_bank(4 + s)
                for s in range(NSUB):
                    for g in range(NG):
                        self.tr(self.pb16[4 + s][:, g * 128:(g + 1) * 128],
                                h1_tm[:, s, g * 128:(g + 1) * 128],
                                self.ident16[:])
                h1T_mark[t] = self.cT.n
                self.wait("scalar", self.cT, h1T_mark[t])
                for s in range(NSUB):
                    for g in range(NG):
                        self.act(h1_fm[:, g, 1 + s * 128:1 + (s + 1) * 128],
                                 self.pb16[4 + s][:, g * 128:(g + 1) * 128],
                                 Act.Identity, scale=cv[:, C.C_LAG, g:g + 1],
                                 bias=cv[:, C.C_LAB, g:g + 1])
                h1evac_mark[t] = self.cS.n
                for s in range(NSUB):
                    self.guard_bank(4 + s, "scalar")

                if t > 0:
                    self.wait("vector", self.dA, nd_spill[t - 1])
                self.wait("vector", self.dA, u0_spill[t])
                self.wait("vector", self.cS, h1evac_mark[t])
                self.tt("vector", diff, h1_fm[:, :, 1:TT + 1],
                        h1_fm[:, :, 0:TT], Alu.subtract)
                if t == 0:
                    self.wait("tensor", self.dG, wLD)
                ek_marks = []
                if t > 0:
                    self.wait("scalar", self.dA, nd_spill[t - 1])
                for mat, (wsb, col) in enumerate(((wk_sb, C.C_MK),
                                                  (wv_sb, C.C_MV),
                                                  (wr_sb, C.C_MR))):
                    if not (t == 0 and mat == 0):
                        self.wait("vector", self.cT,
                                  self._xmix_use if mat == 0 else self.cT.n)
                    for g in range(NG):
                        self.stt("vector", xmix[:, g, :], diff[:, g, :],
                                 cv[:, col, g:g + 1], h1_fm[:, g, 0:TT],
                                 Alu.mult, Alu.add)
                    vMix = self.cV.n
                    self.wait("tensor", self.cV, vMix)
                    for oc in range(NG):
                        bank = (mat * NG + oc) % 3
                        self.use_bank(bank)
                        for ic in range(NG):
                            self.mm(self.pb[bank][:, 0:TT],
                                    wsb[:, ic, oc * 128:(oc + 1) * 128],
                                    xmix[:, ic, :], ic == 0, ic == NG - 1)
                        tMM = self.cT.n
                        self._xmix_use = tMM
                        if mat == 0:
                            self.wait("scalar", self.cT, tMM)
                            self.act(ek[:, oc, :], self.pb[bank][:, 0:TT],
                                     Act.Exp)
                            self.guard_bank(bank, "scalar")
                            ek_marks.append(self.cS.n)
                        elif mat == 1:
                            self.wait("vector", self.cT, tMM)
                            self.wait("vector", self.cS, ek_marks[oc])
                            self.tt("vector", ekv[:, oc, :],
                                    self.pb[bank][:, 0:TT], ek[:, oc, :],
                                    Alu.mult)
                            self.guard_bank(bank, "vector")
                        else:
                            self.wait("scalar", self.cT, tMM)
                            self.act(r16(oc), self.pb[bank][:, 0:TT],
                                     Act.Tanh, scale=0.5)
                            self.guard_bank(bank, "scalar")
                mmr_mark[t] = self.cT.n
                sR = self.cS.n
                self.wait("vector", self.cS, sR)
                self.ts("vector", r4, r4, 0.5, 0.5, Alu.mult, Alu.add)
                for g in range(NG):
                    self.vcopy(q_arr[:, g, 0:1], q_arr[:, g, TT:TT + 1])
                    self.vcopy(p_arr[:, g, 0:1], p_arr[:, g, TT:TT + 1])
                for g in range(NG):
                    self.sched("vector", lambda e, g=g: e.tensor_tensor_scan(
                        out=q_arr[:, g, 1:TT + 1],
                        data0=cv[:, C.C_EW, g:g + 1].to_broadcast([128, TT]),
                        data1=ek[:, g, :], initial=q_arr[:, g, 0:1],
                        op0=Alu.mult, op1=Alu.add), inc=self.cV)
                    self.sched("vector", lambda e, g=g: e.tensor_tensor_scan(
                        out=p_arr[:, g, 1:TT + 1],
                        data0=cv[:, C.C_EW, g:g + 1].to_broadcast([128, TT]),
                        data1=ekv[:, g, :], initial=p_arr[:, g, 0:1],
                        op0=Alu.mult, op1=Alu.add), inc=self.cV)
                for g in range(NG):
                    self.stt("vector", ekv[:, g, :], ekv[:, g, :],
                             cv[:, C.C_EU, g:g + 1], p_arr[:, g, 0:TT],
                             Alu.mult, Alu.add)
                    self.stt("vector", ek[:, g, :], ek[:, g, :],
                             cv[:, C.C_EU, g:g + 1], q_arr[:, g, 0:TT],
                             Alu.mult, Alu.add)
                vND = self.cV.n
                self.wait("scalar", self.cV, vND)
                self.dma("scalar", self.s_num[t], flat(ekv))
                self.dma("scalar", self.s_den[t], flat(ek))
                nd_spill[t] = self.dma("scalar", self.s_r[t], r4)

            self.vcopy(self.pq_j[:, 0, 0:NG], p_arr[:, :, TT])
            self.vcopy(self.pq_j[:, 0, NG:16], q_arr[:, :, TT])
            vPack = self.cV.n
            self.wait("gpsimd", self.cV, vPack)
            self.dma("gpsimd", self.cc1_in[0].rearrange("(p f) -> p f", p=128),
                     self.pq_j[:, 0, :])
            self.wait("gpsimd", self.dG)
            self.sched("gpsimd", lambda e: e.collective_compute(
                "AllGather", Alu.bypass, replica_groups=RG,
                ins=[self.cc1_in], outs=[self.cc1_out]), inc=self.cG)
            self.ag1_done = self.cG.n

    def exchange1_consume(self):
        C, cv = self, self.cvr
        self.wait("gpsimd", self.cG, self.ag1_done)
        for j in range(8):
            self.dma("gpsimd", self.pq_j[:, j, :],
                     self.cc1_out[j].rearrange("(p f) -> p f", p=128))
        ccLD = self.dG.n
        self.wait("scalar", self.dG, ccLD)
        for j in range(8):
            self.act(self.coefs[:, :, j], cv[:, C.C_W, :], Act.Exp,
                     scale=self.a8bc[:, j:j + 1])
        sCoef = self.cS.n
        self.wait("vector", self.cS, sCoef)
        self.wait("vector", self.dG, ccLD)
        self.memset(cv[:, C.C_PIN, :], 0.0)
        self.memset(cv[:, C.C_QIN, :], 0.0)
        self.vsync()
        for j in range(8):
            self.tt("vector", cv[:, C.C_TMP, :], self.coefs[:, :, j],
                    self.pq_j[:, j, 0:NG], Alu.mult)
            self.vsync()
            self.tt("vector", cv[:, C.C_PIN, :], cv[:, C.C_PIN, :],
                    cv[:, C.C_TMP, :], Alu.add)
            self.vsync()
            self.tt("vector", cv[:, C.C_TMP, :], self.coefs[:, :, j],
                    self.pq_j[:, j, NG:16], Alu.mult)
            self.vsync()
            self.tt("vector", cv[:, C.C_QIN, :], cv[:, C.C_QIN, :],
                    cv[:, C.C_TMP, :], Alu.add)
            self.vsync()

    def p1b(self):
        C, cv = self, self.cvr
        nc = self.nc
        with ExitStack() as ph:
            psb = lambda n, s, d=F32: ph.enter_context(nc.sbuf_tensor(n, s, d))
            wo_sb = psb("wo_sb", [128, NG, D], BF16)
            nbufs = [psb("nbuf0", [128, NG, TT])] * 2
            dbufs = [psb("dbuf0", [128, NG, TT])] * 2
            rbufs = [psb("rbuf0", [128, NG, TT], BF16)] * 2
            u0_tms = [psb("u0b_tm0", [128, NSUB, D])] * 2
            Dfm = psb("Dfm", [128, NG, TT])
            a_fm = psb("a_fm", [128, NG, TT])
            wkv16 = Dfm.ap().bitcast(BF16)[:, :, 0:TT]
            xb_tm = psb("xb_tm", [128, NSUB, D])
            flat = lambda a: a.ap().rearrange("p a b -> p (a b)")

            self.wait("gpsimd", self.cT)
            self.wait("gpsimd", self.cS)
            self.dma("gpsimd", wo_sb.ap(),
                     self.w_wo.rearrange("(ic p) o -> p ic o", p=128))
            wLD = self.dG.n

            consume_mark = [0] * NT
            aT_mark = [0] * NT
            wo_mm_mark = [0] * NT
            order = [2, 3, 4, 5, 6, 7, 0, 1]
            prev = -1
            for t in order:
                if t == 0:
                    self.exchange1_consume()
                nbuf, dbuf = nbufs[t % 2], dbufs[t % 2]
                rbuf, u0_tm = rbufs[t % 2], u0_tms[t % 2]
                if prev >= 0:
                    self.wait("scalar", self.cT, wo_mm_mark[prev])
                    self.wait("scalar", self.cV, consume_mark[prev])
                m1 = self.dma("scalar", flat(nbuf), self.s_num[t])
                m2 = self.dma("scalar", flat(dbuf), self.s_den[t])
                if prev >= 0:
                    self.wait("scalar", self.cV, consume_mark[prev])
                m3 = self.dma("scalar", flat(rbuf), self.s_r[t])
                if prev >= 0:
                    self.wait("scalar", self.cV, aT_mark[prev])
                m4 = self.dma("scalar", u0_tm[:],
                              self.s_u0[t * TT:(t + 1) * TT].rearrange(
                                  "(s p) h -> p s h", p=128))
                self.wait("vector", self.dA, m2)
                if t < 2:
                    self.ts("vector", cv[:, C.C_WTB, :], cv[:, C.C_W, :],
                            float(t * TT), None, Alu.mult)
                    vWT = self.cV.n
                    self.wait("scalar", self.cV, vWT)
                    for g in range(NG):
                        self.act(Dfm[:, g, :], self.ramp01[:], Act.Exp,
                                 scale=cv[:, C.C_W, g:g + 1],
                                 bias=cv[:, C.C_WTB, g:g + 1])
                    sD = self.cS.n
                    self.wait("vector", self.cS, sD)
                    for g in range(NG):
                        self.stt("vector", nbuf[:, g, :], Dfm[:, g, :],
                                 cv[:, C.C_PIN, g:g + 1], nbuf[:, g, :],
                                 Alu.mult, Alu.add)
                        self.stt("vector", dbuf[:, g, :], Dfm[:, g, :],
                                 cv[:, C.C_QIN, g:g + 1], dbuf[:, g, :],
                                 Alu.mult, Alu.add)
                self.sched("vector", lambda e: e.reciprocal(
                    out=flat(dbuf), in_=flat(dbuf)), inc=self.cV)
                self.tt("vector", wkv16, nbuf.ap(), dbuf.ap(), Alu.mult)
                self.wait("vector", self.dA, m3)
                self.tt("vector", wkv16, wkv16, rbuf.ap(), Alu.mult)
                consume_mark[t] = self.cV.n

                self.wait("tensor", self.cV, consume_mark[t])
                if t == 0:
                    self.wait("tensor", self.dG, wLD)
                for oc in range(NG):
                    bank = oc % 3
                    self.use_bank(bank)
                    for ic in range(NG):
                        self.mm(self.pb[bank][:, 0:TT],
                                wo_sb[:, ic, oc * 128:(oc + 1) * 128],
                                wkv16[:, ic, :], ic == 0, ic == NG - 1)
                    tMM = self.cT.n
                    self.wait("vector", self.cT, tMM)
                    self.vcopy(a_fm[:, oc, :], self.pb[bank][:, 0:TT])
                    self.guard_bank(bank, "vector")
                vA = self.cV.n
                wo_mm_mark[t] = self.cT.n

                self.wait("tensor", self.cV, vA)
                self.wait("scalar", self.dA, m4)
                for s in range(NSUB):
                    b0, b1 = 4 + 2 * (s % 2), 5 + 2 * (s % 2)
                    self.use_bank(b0)
                    self.use_bank(b1)
                    for g in range(NG):
                        bank = b0 if g < 4 else b1
                        self.tr(self.pb[bank].ap()[:, (g % 4) * 128:
                                                   (g % 4) * 128 + 128],
                                a_fm[:, g, s * 128:(s + 1) * 128],
                                self.ident32[:])
                    tS = self.cT.n
                    self.wait("vector", self.cT, tS)
                    self.tt("vector", xb_tm[:, s, 0:512], self.pb[b0][:],
                            u0_tm[:, s, 0:512], Alu.add)
                    self.tt("vector", xb_tm[:, s, 512:1024], self.pb[b1][:],
                            u0_tm[:, s, 512:1024], Alu.add)
                    self.guard_bank(b0, "vector")
                    self.guard_bank(b1, "vector")
                aT_mark[t] = self.cV.n
                self.wait("scalar", self.cV, aT_mark[t])
                self.dma("scalar", self.s_xb[t * TT:(t + 1) * TT].rearrange(
                    "(s p) h -> p s h", p=128), xb_tm[:])
                if t == NT - 1:
                    self.wait("gpsimd", self.cV, aT_mark[t])
                    self.dma("gpsimd", self.cc2_in, xb_tm[127:128, 3, :])
                    self.wait("gpsimd", self.dG)
                    self.sched("gpsimd", lambda e: e.collective_compute(
                        "AllGather", Alu.bypass, replica_groups=RG,
                        ins=[self.cc2_in], outs=[self.cc2_out]), inc=self.cG)
                    self.ag2_done = self.cG.n
                prev = t

    def p2a(self):
        C, cv = self, self.cvr
        nc = self.nc
        with ExitStack() as ph:
            psb = lambda n, s, d=F32: ph.enter_context(nc.sbuf_tensor(n, s, d))
            fwk_sb = psb("fwk_sb", [128, NG, 4 * D], BF16)
            fwr_sb = psb("fwr_sb", [128, NG, D], BF16)
            xb_tm = psb("xb2_tm", [128, NSUB, D])
            st = psb("st2", [128, NSUB, 6])
            h2_fm = psb("h2_fm", [128, NG, TT + 1], BF16)
            kf16 = psb("kf16", [128, 32, TT], BF16)
            xk2 = psb("xk2", [128, NG, TT], BF16)
            xr2 = psb("xr2", [128, NG, TT], BF16)
            ktmp = psb("ktmp", [128, 2, TT])
            cc2_sb = psb("cc2_sb", [8, D])
            h2_tm = kf16.ap()[:, 8:16, :].rearrange(
                "p (a two) t -> p a (two t)", two=2)
            rf4 = xb_tm.ap().bitcast(BF16)[:, :, D:2 * D].rearrange(
                "p a (g2 t) -> p a g2 t", g2=2)
            rf16 = lambda oc: rf4[:, oc // 2, oc % 2, :]
            diff = kf16.ap()[:, 0:NG, 0:TT]
            flat = lambda a: a.ap().rearrange("p a b -> p (a b)")

            self.dma("gpsimd", fwk_sb.ap(),
                     self.w_fwk.rearrange("(ic p) o -> p ic o", p=128))
            self.dma("gpsimd", fwr_sb.ap(),
                     self.w_fwr.rearrange("(ic p) o -> p ic o", p=128))
            wLD = self.dG.n
            self.dma("gpsimd", self.row_gx[:], self.wvec["ln_ffn_g"])
            self.dma("gpsimd", self.row_bx[:], self.wvec["ln_ffn_b"])
            rowLD = self.dG.n

            self.wait("gpsimd", self.cG, self.ag2_done)
            ccm = self.dma("gpsimd", cc2_sb[:], self.cc2_out)
            self.wait("tensor", self.dG, ccm)
            self.use_bank(6)
            self.use_bank(7)
            self.mm(self.pb[6][0:1, 0:512], self.sel8_sb[:],
                    cc2_sb[:, 0:512], True, True)
            self.mm(self.pb[7][0:1, 0:512], self.sel8_sb[:],
                    cc2_sb[:, 512:1024], True, True)
            tSel = self.cT.n
            self.wait("vector", self.cT, tSel)
            self.vcopy(self.row_a[0:1, 0:512], self.pb[6][0:1, 0:512])
            self.vcopy(self.row_a[0:1, 512:1024], self.pb[7][0:1, 0:512])
            self.guard_bank(6, "vector")
            self.guard_bank(7, "vector")
            self.wait("vector", self.dG, rowLD)
            self.row_ln(self.bnd_h2, self.row_gx[:], self.row_bx[:])

            kf_spill = [0] * NT
            h2T_mark = [0] * NT
            h2evac_marks = [0] * NT
            mm_mark = [0] * NT
            for t in range(NT):
                if t > 0:
                    self.wait("scalar", self.cT, h2T_mark[t - 1])
                    self.wait("scalar", self.dA, kf_spill[t - 1])
                xm = self.dma("scalar", xb_tm[:],
                              self.s_xb[t * TT:(t + 1) * TT].rearrange(
                                  "(s p) h -> p s h", p=128))
                self.wait("scalar", self.dA, xm)
                kdump = ktmp.ap().rearrange("p a b -> p (a b)")
                for s in range(NSUB):
                    self.act(kdump, xb_tm[:, s, :],
                             Act.Identity, accum=st[:, s, 0:1])
                    self.act(kdump, xb_tm[:, s, :],
                             Act.Square, accum=st[:, s, 1:2])
                sStat = self.cS.n
                self.wait("vector", self.cS, sStat)
                self.ln_finish(st[:, :, 0], st[:, :, 1], st[:, :, 2],
                               st[:, :, 3], st[:, :, 4])
                sFin = self.cS.n
                self.wait("vector", self.cS, sFin)
                if t > 0:
                    self.wait("vector", self.dA, kf_spill[t - 1])
                for s in range(NSUB):
                    self.ts("vector", h2_tm[:, s, :], xb_tm[:, s, :],
                            st[:, s, 2:3], st[:, s, 3:4], Alu.subtract, Alu.mult)
                vH2 = self.cV.n
                if t == 0:
                    self.vcopy(h2_fm[:, :, 0:1], self.bnd_h2[:])
                else:
                    self.wait("vector", self.cS, h2evac_marks[t - 1])
                    self.vcopy(h2_fm[:, :, 0:1], h2_fm[:, :, TT:TT + 1])
                self.wait("tensor", self.cV, self.cV.n)
                for s in range(NSUB):
                    self.use_bank(4 + s)
                for s in range(NSUB):
                    for g in range(NG):
                        self.tr(self.pb16[4 + s][:, g * 128:(g + 1) * 128],
                                h2_tm[:, s, g * 128:(g + 1) * 128],
                                self.ident16[:])
                h2T_mark[t] = self.cT.n
                self.wait("scalar", self.cT, h2T_mark[t])
                for s in range(NSUB):
                    for g in range(NG):
                        self.act(h2_fm[:, g, 1 + s * 128:1 + (s + 1) * 128],
                                 self.pb16[4 + s][:, g * 128:(g + 1) * 128],
                                 Act.Identity, scale=cv[:, C.C_LFG, g:g + 1],
                                 bias=cv[:, C.C_LFB, g:g + 1])
                h2evac_marks[t] = self.cS.n
                for s in range(NSUB):
                    self.guard_bank(4 + s, "scalar")

                if t > 0:
                    self.wait("vector", self.dA, kf_spill[t - 1])
                self.wait("vector", self.cS, h2evac_marks[t])
                self.tt("vector", diff, h2_fm[:, :, 1:TT + 1],
                        h2_fm[:, :, 0:TT], Alu.subtract)
                if t > 0:
                    self.wait("vector", self.cT, mm_mark[t - 1])
                for dst, col in ((xk2, C.C_FMK), (xr2, C.C_FMR)):
                    for g in range(NG):
                        self.stt("vector", dst[:, g, :], diff[:, g, :],
                                 cv[:, col, g:g + 1], h2_fm[:, g, 0:TT],
                                 Alu.mult, Alu.add)
                vMix = self.cV.n
                if "h2" in self.taps:
                    self.wait("scalar", self.cV, vMix)
                    self.dma("scalar",
                             self.s_h2[t].rearrange("p (g c) -> p g c", g=NG),
                             h2_fm[:, :, 1:TT + 1])
                    self.dma("scalar", self.s_xk2[t], flat(xk2))

                self.wait("tensor", self.cV, vMix)
                if t == 0:
                    self.wait("tensor", self.dG, wLD)
                kf_sq_mark = [0, 0]
                for oc in range(32):
                    bank = oc % 3
                    self.use_bank(bank)
                    for ic in range(NG):
                        self.mm(self.pb[bank][:, 0:TT],
                                fwk_sb[:, ic, oc * 128:(oc + 1) * 128],
                                xk2[:, ic, :], ic == 0, ic == NG - 1)
                    tMM = self.cT.n
                    self.wait("scalar", self.cT, tMM)
                    slot = oc % 2
                    self.act(ktmp[:, slot, :], self.pb[bank][:, 0:TT], Act.Relu)
                    self.guard_bank(bank, "scalar")
                    self.act(kf16[:, oc, :], ktmp[:, slot, :], Act.Square)
                    kf_sq_mark[slot] = self.cS.n
                for oc in range(NG):
                    bank = oc % 3
                    self.use_bank(bank)
                    for ic in range(NG):
                        self.mm(self.pb[bank][:, 0:TT],
                                fwr_sb[:, ic, oc * 128:(oc + 1) * 128],
                                xr2[:, ic, :], ic == 0, ic == NG - 1)
                    tMM = self.cT.n
                    self.wait("scalar", self.cT, tMM)
                    self.act(rf16(oc), self.pb[bank][:, 0:TT],
                             Act.Tanh, scale=0.5)
                    self.guard_bank(bank, "scalar")
                mm_mark[t] = self.cT.n
                sRF = self.cS.n
                self.wait("vector", self.cS, sRF)
                self.ts("vector", rf4, rf4, 0.5, 0.5, Alu.mult, Alu.add)
                vRF = self.cV.n
                self.wait("scalar", self.cV, vRF)
                self.dma("scalar", self.s_kf[t], flat(kf16))
                kf_spill[t] = self.dma("scalar", self.s_rf[t], rf4)

    def p2b(self):
        C, cv = self, self.cvr
        nc = self.nc
        with ExitStack() as ph:
            psb = lambda n, s, d=F32: ph.enter_context(nc.sbuf_tensor(n, s, d))
            fwv_sb = psb("fwv_sb", [128, 32, D], BF16)
            kf16 = psb("kf16b", [128, 32, TT], BF16)
            xb_tm = psb("xb3_tm", [128, NSUB, D])
            x0_tm = psb("x0_tm", [128, NSUB, D])
            f_fm = psb("f_fm", [128, NG, TT])
            x1_tm = psb("x1_tm", [128, NSUB, D])
            st = psb("st3", [128, NSUB, 6])
            u1_tm = x0_tm.ap().bitcast(BF16)[:, :, 0:D]
            rf4 = x1_tm.ap().bitcast(BF16)[:, :, 0:D].rearrange(
                "p a (g2 t) -> p a g2 t", g2=2)
            rf16 = lambda oc: rf4[:, oc // 2, oc % 2, :]
            u1buf = f_fm.ap().bitcast(BF16)[:, 4:8, 0:2 * TT].rearrange(
                "p a (g2 t) -> p (a g2) t", g2=2)
            flat = lambda a: a.ap().rearrange("p a b -> p (a b)")

            self.dma("gpsimd", fwv_sb.ap(),
                     self.w_fwv.rearrange("(ic p) o -> p ic o", p=128))
            wLD = self.dG.n

            ld_mark = [0] * NT
            u1T_mark = [0] * NT
            u1_spill = [0] * NT
            mm_mark = [0] * NT
            asm_mark = [0] * NT
            for t in range(NT):
                if t > 0:
                    self.wait("scalar", self.cT, mm_mark[t - 1])
                m1 = self.dma("scalar", flat(kf16), self.s_kf[t])
                if t > 0:
                    self.wait("scalar", self.cV, ld_mark[t - 1])
                m2 = self.dma("scalar", rf4, self.s_rf[t])
                if t > 0:
                    self.wait("scalar", self.cV, asm_mark[t - 1])
                m3 = self.dma("scalar", xb_tm[:],
                              self.s_xb[t * TT:(t + 1) * TT].rearrange(
                                  "(s p) h -> p s h", p=128))
                if t > 0:
                    self.wait("scalar", self.cT, u1T_mark[t - 1])
                m4 = self.dma("scalar", x0_tm[:],
                              self.xs[t * TT:(t + 1) * TT].rearrange(
                                  "(s p) h -> p s h", p=128))
                self.wait("tensor", self.dA, m1)
                if t == 0:
                    self.wait("tensor", self.dG, wLD)
                if t > 0:
                    self.wait("vector", self.dA, u1_spill[t - 1])
                for oc in range(NG):
                    bank = oc % 3
                    self.use_bank(bank)
                    for ic in range(32):
                        self.mm(self.pb[bank][:, 0:TT],
                                fwv_sb[:, ic, oc * 128:(oc + 1) * 128],
                                kf16[:, ic, :], ic == 0, ic == 31)
                    tMM = self.cT.n
                    self.wait("vector", self.cT, tMM)
                    if oc == 0:
                        self.wait("vector", self.dA, m2)
                    self.tt("vector", f_fm[:, oc, :], self.pb[bank][:, 0:TT],
                            rf16(oc), Alu.mult)
                    self.guard_bank(bank, "vector")
                vF = self.cV.n
                mm_mark[t] = self.cT.n

                self.wait("tensor", self.cV, vF)
                self.wait("vector", self.dA, m3)
                for s in range(NSUB):
                    b0, b1 = 4 + 2 * (s % 2), 5 + 2 * (s % 2)
                    self.use_bank(b0)
                    self.use_bank(b1)
                    for g in range(NG):
                        bank = b0 if g < 4 else b1
                        self.tr(self.pb[bank].ap()[:, (g % 4) * 128:
                                                   (g % 4) * 128 + 128],
                                f_fm[:, g, s * 128:(s + 1) * 128],
                                self.ident32[:])
                    tS = self.cT.n
                    self.wait("vector", self.cT, tS)
                    self.tt("vector", x1_tm[:, s, 0:512], self.pb[b0][:],
                            xb_tm[:, s, 0:512], Alu.add)
                    self.tt("vector", x1_tm[:, s, 512:1024], self.pb[b1][:],
                            xb_tm[:, s, 512:1024], Alu.add)
                    self.guard_bank(b0, "vector")
                    self.guard_bank(b1, "vector")
                self.wait("vector", self.dA, m4)
                self.tt("vector", flat(x1_tm), flat(x1_tm), flat(x0_tm), Alu.add)
                vX1 = self.cV.n
                asm_mark[t] = vX1
                self.wait("scalar", self.cV, vX1)
                self.dma("scalar", self.s_x1[t * TT:(t + 1) * TT].rearrange(
                    "(s p) h -> p s h", p=128), x1_tm[:])

                fdump = f_fm.ap()[:, 0:2, :].rearrange("p a b -> p (a b)")
                for s in range(NSUB):
                    self.act(fdump, x1_tm[:, s, :], Act.Identity,
                             accum=st[:, s, 0:1])
                    self.act(fdump, x1_tm[:, s, :], Act.Square,
                             accum=st[:, s, 1:2])
                sStat = self.cS.n
                self.wait("vector", self.cS, sStat)
                self.ln_finish(st[:, :, 0], st[:, :, 1], st[:, :, 2],
                               st[:, :, 3], st[:, :, 4])
                sFin = self.cS.n
                self.wait("vector", self.cS, sFin)
                for s in range(NSUB):
                    self.ts("vector", u1_tm[:, s, :], x1_tm[:, s, :],
                            st[:, s, 2:3], st[:, s, 3:4], Alu.subtract, Alu.mult)
                vU1 = self.cV.n
                ld_mark[t] = vU1
                self.wait("tensor", self.cV, vU1)
                for s in range(NSUB):
                    self.use_bank(4 + s)
                for s in range(NSUB):
                    for g in range(NG):
                        self.tr(self.pb16[4 + s][:, g * 128:(g + 1) * 128],
                                u1_tm[:, s, g * 128:(g + 1) * 128],
                                self.ident16[:])
                u1T_mark[t] = self.cT.n
                self.wait("vector", self.cT, u1T_mark[t])
                for s in range(NSUB):
                    self.vcopy(
                        u1buf[:, :, s * 128:(s + 1) * 128],
                        self.pb16[4 + s].rearrange("p (g c) -> p g c", g=NG))
                    self.guard_bank(4 + s, "vector")
                vU1e = self.cV.n
                self.wait("scalar", self.cV, vU1e)
                u1_spill[t] = self.dma("scalar", self.s_u1[t],
                                       u1buf.rearrange("p g t -> p (g t)"))

    def p3(self):
        C, cv = self, self.cvr
        nc = self.nc
        with ExitStack() as ph:
            psb = lambda n, s, d=F32: ph.enter_context(nc.sbuf_tensor(n, s, d))
            shg = psb("shg", [128, NG, D], BF16)
            shu = psb("shu", [128, NG, D], BF16)
            shd = shg
            u1q = psb("u1q", [128, NG, QTR], BF16)
            acc = psb("acc", [128, NG, QTR])
            gssh = psb("gssh", [128, NG, QTR], BF16)
            wge = psb("wge0", [128, NG, DI], BF16)
            wue = psb("wue0", [128, NG, DI], BF16)
            wde = psb("wde0", [128, 4, D], BF16)
            gs16 = psb("gs16", [128, 4, QTR], BF16)
            wb_sb = psb("wb_sb", [128, QTR])
            sc_all = psb("sc_all", [128, 8, NE])
            wmat = psb("wmat", [128, 8, NE])
            wfm = psb("wfm", [32, QTR])
            wfm16 = psb("wfm16", [32, QTR], BF16)
            wrow = psb("wrow", [1, 2, QTR], BF16)
            gsum8 = psb("gsum8", [128, 8])
            m8 = psb("m8", [128, 8])
            msk = psb("msk", [128, NE])
            stmp = psb("stmp", [128, 8, TT])
            utmp = gssh.ap().bitcast(F32)
            dtmp = psb("dtmp", [128, 2, TT])
            x1_tm = psb("x1b_tm", [128, NSUB, D])
            flat = lambda a: a.ap().rearrange("p a b -> p (a b)")
            self.memset(gsum8[:, 4:8], -1e30)
            self.wait("sync", self.cG, self.agw_done)
            self.wait("gpsimd", self.cG, self.agw_done)

            self._a_done = 0
            self._d_done = 0
            self._d_mms = 0
            self._gs_read = [0, 0]
            self._shg_use = 0
            self._wrow_use = 0
            u1q_use = 0
            ylast = 0
            for q in range(NQ):
                self.wait("sync", self.cT, u1q_use)
                uq = self.dma("sync", u1q[:, :, 0:TT], self.s_u1[2 * q])
                uq = self.dma("sync", u1q[:, :, TT:QTR], self.s_u1[2 * q + 1])
                self.wait("tensor", self.dS, uq)

                for i in range(8):
                    self.use_bank(3)
                    for ic in range(NG):
                        self.mm(self.pb[3][:, i * 64:i * 64 + NE],
                                u1q[:, ic, i * 128:(i + 1) * 128],
                                self.gwT[:, ic, :], ic == 0, ic == NG - 1)
                tLG = self.cT.n
                self.wait("scalar", self.cT, tLG)
                for i in range(8):
                    self.act(sc_all[:, i, :], self.pb[3][:, i * 64:i * 64 + NE],
                             Act.Tanh, scale=0.5)
                self.guard_bank(3, "scalar")
                sLG = self.cS.n
                self.wait("vector", self.cS, sLG)
                self.ts("vector", flat(sc_all), flat(sc_all), 0.5, 0.5,
                        Alu.mult, Alu.add)
                for i in range(8):
                    sc = sc_all[:, i, :]
                    wm = wmat[:, i, :]
                    self.tt("vector", wm, sc, self.gbias_bc[:], Alu.add)
                    self.vsync()
                    for grp in range(4):
                        self.sched("vector", lambda e, grp=grp, wm=wm: e.max(
                            out=m8[:], in_=wm[:, grp * 8:(grp + 1) * 8]),
                            inc=self.cV)
                        self.vsync()
                        self.tt("vector", gsum8[:, grp:grp + 1], m8[:, 0:1],
                                m8[:, 1:2], Alu.add)
                        self.vsync()
                    self.sched("vector", lambda e: e.max(
                        out=m8[:], in_=gsum8[:]), inc=self.cV)
                    self.vsync()
                    self.sched("vector", lambda e: e.tensor_single_scalar(
                        out=gsum8[:, 0:4], in_=gsum8[:, 0:4],
                        scalar=m8[:, 1:2], op=Alu.is_ge), inc=self.cV)
                    self.vsync()
                    self.tt("vector",
                            msk.ap().rearrange("p (g e) -> p g e", g=4),
                            gsum8.ap()[:, 0:4].unsqueeze(2).to_broadcast(
                                [128, 4, 8]),
                            wm.rearrange("p (g e) -> p g e", g=4), Alu.mult)
                    self.vsync()
                    self.sched("vector", lambda e: e.max(
                        out=m8[:], in_=msk[:]), inc=self.cV)
                    self.vsync()
                    self.sched("vector", lambda e: e.tensor_single_scalar(
                        out=msk[:], in_=msk[:], scalar=m8[:, 7:8],
                        op=Alu.is_ge), inc=self.cV)
                    self.vsync()
                    self.tt("vector", wm, sc, msk[:], Alu.mult)
                    self.vsync()
                    self.sched("vector", lambda e, wm=wm: e.reduce_sum(
                        out=m8[:, 0:1], in_=wm, axis=mybir.AxisListType.X),
                        inc=self.cV)
                    self.vsync()
                    self.ts("vector", m8[:, 0:1], m8[:, 0:1], 1e-20, None,
                            Alu.add)
                    self.vsync()
                    self.sched("vector", lambda e: e.reciprocal(
                        out=m8[:, 1:2], in_=m8[:, 0:1]), inc=self.cV)
                    self.vsync()
                    self.ts("vector", m8[:, 1:2], m8[:, 1:2], 2.5, None,
                            Alu.mult)
                    self.vsync()
                    self.sched("vector", lambda e, wm=wm: e.tensor_single_scalar(
                        out=wm, in_=wm, scalar=m8[:, 1:2], op=Alu.mult),
                        inc=self.cV)
                    self.vsync()
                vGate = self.cV.n
                self.wait("sync", self.cT, self._shg_use)
                self.dma("sync", flat(shg), self.s_shT[0])
                shLD = self.dma("sync", flat(shu), self.s_shT[1])
                self.wait("tensor", self.dS, shLD)
                sh_mark = [0, 0, 0, 0]
                for tk in range(2):
                    for oc in range(NG):
                        b0 = (oc % 2) * 2
                        b1 = b0 + 1
                        self.use_bank(b0)
                        self.use_bank(b1)
                        for ic in range(NG):
                            self.mm(self.pb[b0][:, 0:TT],
                                    shg[:, ic, oc * 128:(oc + 1) * 128],
                                    u1q[:, ic, tk * TT:(tk + 1) * TT],
                                    ic == 0, ic == NG - 1)
                        for ic in range(NG):
                            self.mm(self.pb[b1][:, 0:TT],
                                    shu[:, ic, oc * 128:(oc + 1) * 128],
                                    u1q[:, ic, tk * TT:(tk + 1) * TT],
                                    ic == 0, ic == NG - 1)
                        tMM = self.cT.n
                        self.wait("scalar", self.cT, tMM)
                        self.wait("scalar", self.cV, sh_mark[oc % 4])
                        self.act(stmp[:, oc % 4, :], self.pb[b0][:, 0:TT],
                                 Act.Silu)
                        self.guard_bank(b0, "scalar")
                        sSi = self.cS.n
                        self.wait("vector", self.cS, sSi)
                        self.tt("vector", stmp[:, 4 + oc % 4, :],
                                self.pb[b1][:, 0:TT], stmp[:, oc % 4, :],
                                Alu.mult)
                        self.guard_bank(b1, "vector")
                        self.vcopy(gssh[:, oc, tk * TT:(tk + 1) * TT],
                                   stmp[:, 4 + oc % 4, :])
                        sh_mark[oc % 4] = self.cV.n
                vSh = self.cV.n
                guDone = self.cT.n
                self.wait("sync", self.cT, guDone)
                shdLD = self.dma("sync", flat(shd), self.s_shT[2])
                self.wait("tensor", self.cV, vGate)
                self.use_bank(2)
                self.use_bank(3)
                for i in range(8):
                    bk = 2 + i // 4
                    self.tr(self.pb[bk][0:NE, (i % 4) * 128:(i % 4) * 128 + 128],
                            wmat[:, i, :], self.ident32[:])
                tWM = self.cT.n
                self.wait("vector", self.cT, tWM)
                for i in range(8):
                    bk = 2 + i // 4
                    self.vcopy(wfm[:, i * 128:(i + 1) * 128],
                               self.pb[bk][0:NE, (i % 4) * 128:(i % 4) * 128 + 128])
                self.guard_bank(2, "vector")
                self.guard_bank(3, "vector")
                self.vcopy(wfm16[:], wfm[:])
                vWF = self.cV.n
                self.wait("tensor", self.cV, vSh)
                self.wait("tensor", self.dS, shdLD)
                for tk in range(2):
                    for oc in range(NG):
                        bank = oc % 2
                        self.use_bank(bank)
                        for ic in range(NG):
                            self.mm(self.pb[bank][:, 0:TT],
                                    shd[:, ic, oc * 128:(oc + 1) * 128],
                                    gssh[:, ic, tk * TT:(tk + 1) * TT],
                                    ic == 0, ic == NG - 1)
                        tMM = self.cT.n
                        self.wait("vector", self.cT, tMM)
                        self.vcopy(acc[:, oc, tk * TT:(tk + 1) * TT],
                                   self.pb[bank][:, 0:TT])
                        self.guard_bank(bank, "vector")
                self._shg_use = self.cT.n

                dn_marks = {}
                vShD = self.cV.n
                self.wait("gpsimd", self.cV, vShD)
                self.wait("scalar", self.cV, vWF)
                self.wait("scalar", self.cT, self._wrow_use)
                wrow_mark = [self.dma("scalar", wrow[:, 0, :], wfm16[0:1, :]), 0]
                wrow_mark_use = [0, 0]
                for e in range(NE):
                    self.wait("sync", self.cT, self._a_done)
                    for part, dst in ((0, wge), (1, wue)):
                        src = self.ccw_out[e // EPL,
                                           ((e % EPL) * 3 + part) * EMAT:
                                           ((e % EPL) * 3 + part + 1) * EMAT]
                        ldGU = self.dma("sync", flat(dst),
                                        src.rearrange("(p f) -> p f", p=128))
                    self.wait("sync", self.cT, self._d_done)
                    srcd = self.ccw_out[e // EPL,
                                        ((e % EPL) * 3 + 2) * EMAT:
                                        ((e % EPL) * 3 + 3) * EMAT]
                    ldD = self.dma("sync", flat(wde),
                                   srcd.rearrange("(p f) -> p f", p=128))
                    self.wait("tensor", self.cV, vWF)
                    self.wait("tensor", self.dA, wrow_mark[e % 2])
                    self.use_bank(6)
                    self.use_bank(7)
                    self.mm(self.pb[6][:, 0:TT], self.ones16[:],
                            wrow[:, e % 2, 0:TT], True, True)
                    self.mm(self.pb[7][:, 0:TT], self.ones16[:],
                            wrow[:, e % 2, TT:QTR], True, True)
                    self._wrow_use = self.cT.n
                    tWB = self.cT.n
                    if e + 1 < NE:
                        self.wait("scalar", self.cT, wrow_mark_use[(e + 1) % 2])
                        wrow_mark[(e + 1) % 2] = self.dma(
                            "scalar", wrow[:, (e + 1) % 2, :],
                            wfm16[e + 1:e + 2, :])
                    wrow_mark_use[e % 2] = self.cT.n
                    self.wait("vector", self.cT, tWB)
                    self.vcopy(wb_sb[:, 0:TT], self.pb[6][:, 0:TT])
                    self.vcopy(wb_sb[:, TT:QTR], self.pb[7][:, 0:TT])
                    self.guard_bank(6, "vector")
                    self.guard_bank(7, "vector")
                    vWB = self.cV.n

                    self.wait("tensor", self.dS, ldGU)
                    gu_marks = {}
                    for tk in range(2):
                        for oc in range(4):
                            self.use_bank(4 * tk + oc)
                            for ic in range(NG):
                                self.mm(self.pb[4 * tk + oc][:, 0:TT],
                                        wge[:, ic, oc * 128:(oc + 1) * 128],
                                        u1q[:, ic, tk * TT:(tk + 1) * TT],
                                        ic == 0, ic == NG - 1)
                        gu_marks[("g", tk)] = self.cT.n
                        self.wait("scalar", self.cT, gu_marks[("g", tk)])
                        self.wait("scalar", self.cV, self._gs_read[tk])
                        for oc in range(4):
                            self.act(stmp[:, 4 * tk + oc, :],
                                     self.pb[4 * tk + oc][:, 0:TT], Act.Silu)
                            self.guard_bank(4 * tk + oc, "scalar")
                        gu_marks[("s", tk)] = self.cS.n
                    for tk in range(2):
                        for oc in range(4):
                            self.use_bank(4 * tk + oc)
                            for ic in range(NG):
                                self.mm(self.pb[4 * tk + oc][:, 0:TT],
                                        wue[:, ic, oc * 128:(oc + 1) * 128],
                                        u1q[:, ic, tk * TT:(tk + 1) * TT],
                                        ic == 0, ic == NG - 1)
                        gu_marks[("u", tk)] = self.cT.n
                        self.wait("vector", self.cT, gu_marks[("u", tk)])
                        self.wait("vector", self.cS, gu_marks[("s", tk)])
                        if tk == 0:
                            self.wait("vector", self.cT, self._d_mms)
                        for oc in range(4):
                            self.tt("vector", utmp[:, 4 * tk + oc, :],
                                    self.pb[4 * tk + oc][:, 0:TT],
                                    wb_sb[:, tk * TT:(tk + 1) * TT], Alu.mult)
                            self.guard_bank(4 * tk + oc, "vector")
                        for oc in range(4):
                            self.tt("vector",
                                    gs16[:, oc, tk * TT:(tk + 1) * TT],
                                    stmp[:, 4 * tk + oc, :],
                                    utmp[:, 4 * tk + oc, :], Alu.mult)
                        gu_marks[("gs", tk)] = self.cV.n
                        self._gs_read[tk] = self.cV.n
                    self._a_done = self.cT.n
                    self.wait("tensor", self.dS, ldD)
                    for tk in range(2):
                        self.wait("tensor", self.cV, gu_marks[("gs", tk)])
                        for oc in range(NG):
                            bank = 2 * tk + oc % 2
                            self.use_bank(bank)
                            for ic in range(4):
                                self.mm(self.pb[bank][:, 0:TT],
                                        wde[:, ic, oc * 128:(oc + 1) * 128],
                                        gs16[:, ic, tk * TT:(tk + 1) * TT],
                                        ic == 0, ic == 3)
                            tDN = self.cT.n
                            self.wait("scalar", self.cT, tDN)
                            dslot = oc % 2
                            self.wait("scalar", self.dG,
                                      dn_marks.get(dslot, 0))
                            self.scopy(dtmp[:, dslot, :], self.pb[bank][:, 0:TT])
                            self.guard_bank(bank, "scalar")
                            sCp = self.cS.n
                            self.wait("gpsimd", self.cS, sCp)
                            dn_marks[dslot] = self.dma(
                                "gpsimd", acc[:, oc, tk * TT:(tk + 1) * TT],
                                dtmp[:, dslot, :], accum_op=Alu.add)
                    self._d_done = self.cT.n
                    self._d_mms = self.cT.n

                self.wait("vector", self.dG)
                u1q_use = self._a_done
                for tk in range(2):
                    self.wait("scalar", self.dG, ylast)
                    xm = self.dma("scalar", x1_tm[:],
                                  self.s_x1[q * QTR + tk * TT:
                                            q * QTR + (tk + 1) * TT].rearrange(
                                      "(s p) h -> p s h", p=128))
                    self.wait("vector", self.dA, xm)
                    self.wait("tensor", self.dA, xm)
                    for s in range(NSUB):
                        b0, b1 = 4 + 2 * (s % 2), 5 + 2 * (s % 2)
                        self.use_bank(b0)
                        self.use_bank(b1)
                        for g in range(NG):
                            bank = b0 if g < 4 else b1
                            self.tr(self.pb[bank].ap()[:, (g % 4) * 128:
                                                       (g % 4) * 128 + 128],
                                    acc[:, g, tk * TT + s * 128:
                                        tk * TT + (s + 1) * 128],
                                    self.ident32[:])
                        tS = self.cT.n
                        self.wait("vector", self.cT, tS)
                        self.tt("vector", x1_tm[:, s, 0:512], self.pb[b0][:],
                                x1_tm[:, s, 0:512], Alu.add)
                        self.tt("vector", x1_tm[:, s, 512:1024], self.pb[b1][:],
                                x1_tm[:, s, 512:1024], Alu.add)
                        self.guard_bank(b0, "vector")
                        self.guard_bank(b1, "vector")
                    vOut = self.cV.n
                    self.wait("gpsimd", self.cV, vOut)
                    ylast = self.dma(
                        "gpsimd",
                        self.y[q * QTR + tk * TT:q * QTR + (tk + 1) * TT].rearrange(
                            "(s p) h -> p s h", p=128), x1_tm[:])

    def finish(self):
        for name, (tap, src) in self.tap_outs.items():
            self.wait("gpsimd", self.dA)
            self.wait("gpsimd", self.dS)
            self.dma("gpsimd", tap, src)
        self.wait("gpsimd", self.dG)
        self.wait("sync", self.dG)


_BUILT = {}


def _get_nc(taps=()):
    key = tuple(sorted(taps))
    if key not in _BUILT:
        mk = MK(taps)
        nc = mk.build()
        _BUILT[key] = (nc, mk.tap_outs)
    return _BUILT[key]


def make_in_maps(inputs):
    x = np.ascontiguousarray(np.asarray(inputs["L0_attn_x"],
                                        dtype=np.float32).reshape(-1, D))
    g = lambda n: np.ascontiguousarray(np.asarray(inputs[n], np.float32))
    vec = lambda n: g(n).reshape(1, D)
    common = {
        "w_ln_att_g": vec("L0_attn_ln_att_g"),
        "w_ln_att_b": vec("L0_attn_ln_att_b"),
        "w_ln_ffn_g": vec("L0_attn_ln_ffn_g"),
        "w_ln_ffn_b": vec("L0_attn_ln_ffn_b"),
        "w_td": vec("L0_attn_att_time_decay"),
        "w_tf": vec("L0_attn_att_time_first"),
        "w_mk": vec("L0_attn_att_mix_k"),
        "w_mv": vec("L0_attn_att_mix_v"),
        "w_mr": vec("L0_attn_att_mix_r"),
        "w_fmk": vec("L0_attn_ffn_mix_k"),
        "w_fmr": vec("L0_attn_ffn_mix_r"),
        "w_wk": g("L0_attn_att_wk"),
        "w_wv": g("L0_attn_att_wv"),
        "w_wr": g("L0_attn_att_wr"),
        "w_wo": g("L0_attn_att_wo"),
        "w_fwk": g("L0_attn_ffn_wk"),
        "w_fwv": g("L0_attn_ffn_wv"),
        "w_fwr": g("L0_attn_ffn_wr"),
        "w_gw": g("L0_ffn_gate_w"),
        "w_gb": g("L0_ffn_gate_bias").reshape(1, NE),
        "w_sg": g("L0_ffn_shared_gate"),
        "w_su": g("L0_ffn_shared_up"),
        "w_sd": g("L0_ffn_shared_down"),
    }
    eg = g("L0_ffn_expert_gate")
    eu = g("L0_ffn_expert_up")
    ed = g("L0_ffn_expert_down")
    maps = []
    for c in range(NCORES):
        seq, pos = c // 4, c % 4
        a8 = np.full(8, 1e9, np.float32)
        sel8 = np.zeros((8, 1), np.float32)
        for j in range(NCORES):
            if j // 4 == seq and j < c:
                a8[j] = float(TPC) * (c - 1 - j)
        if pos > 0:
            sel8[c - 1, 0] = 1.0
            xp = x[c * TPC - 1:c * TPC]
            pm = 1.0
        else:
            xp = np.zeros((1, D), np.float32)
            pm = 0.0
        m = dict(common)
        m["xs"] = x[c * TPC:(c + 1) * TPC]
        m["xp"] = np.ascontiguousarray(xp)
        m["a8"] = a8.reshape(1, 8)
        m["sel8"] = sel8
        m["pmask"] = np.array([[pm]], np.float32)
        m["w_meg"] = np.ascontiguousarray(eg[c * EPL:(c + 1) * EPL])
        m["w_meu"] = np.ascontiguousarray(eu[c * EPL:(c + 1) * EPL])
        m["w_med"] = np.ascontiguousarray(ed[c * EPL:(c + 1) * EPL])
        maps.append(m)
    return maps


def kernel(**inputs):
    nc, _ = _get_nc()
    maps = make_in_maps(inputs)
    res = run_bass_kernel_spmd(nc, maps, core_ids=list(range(NCORES)))
    out = np.concatenate([res.results[c]["y"] for c in range(NCORES)], axis=0)
    return out.reshape(B, L, D)


# version 14
# speedup vs baseline: 1.3553x; 1.2115x over previous
import numpy as np
from contextlib import ExitStack

import concourse.bass as bass
import concourse.mybir as mybir
from concourse.bass_utils import run_bass_kernel_spmd

F32 = mybir.dt.float32
BF16 = mybir.dt.bfloat16
Alu = mybir.AluOpType
Act = mybir.ActivationFunctionType

NCORES = 8
B, L, D = 2, 16384, 1024
TPC = 4096
NT = 8
TT = 512
NSUB = 4
NG = 8
NE = 32
EPL = 4
DI = 512
EPS = 1e-5
RG = [[0, 1, 2, 3, 4, 5, 6, 7]]
EMAT = DI * D
WSH = EPL * 3 * EMAT
QTR = 1024
NQ = TPC // QTR
ENGS = ("tensor", "vector", "scalar", "sync", "gpsimd")


class Sem:
    def __init__(self, nc, es, name):
        self.h = es.enter_context(nc.semaphore(name))
        self.n = 0


class MK:
    def __init__(self, taps=()):
        self.taps = taps
        self.nc = bass.Bass("TRN2", target_bir_lowering=False, debug=False)
        self.es = ExitStack()
        self.ops = {e: [] for e in ENGS}
        self.tap_outs = {}
        self.bank_guard = {}

    def sched(self, eng, fn, inc=None, k=1):
        if inc is not None:
            inc.n += k

            def thunk(e, fn=fn, inc=inc, k=k):
                fn(e).then_inc(inc.h, k)
        else:
            def thunk(e, fn=fn):
                fn(e)
        self.ops[eng].append(thunk)

    def wait(self, eng, sem, val=None):
        v = sem.n if val is None else val
        self.ops[eng].append(lambda e, h=sem.h, v=v: e.wait_ge(h, v))

    def dma(self, eng, out, in_, **kw):
        sem = {"sync": self.dS, "scalar": self.dA, "gpsimd": self.dG}[eng]
        self.sched(eng, lambda e, o=out, i=in_, kw=kw: e.dma_start(out=o, in_=i, **kw),
                   inc=sem, k=16)
        return sem.n

    def barrier(self, engs=ENGS):
        if "sync" in engs:
            self.wait("sync", self.dS)
        if "scalar" in engs:
            self.wait("scalar", self.dA)
        if "gpsimd" in engs:
            self.wait("gpsimd", self.dG)
        marks = {}
        for e in engs:
            s = self.fence[e]
            self.sched(e, lambda en: en.nop(), inc=s)
            marks[e] = s.n
        for e in engs:
            for o in engs:
                if o != e:
                    self.wait(e, self.fence[o], marks[o])

    def emit(self, blk):
        for e in ENGS:
            getattr(blk, e)(lambda eng, e=e: [f(eng) for f in self.ops[e]])

    def csem(self, eng):
        return {"vector": self.cV, "scalar": self.cS, "gpsimd": self.cG}[eng]

    def ts(self, eng, out, in0, s1, s2, op0, op1=Alu.bypass, accum=None):
        self.sched(eng, lambda e: e.tensor_scalar(
            out=out, in0=in0, scalar1=s1, scalar2=s2, op0=op0, op1=op1,
            accum_out=accum), inc=self.csem(eng))

    def tt(self, eng, out, in0, in1, op):
        self.sched(eng, lambda e: e.tensor_tensor(out=out, in0=in0, in1=in1, op=op),
                   inc=self.csem(eng))

    def stt(self, eng, out, in0, scalar, in1, op0, op1):
        self.sched(eng, lambda e: e.scalar_tensor_tensor(
            out=out, in0=in0, scalar=scalar, in1=in1, op0=op0, op1=op1),
            inc=self.csem(eng))

    def act(self, out, in_, func, bias=0.0, scale=1.0, accum=None):
        self.sched("scalar", lambda e: e.activation(
            out=out, in_=in_, func=func, bias=bias, scale=scale,
            accum_out=accum), inc=self.cS)

    def scopy(self, out, in_):
        self.sched("scalar", lambda e: e.copy(out=out, in_=in_), inc=self.cS)

    def vcopy(self, out, in_):
        self.sched("vector", lambda e: e.tensor_copy(out=out, in_=in_), inc=self.cV)

    def memset(self, ap, val):
        self.sched("vector", lambda e: e.memset(ap, val), inc=self.cV)

    def mm(self, out, lhsT, rhs, start, stop):
        self.sched("tensor", lambda e: e.matmul(
            out=out, lhsT=lhsT, rhs=rhs, start=start, stop=stop),
            inc=self.cT if stop else None)

    def tr(self, out, in_, ident):
        self.sched("tensor", lambda e: e.transpose(
            out=out, in_=in_, identity=ident), inc=self.cT)

    def use_bank(self, bank):
        g = self.bank_guard.get(bank)
        if g is not None:
            sem, mark = g
            self.wait("tensor", sem, mark)

    def guard_bank(self, bank, eng):
        self.bank_guard[bank] = (self.csem(eng), self.csem(eng).n)

    def vsync(self):
        self.wait("vector", self.cV)

    def ln_finish(self, s_sum, s_sq, mean, rstd, tmp, nelem=D):
        inv = 1.0 / nelem
        self.ts("vector", mean, s_sum, inv, None, Alu.mult)
        self.ts("vector", rstd, s_sq, inv, None, Alu.mult)
        self.vsync()
        self.tt("vector", tmp, mean, mean, Alu.mult)
        self.vsync()
        self.tt("vector", rstd, rstd, tmp, Alu.subtract)
        self.vsync()
        self.ts("vector", rstd, rstd, EPS, None, Alu.add)
        vm = self.cV.n
        self.wait("scalar", self.cV, vm)
        self.act(rstd, rstd, Act.Ln)
        self.act(rstd, rstd, Act.Exp, scale=-0.5)

    def build(self):
        nc, es = self.nc, self.es
        dram = lambda n, s, k="ExternalInput", d=F32: nc.dram_tensor(
            n, s, d, kind=k).ap()
        self.sbuf = lambda n, s, d=F32: es.enter_context(nc.sbuf_tensor(n, s, d))
        sbuf = self.sbuf

        self.xs = dram("xs", [TPC, D])
        self.xp = dram("xp", [1, D])
        self.a8 = dram("a8", [1, 8])
        self.sel8 = dram("sel8", [8, 1])
        self.pmask = dram("pmask", [1, 1])
        names = ["ln_att_g", "ln_att_b", "ln_ffn_g", "ln_ffn_b", "td", "tf",
                 "mk", "mv", "mr", "fmk", "fmr"]
        self.wvec = {n: dram("w_" + n, [1, D]) for n in names}
        self.w_wk = dram("w_wk", [D, D])
        self.w_wv = dram("w_wv", [D, D])
        self.w_wr = dram("w_wr", [D, D])
        self.w_wo = dram("w_wo", [D, D])
        self.w_fwk = dram("w_fwk", [D, 4 * D])
        self.w_fwv = dram("w_fwv", [4 * D, D])
        self.w_fwr = dram("w_fwr", [D, D])
        self.w_gw = dram("w_gw", [NE, D])
        self.w_gb = dram("w_gb", [1, NE])
        self.w_meg = dram("w_meg", [EPL, DI, D])
        self.w_meu = dram("w_meu", [EPL, DI, D])
        self.w_med = dram("w_med", [EPL, D, DI])
        self.w_sg = dram("w_sg", [D, D])
        self.w_su = dram("w_su", [D, D])
        self.w_sd = dram("w_sd", [D, D])
        self.y = dram("y", [TPC, D], k="ExternalOutput")

        self.s_num = dram("s_num", [NT, 128, NG * TT], k="Internal")
        self.s_den = dram("s_den", [NT, 128, NG * TT], k="Internal")
        self.s_r = dram("s_r", [NT, 128, NG * TT], k="Internal", d=BF16)
        self.s_u0 = dram("s_u0", [TPC, D], k="Internal")
        self.s_xb = dram("s_xb", [TPC, D], k="Internal")
        self.s_x1 = dram("s_x1", [TPC, D], k="Internal")
        self.s_u1 = dram("s_u1", [NT, 128, NG * TT], k="Internal", d=BF16)
        self.s_kf = dram("s_kf", [NT, 128, 32 * TT], k="Internal", d=BF16)
        self.s_rf = dram("s_rf", [NT, 128, NG * TT], k="Internal", d=BF16)
        self.s_shT = dram("s_shT", [3, 128, NG * D], k="Internal", d=BF16)
        self.cc1_in = dram("cc1_in", [1, 2 * D], k="Internal")
        self.cc1_out = dram("cc1_out", [8, 2 * D], k="Internal")
        self.cc2_in = dram("cc2_in", [1, D], k="Internal")
        self.cc2_out = dram("cc2_out", [8, D], k="Internal")
        self.ccw_in = dram("ccw_in", [1, WSH], k="Internal", d=BF16)
        self.ccw_out = dram("ccw_out", [8, WSH], k="Internal", d=BF16)

        if "h2" in self.taps:
            self.s_h2 = dram("s_h2", [NT, 128, NG * TT], k="Internal", d=BF16)
            self.s_xk2 = dram("s_xk2", [NT, 128, NG * TT], k="Internal", d=BF16)
        for t in self.taps:
            src = {"num": self.s_num, "den": self.s_den, "r": self.s_r,
                   "u0": self.s_u0, "xb": self.s_xb, "x1": self.s_x1,
                   "u1": self.s_u1, "kf": self.s_kf, "rf": self.s_rf,
                   "h2": getattr(self, "s_h2", None),
                   "xk2": getattr(self, "s_xk2", None)}[t]
            self.tap_outs[t] = (nc.dram_tensor(
                "tap_" + t, list(src.shape), src.dtype,
                kind="ExternalOutput").ap(), src)

        self.c_id = nc.inline_tensor(np.eye(128, dtype=np.float32), "c_id").ap()
        self.c_ramp = nc.inline_tensor(
            np.arange(TT, dtype=np.float32)[None, :], "c_ramp").ap()
        self.c_ones = nc.inline_tensor(
            np.ones((1, 128), dtype=np.float32), "c_ones").ap()

        self.ident32 = sbuf("ident32", [128, 128])
        self.ident16 = sbuf("ident16", [128, 128], BF16)
        self.ones32 = sbuf("ones32", [1, 128])
        self.ones16 = sbuf("ones16", [1, 128], BF16)
        self.ramp01 = sbuf("ramp01", [128, TT])
        self.rampsb = sbuf("rampsb", [1, TT])
        NCON = 18
        self.cvr = sbuf("cfm", [128, NCON, NG])
        (self.C_MK, self.C_MV, self.C_MR, self.C_FMK, self.C_FMR, self.C_LAG,
         self.C_LAB, self.C_LFG, self.C_LFB, self.C_TD, self.C_TF, self.C_W,
         self.C_EW, self.C_EU, self.C_PIN, self.C_QIN, self.C_WTB,
         self.C_TMP) = range(NCON)
        self.a8sb = sbuf("a8sb", [1, 8])
        self.a8bc = sbuf("a8bc", [128, 8])
        self.pmask_sb = sbuf("pmask_sb", [1, 1])
        self.sel8_sb = sbuf("sel8_sb", [8, 1])
        self.gwT = sbuf("gwT", [128, NG, NE], BF16)
        self.gbias_row = sbuf("gbias_row", [1, NE])
        self.gbias_bc = sbuf("gbias_bc", [128, NE])
        self.bnd_h1 = sbuf("bnd_h1", [128, NG, 1], BF16)
        self.bnd_h2 = sbuf("bnd_h2", [128, NG, 1], BF16)
        self.row_a = sbuf("row_a", [1, D])
        self.row_b = sbuf("row_b", [1, D])
        self.row_gx = sbuf("row_gx", [1, D])
        self.row_bx = sbuf("row_bx", [1, D])
        self.row16 = sbuf("row16", [1, D], BF16)
        self.rst = sbuf("rst", [1, 8])
        self.pq_j = sbuf("pq_j", [128, 8, 16])
        self.coefs = sbuf("coefs", [128, NG, 8])

        self.pb = [es.enter_context(nc.psum_tensor(f"pb{i}", [128, 512], F32))
                   for i in range(8)]
        self.pb16 = [p.ap().bitcast(BF16) for p in self.pb]

        for s in ("dG", "dS", "dA", "cT", "cV", "cS", "cG"):
            setattr(self, s, Sem(nc, es, s))
        self.fence = {e: Sem(nc, es, "f_" + e) for e in ENGS}

        blk = es.enter_context(nc.Block())

        with ExitStack() as stg_scope:
            pst = lambda n, s, d=BF16: stg_scope.enter_context(
                nc.sbuf_tensor(n, s, d))
            self.stg = [pst("stg0", [128, 4096])]
            self.wtb = pst("wtb", [128, 4096])

            self.p0()
            self.p1a_and_exchange()
            self.barrier(engs=("tensor", "vector", "scalar"))
            self.p1b()
            self.barrier()
            self.p2a()
            self.barrier()
            self.p2b()
            self.barrier()
        self.p3()
        self.finish()

        self.emit(blk)
        es.close()
        return nc

    def load_cvec(self, col, src):
        self.dma("gpsimd", self.cvr[:, col, :],
                 src.rearrange("o (g p) -> (p o) g", p=128),
                 allow_slow_non_contiguous=True)

    def p0(self):
        C, cv = self, self.cvr
        self.dma("sync", self.ident32[:], self.c_id)
        self.dma("sync", self.ones32[:], self.c_ones)
        self.dma("sync", self.rampsb[:], self.c_ramp)
        self.dma("sync", self.gbias_row[:], self.w_gb)
        self.dma("gpsimd", self.ident16[:], self.c_id)
        self.dma("gpsimd", self.ones16[:], self.c_ones)
        self.dma("gpsimd", self.a8sb[:], self.a8)
        self.dma("gpsimd", self.pmask_sb[:], self.pmask)
        self.dma("gpsimd", self.sel8_sb[:], self.sel8)
        self.dma("gpsimd", self.row_gx[:], self.wvec["ln_att_g"])
        self.dma("gpsimd", self.row_bx[:], self.wvec["ln_att_b"])
        for col, src in ((C.C_MK, "mk"), (C.C_MV, "mv"), (C.C_MR, "mr"),
                         (C.C_FMK, "fmk"), (C.C_FMR, "fmr"),
                         (C.C_LAG, "ln_att_g"), (C.C_LAB, "ln_att_b"),
                         (C.C_LFG, "ln_ffn_g"), (C.C_LFB, "ln_ffn_b"),
                         (C.C_TD, "td"), (C.C_TF, "tf")):
            self.load_cvec(col, self.wvec[src])
        self.dma("gpsimd", self.row_a[:], self.xp)
        gLD = self.dG.n
        self.gLD = gLD
        self.dS_consts = self.dS.n

        self.wait("scalar", self.dG, gLD)
        self.act(cv[:, C.C_W, :], cv[:, C.C_TD, :], Act.Exp)
        sW = self.cS.n
        self.wait("vector", self.cS, sW)
        self.ts("vector", cv[:, C.C_W, :], cv[:, C.C_W, :], -1.0, None, Alu.mult)
        vW = self.cV.n
        self.wait("scalar", self.cV, vW)
        self.act(cv[:, C.C_EW, :], cv[:, C.C_W, :], Act.Exp)
        self.act(cv[:, C.C_EU, :], cv[:, C.C_TF, :], Act.Exp)

        self.wait("tensor", self.dS)
        self.wait("tensor", self.dG, gLD)
        self.mm(self.pb[6][:, 0:TT], self.ones32[:], self.rampsb[:], True, True)
        self.mm(self.pb[7][:, 0:8], self.ones32[:], self.a8sb[:], True, True)
        self.mm(self.pb[7][:, 8:8 + NE], self.ones32[:], self.gbias_row[:],
                True, True)
        tBC = self.cT.n
        self.wait("vector", self.cT, tBC)
        self.vcopy(self.ramp01[:], self.pb[6][:, 0:TT])
        self.vcopy(self.a8bc[:], self.pb[7][:, 0:8])
        self.vcopy(self.gbias_bc[:], self.pb[7][:, 8:8 + NE])

        for b in (6, 7):
            self.guard_bank(b, "vector")
        self.pb_free_v = self.cV.n

    def build_gwT(self, gw16):
        self.wait("tensor", self.dG)
        for hc in range(NG):
            self.tr(self.pb16[5][:, hc * 64:hc * 64 + NE],
                    gw16[:, hc * 128:(hc + 1) * 128],
                    self.ident16[0:NE, 0:NE])
        tGW = self.cT.n
        self.wait("vector", self.cT, tGW)
        for hc in range(NG):
            self.vcopy(self.gwT[:, hc, :], self.pb16[5][:, hc * 64:hc * 64 + NE])
        self.guard_bank(5, "vector")

    def p0_staging(self):
        mats = []
        for el in range(EPL):
            mats.append((self.w_meg[el], 4, 8,
                         ("ccw", (el * 3 + 0) * EMAT, EMAT)))
            mats.append((self.w_meu[el], 4, 8,
                         ("ccw", (el * 3 + 1) * EMAT, EMAT)))
            mats.append((self.w_med[el], 8, 4,
                         ("ccw", (el * 3 + 2) * EMAT, EMAT)))
        for mi, src in enumerate((self.w_sg, self.w_su, self.w_sd)):
            for half in range(2):
                mats.append((src[:, half * 512:(half + 1) * 512], 8, 4,
                             ("shT", mi, half)))
        store_mark = 0
        for i, (src, nrb, ncb, dest) in enumerate(mats):
            stg = self.stg[0]
            if i >= 2:
                pass
            ldm = self.dma(
                "gpsimd",
                stg.ap().rearrange("p (rb c) -> p rb c", rb=nrb),
                src.rearrange("(rb p) c -> p rb c", p=128))
            self.wait("sync", self.dG, ldm)
            if i >= 1:
                self.wait("sync", self.dG, store_mark)
            for rb in range(nrb):
                for cb in range(ncb):
                    self.sched("sync", lambda e, rb=rb, cb=cb, nrb=nrb, ncb=ncb,
                               stg=stg: e.dma_start_transpose(
                                   out=self.wtb[:, (cb * nrb + rb) * 128:
                                                (cb * nrb + rb) * 128 + 128],
                                   in_=stg[:, (rb * ncb + cb) * 128:
                                           (rb * ncb + cb) * 128 + 128]),
                               inc=self.dS, k=16)
            trm = self.dS.n
            self.wait("gpsimd", self.dS, trm)
            if dest[0] == "ccw":
                _, off, n = dest
                store_mark = self.dma(
                    "gpsimd",
                    self.ccw_in[0, off:off + n].rearrange("(p f) -> p f", p=128),
                    self.wtb[:])
                if i == EPL * 3 - 1:
                    self.ccw_store_mark = store_mark
            else:
                _, mi, half = dest
                store_mark = self.dma(
                    "gpsimd", self.s_shT[mi, :, half * 4096:(half + 1) * 4096],
                    self.wtb[:])


    def row_ln(self, dst_bnd, g_row, b_row):
        self.act(self.row_b[:], self.row_a[:], Act.Identity,
                 accum=self.rst[:, 0:1])
        self.act(self.row_b[:], self.row_a[:], Act.Square,
                 accum=self.rst[:, 1:2])
        sM = self.cS.n
        self.wait("vector", self.cS, sM)
        self.ln_finish(self.rst[:, 0:1], self.rst[:, 1:2], self.rst[:, 2:3],
                       self.rst[:, 3:4], self.rst[:, 4:5])
        sM2 = self.cS.n
        self.wait("vector", self.cS, sM2)
        self.ts("vector", self.row_b[:], self.row_a[:], self.rst[:, 2:3],
                self.rst[:, 3:4], Alu.subtract, Alu.mult)
        self.tt("vector", self.row_b[:], self.row_b[:], g_row, Alu.mult)
        self.tt("vector", self.row_b[:], self.row_b[:], b_row, Alu.add)
        self.ts("vector", self.row16[:], self.row_b[:],
                self.pmask_sb[0:1, 0:1], None, Alu.mult)
        vD = self.cV.n
        self.use_bank(6)
        self.wait("tensor", self.cV, vD)
        for g in range(NG):
            self.tr(self.pb16[6][:, g * 64:g * 64 + 1],
                    self.row16[0:1, g * 128:(g + 1) * 128],
                    self.ident16[0:1, 0:1])
        tD = self.cT.n
        self.wait("vector", self.cT, tD)
        for g in range(NG):
            self.vcopy(dst_bnd[:, g, 0:1], self.pb16[6][:, g * 64:g * 64 + 1])
        self.guard_bank(6, "vector")

    def p1a_and_exchange(self):
        C, cv = self, self.cvr
        nc = self.nc
        with ExitStack() as ph:
            psb = lambda n, s, d=F32: ph.enter_context(nc.sbuf_tensor(n, s, d))
            wk_sb = psb("wk_sb", [128, NG, D], BF16)
            wv_sb = psb("wv_sb", [128, NG, D], BF16)
            wr_sb = psb("wr_sb", [128, NG, D], BF16)
            x_tm = psb("x_tm", [128, NSUB, D])
            u0_tm = psb("u0_tm", [128, NSUB, D])
            st = psb("st", [128, NSUB, 10])
            h1_fm = psb("h1_fm", [128, NG, TT + 1], BF16)
            xmix = psb("xmix", [128, NG, TT], BF16)
            ek = psb("ek", [128, NG, TT])
            ekv = psb("ekv", [128, NG, TT])
            q_arr = psb("q_arr", [128, NG, TT + 1])
            p_arr = psb("p_arr", [128, NG, TT + 1])
            h1_tm = x_tm.ap().bitcast(BF16)[:, :, 0:D]
            r4 = x_tm.ap().bitcast(BF16)[:, :, D:2 * D].rearrange(
                "p a (g2 t) -> p a g2 t", g2=2)
            r16 = lambda oc: r4[:, oc // 2, oc % 2, :]
            diff = u0_tm.ap().bitcast(BF16).rearrange(
                "p a b -> p (a b)")[:, 0:NG * TT].rearrange(
                "p (g t) -> p g t", g=NG)
            flat = lambda a: a.ap().rearrange("p a b -> p (a b)")

            gw16 = psb("gw16", [NE, D], BF16)
            for dst, src in ((wk_sb, self.w_wk), (wv_sb, self.w_wv),
                             (wr_sb, self.w_wr)):
                self.dma("gpsimd", dst.ap(),
                         src.rearrange("(ic p) o -> p ic o", p=128))
            self.dma("gpsimd", gw16[:], self.w_gw)
            wLD = self.dG.n
            self.build_gwT(gw16)
            self.p0_staging()

            self.wait("scalar", self.dG, wLD)
            self.wait("vector", self.dG, wLD)
            self.wait("vector", self.dS, self.dS_consts)
            self.row_ln(self.bnd_h1, self.row_gx[:], self.row_bx[:])
            self.memset(q_arr[:, :, TT:TT + 1], 0.0)
            self.memset(p_arr[:, :, TT:TT + 1], 0.0)

            u0_spill = [0] * NT
            nd_spill = [0] * NT
            r_spill = [0] * NT
            h1T_mark = [0] * NT
            mixv_mark = [0] * NT
            h1evac_mark = [0] * NT

            def front(t):
                rows = self.xs[t * TT:(t + 1) * TT].rearrange(
                    "(s p) h -> p s h", p=128)
                if t > 0:
                    self.wait("scalar", self.cT, h1T_mark[t - 1])
                    self.wait("scalar", self.dA, r_spill[t - 1])
                xm = self.dma("scalar", x_tm[:], rows)
                self.wait("scalar", self.dA, xm)
                if t > 0:
                    self.wait("scalar", self.dA, u0_spill[t - 1])
                    self.wait("scalar", self.cV, mixv_mark[t - 1])
                for s in range(NSUB):
                    self.act(u0_tm[:, s, :], x_tm[:, s, :], Act.Identity,
                             accum=st[:, s, 0:1])
                    self.act(u0_tm[:, s, :], x_tm[:, s, :], Act.Square,
                             accum=st[:, s, 1:2])
                sStat = self.cS.n
                self.wait("vector", self.cS, sStat)
                self.ln_finish(st[:, :, 0], st[:, :, 1], st[:, :, 2],
                               st[:, :, 3], st[:, :, 4])
                sFin = self.cS.n
                self.wait("vector", self.cS, sFin)
                for s in range(NSUB):
                    self.ts("vector", u0_tm[:, s, :], x_tm[:, s, :],
                            st[:, s, 2:3], st[:, s, 3:4], Alu.subtract, Alu.mult)
                vU0 = self.cV.n
                self.wait("scalar", self.cV, vU0)
                u0_spill[t] = self.dma(
                    "scalar", self.s_u0[t * TT:(t + 1) * TT].rearrange(
                        "(s p) h -> p s h", p=128), u0_tm[:])
                for s in range(NSUB):
                    self.act(x_tm[:, s, :], u0_tm[:, s, :], Act.Identity,
                             accum=st[:, s, 5:6])
                    self.act(x_tm[:, s, :], u0_tm[:, s, :], Act.Square,
                             accum=st[:, s, 6:7])
                sStat2 = self.cS.n
                self.wait("vector", self.cS, sStat2)
                self.ln_finish(st[:, :, 5], st[:, :, 6], st[:, :, 7],
                               st[:, :, 8], st[:, :, 9])
                sFin2 = self.cS.n
                self.wait("vector", self.cS, sFin2)
                for s in range(NSUB):
                    self.ts("vector", h1_tm[:, s, :], u0_tm[:, s, :],
                            st[:, s, 7:8], st[:, s, 8:9], Alu.subtract, Alu.mult)
                if t == 0:
                    self.vcopy(h1_fm[:, :, 0:1], self.bnd_h1[:])
                else:
                    self.wait("vector", self.cS, h1evac_mark[t - 1])
                    self.vcopy(h1_fm[:, :, 0:1], h1_fm[:, :, TT:TT + 1])
                self.wait("tensor", self.cV, self.cV.n)
                for s in range(NSUB):
                    self.use_bank(4 + s)
                for s in range(NSUB):
                    for g in range(NG):
                        self.tr(self.pb16[4 + s][:, g * 128:(g + 1) * 128],
                                h1_tm[:, s, g * 128:(g + 1) * 128],
                                self.ident16[:])
                h1T_mark[t] = self.cT.n
                self.wait("scalar", self.cT, h1T_mark[t])
                for s in range(NSUB):
                    for g in range(NG):
                        self.act(h1_fm[:, g, 1 + s * 128:1 + (s + 1) * 128],
                                 self.pb16[4 + s][:, g * 128:(g + 1) * 128],
                                 Act.Identity, scale=cv[:, C.C_LAG, g:g + 1],
                                 bias=cv[:, C.C_LAB, g:g + 1])
                h1evac_mark[t] = self.cS.n
                for s in range(NSUB):
                    self.guard_bank(4 + s, "scalar")

            def mix_into(col, t):
                self.wait("vector", self.cT, self._xmix_use)
                for g in range(NG):
                    self.stt("vector", xmix[:, g, :], diff[:, g, :],
                             cv[:, col, g:g + 1], h1_fm[:, g, 0:TT],
                             Alu.mult, Alu.add)
                self.wait("tensor", self.cV, self.cV.n)

            def proj(wsb, base, t):
                marks = []
                for oc in range(NG):
                    bank = (base + oc) % 3
                    self.use_bank(bank)
                    for ic in range(NG):
                        self.mm(self.pb[bank][:, 0:TT],
                                wsb[:, ic, oc * 128:(oc + 1) * 128],
                                xmix[:, ic, :], ic == 0, ic == NG - 1)
                    marks.append((bank, self.cT.n))
                self._xmix_use = self.cT.n
                return marks

            self._xmix_use = 0
            front(0)
            for t in range(NT):
                self.wait("vector", self.dA, u0_spill[t])
                self.wait("vector", self.cS, h1evac_mark[t])
                if t > 0:
                    self.wait("vector", self.dA, nd_spill[t - 1])
                self.tt("vector", diff, h1_fm[:, :, 1:TT + 1],
                        h1_fm[:, :, 0:TT], Alu.subtract)
                if t == 0:
                    self.wait("tensor", self.dG, wLD)
                if t > 0:
                    self.wait("scalar", self.dA, nd_spill[t - 1])
                mix_into(C.C_MR, t)
                for oc, (bank, m) in enumerate(proj(wr_sb, 0, t)):
                    self.wait("scalar", self.cT, m)
                    self.act(r16(oc), self.pb[bank][:, 0:TT],
                             Act.Tanh, scale=0.5)
                    self.guard_bank(bank, "scalar")
                sR = self.cS.n
                self.wait("vector", self.cS, sR)
                self.ts("vector", r4, r4, 0.5, 0.5, Alu.mult, Alu.add)
                vR = self.cV.n
                self.wait("scalar", self.cV, vR)
                r_spill[t] = self.dma("scalar", self.s_r[t], r4)
                mix_into(C.C_MK, t)
                ek_marks = []
                for oc, (bank, m) in enumerate(proj(wk_sb, 2, t)):
                    self.wait("scalar", self.cT, m)
                    self.act(ek[:, oc, :], self.pb[bank][:, 0:TT], Act.Exp)
                    self.guard_bank(bank, "scalar")
                    ek_marks.append(self.cS.n)
                mix_into(C.C_MV, t)
                mixv_mark[t] = self.cV.n
                for oc, (bank, m) in enumerate(proj(wv_sb, 1, t)):
                    self.wait("vector", self.cT, m)
                    self.wait("vector", self.cS, ek_marks[oc])
                    self.tt("vector", ekv[:, oc, :], self.pb[bank][:, 0:TT],
                            ek[:, oc, :], Alu.mult)
                    self.guard_bank(bank, "vector")
                if t + 1 < NT:
                    front(t + 1)
                for g in range(NG):
                    self.vcopy(q_arr[:, g, 0:1], q_arr[:, g, TT:TT + 1])
                    self.vcopy(p_arr[:, g, 0:1], p_arr[:, g, TT:TT + 1])
                for g in range(NG):
                    self.sched("vector", lambda e, g=g: e.tensor_tensor_scan(
                        out=q_arr[:, g, 1:TT + 1],
                        data0=cv[:, C.C_EW, g:g + 1].to_broadcast([128, TT]),
                        data1=ek[:, g, :], initial=q_arr[:, g, 0:1],
                        op0=Alu.mult, op1=Alu.add), inc=self.cV)
                    self.sched("vector", lambda e, g=g: e.tensor_tensor_scan(
                        out=p_arr[:, g, 1:TT + 1],
                        data0=cv[:, C.C_EW, g:g + 1].to_broadcast([128, TT]),
                        data1=ekv[:, g, :], initial=p_arr[:, g, 0:1],
                        op0=Alu.mult, op1=Alu.add), inc=self.cV)
                for g in range(NG):
                    self.stt("vector", ekv[:, g, :], ekv[:, g, :],
                             cv[:, C.C_EU, g:g + 1], p_arr[:, g, 0:TT],
                             Alu.mult, Alu.add)
                    self.stt("vector", ek[:, g, :], ek[:, g, :],
                             cv[:, C.C_EU, g:g + 1], q_arr[:, g, 0:TT],
                             Alu.mult, Alu.add)
                vND = self.cV.n
                self.wait("scalar", self.cV, vND)
                self.dma("scalar", self.s_num[t], flat(ekv))
                nd_spill[t] = self.dma("scalar", self.s_den[t], flat(ek))

            self.vcopy(self.pq_j[:, 0, 0:NG], p_arr[:, :, TT])
            self.vcopy(self.pq_j[:, 0, NG:16], q_arr[:, :, TT])
            vPack = self.cV.n
            self.wait("gpsimd", self.cV, vPack)
            self.dma("gpsimd", self.cc1_in[0].rearrange("(p f) -> p f", p=128),
                     self.pq_j[:, 0, :])
            self.wait("gpsimd", self.dG)
            self.sched("gpsimd", lambda e: e.collective_compute(
                "AllGather", Alu.bypass, replica_groups=RG,
                ins=[self.cc1_in], outs=[self.cc1_out]), inc=self.cG)
            self.ag1_done = self.cG.n

    def exchange1_consume(self):
        C, cv = self, self.cvr
        self.wait("gpsimd", self.cG, self.ag1_done)
        for j in range(8):
            self.dma("gpsimd", self.pq_j[:, j, :],
                     self.cc1_out[j].rearrange("(p f) -> p f", p=128))
        ccLD = self.dG.n
        self.wait("scalar", self.dG, ccLD)
        for j in range(8):
            self.act(self.coefs[:, :, j], cv[:, C.C_W, :], Act.Exp,
                     scale=self.a8bc[:, j:j + 1])
        sCoef = self.cS.n
        self.wait("vector", self.cS, sCoef)
        self.wait("vector", self.dG, ccLD)
        self.memset(cv[:, C.C_PIN, :], 0.0)
        self.memset(cv[:, C.C_QIN, :], 0.0)
        self.vsync()
        for j in range(8):
            self.tt("vector", cv[:, C.C_TMP, :], self.coefs[:, :, j],
                    self.pq_j[:, j, 0:NG], Alu.mult)
            self.vsync()
            self.tt("vector", cv[:, C.C_PIN, :], cv[:, C.C_PIN, :],
                    cv[:, C.C_TMP, :], Alu.add)
            self.vsync()
            self.tt("vector", cv[:, C.C_TMP, :], self.coefs[:, :, j],
                    self.pq_j[:, j, NG:16], Alu.mult)
            self.vsync()
            self.tt("vector", cv[:, C.C_QIN, :], cv[:, C.C_QIN, :],
                    cv[:, C.C_TMP, :], Alu.add)
            self.vsync()

    def p1b(self):
        C, cv = self, self.cvr
        nc = self.nc
        with ExitStack() as ph:
            psb = lambda n, s, d=F32: ph.enter_context(nc.sbuf_tensor(n, s, d))
            wo_sb = psb("wo_sb", [128, NG, D], BF16)
            nbufs = [psb("nbuf0", [128, NG, TT])] * 2
            dbufs = [psb("dbuf0", [128, NG, TT])] * 2
            rbufs = [psb("rbuf0", [128, NG, TT], BF16)] * 2
            u0_tms = [psb("u0b_tm0", [128, NSUB, D])] * 2
            Dfm = psb("Dfm", [128, NG, TT])
            a_fm = psb("a_fm", [128, NG, TT])
            wkv16 = Dfm.ap().bitcast(BF16)[:, :, 0:TT]
            xb_tm = psb("xb_tm", [128, NSUB, D])
            flat = lambda a: a.ap().rearrange("p a b -> p (a b)")

            self.wait("gpsimd", self.cT)
            self.wait("gpsimd", self.cS)
            self.dma("gpsimd", wo_sb.ap(),
                     self.w_wo.rearrange("(ic p) o -> p ic o", p=128))
            wLD = self.dG.n

            consume_mark = [0] * NT
            aT_mark = [0] * NT
            wo_mm_mark = [0] * NT
            order = [2, 3, 4, 5, 6, 7, 0, 1]
            prev = -1
            for t in order:
                if t == 0:
                    self.exchange1_consume()
                nbuf, dbuf = nbufs[t % 2], dbufs[t % 2]
                rbuf, u0_tm = rbufs[t % 2], u0_tms[t % 2]
                if prev >= 0:
                    self.wait("scalar", self.cT, wo_mm_mark[prev])
                    self.wait("scalar", self.cV, consume_mark[prev])
                m1 = self.dma("scalar", flat(nbuf), self.s_num[t])
                m2 = self.dma("scalar", flat(dbuf), self.s_den[t])
                if prev >= 0:
                    self.wait("scalar", self.cV, consume_mark[prev])
                m3 = self.dma("scalar", flat(rbuf), self.s_r[t])
                if prev >= 0:
                    self.wait("scalar", self.cV, aT_mark[prev])
                m4 = self.dma("scalar", u0_tm[:],
                              self.s_u0[t * TT:(t + 1) * TT].rearrange(
                                  "(s p) h -> p s h", p=128))
                self.wait("vector", self.dA, m2)
                if t < 2:
                    self.ts("vector", cv[:, C.C_WTB, :], cv[:, C.C_W, :],
                            float(t * TT), None, Alu.mult)
                    vWT = self.cV.n
                    self.wait("scalar", self.cV, vWT)
                    for g in range(NG):
                        self.act(Dfm[:, g, :], self.ramp01[:], Act.Exp,
                                 scale=cv[:, C.C_W, g:g + 1],
                                 bias=cv[:, C.C_WTB, g:g + 1])
                    sD = self.cS.n
                    self.wait("vector", self.cS, sD)
                    for g in range(NG):
                        self.stt("vector", nbuf[:, g, :], Dfm[:, g, :],
                                 cv[:, C.C_PIN, g:g + 1], nbuf[:, g, :],
                                 Alu.mult, Alu.add)
                        self.stt("vector", dbuf[:, g, :], Dfm[:, g, :],
                                 cv[:, C.C_QIN, g:g + 1], dbuf[:, g, :],
                                 Alu.mult, Alu.add)
                self.sched("vector", lambda e: e.reciprocal(
                    out=flat(dbuf), in_=flat(dbuf)), inc=self.cV)
                self.tt("vector", wkv16, nbuf.ap(), dbuf.ap(), Alu.mult)
                self.wait("vector", self.dA, m3)
                self.tt("vector", wkv16, wkv16, rbuf.ap(), Alu.mult)
                consume_mark[t] = self.cV.n

                self.wait("tensor", self.cV, consume_mark[t])
                if t == 0:
                    self.wait("tensor", self.dG, wLD)
                for oc in range(NG):
                    bank = oc % 3
                    self.use_bank(bank)
                    for ic in range(NG):
                        self.mm(self.pb[bank][:, 0:TT],
                                wo_sb[:, ic, oc * 128:(oc + 1) * 128],
                                wkv16[:, ic, :], ic == 0, ic == NG - 1)
                    tMM = self.cT.n
                    self.wait("vector", self.cT, tMM)
                    self.vcopy(a_fm[:, oc, :], self.pb[bank][:, 0:TT])
                    self.guard_bank(bank, "vector")
                vA = self.cV.n
                wo_mm_mark[t] = self.cT.n

                self.wait("tensor", self.cV, vA)
                self.wait("scalar", self.dA, m4)
                for s in range(NSUB):
                    b0, b1 = 4 + 2 * (s % 2), 5 + 2 * (s % 2)
                    self.use_bank(b0)
                    self.use_bank(b1)
                    for g in range(NG):
                        bank = b0 if g < 4 else b1
                        self.tr(self.pb[bank].ap()[:, (g % 4) * 128:
                                                   (g % 4) * 128 + 128],
                                a_fm[:, g, s * 128:(s + 1) * 128],
                                self.ident32[:])
                    tS = self.cT.n
                    self.wait("vector", self.cT, tS)
                    self.tt("vector", xb_tm[:, s, 0:512], self.pb[b0][:],
                            u0_tm[:, s, 0:512], Alu.add)
                    self.tt("vector", xb_tm[:, s, 512:1024], self.pb[b1][:],
                            u0_tm[:, s, 512:1024], Alu.add)
                    self.guard_bank(b0, "vector")
                    self.guard_bank(b1, "vector")
                aT_mark[t] = self.cV.n
                self.wait("scalar", self.cV, aT_mark[t])
                self.dma("scalar", self.s_xb[t * TT:(t + 1) * TT].rearrange(
                    "(s p) h -> p s h", p=128), xb_tm[:])
                if t == NT - 1:
                    self.wait("gpsimd", self.cV, aT_mark[t])
                    self.dma("gpsimd", self.cc2_in, xb_tm[127:128, 3, :])
                    self.wait("gpsimd", self.dG)
                    self.sched("gpsimd", lambda e: e.collective_compute(
                        "AllGather", Alu.bypass, replica_groups=RG,
                        ins=[self.cc2_in], outs=[self.cc2_out]), inc=self.cG)
                    self.ag2_done = self.cG.n
                prev = t

    def p2a(self):
        C, cv = self, self.cvr
        nc = self.nc
        with ExitStack() as ph:
            psb = lambda n, s, d=F32: ph.enter_context(nc.sbuf_tensor(n, s, d))
            fwk_sb = psb("fwk_sb", [128, NG, 4 * D], BF16)
            fwr_sb = psb("fwr_sb", [128, NG, D], BF16)
            xb_tm = psb("xb2_tm", [128, NSUB, D])
            st = psb("st2", [128, NSUB, 6])
            h2_fm = psb("h2_fm", [128, NG, TT + 1], BF16)
            kf16 = psb("kf16", [128, 32, TT], BF16)
            xk2 = psb("xk2", [128, NG, TT], BF16)
            xr2 = psb("xr2", [128, NG, TT], BF16)
            ktmp = psb("ktmp", [128, 2, TT])
            cc2_sb = psb("cc2_sb", [8, D])
            h2_tm = kf16.ap()[:, 8:16, :].rearrange(
                "p (a two) t -> p a (two t)", two=2)
            rf4 = xb_tm.ap().bitcast(BF16)[:, :, D:2 * D].rearrange(
                "p a (g2 t) -> p a g2 t", g2=2)
            rf16 = lambda oc: rf4[:, oc // 2, oc % 2, :]
            diff = kf16.ap()[:, 0:NG, 0:TT]
            flat = lambda a: a.ap().rearrange("p a b -> p (a b)")

            self.wait("gpsimd", self.dG, self.ccw_store_mark)
            self.sched("gpsimd", lambda e: e.collective_compute(
                "AllGather", Alu.bypass, replica_groups=RG,
                ins=[self.ccw_in], outs=[self.ccw_out]), inc=self.cG)
            self.agw_done = self.cG.n
            self.dma("gpsimd", fwk_sb.ap(),
                     self.w_fwk.rearrange("(ic p) o -> p ic o", p=128))
            self.dma("gpsimd", fwr_sb.ap(),
                     self.w_fwr.rearrange("(ic p) o -> p ic o", p=128))
            wLD = self.dG.n
            self.dma("gpsimd", self.row_gx[:], self.wvec["ln_ffn_g"])
            self.dma("gpsimd", self.row_bx[:], self.wvec["ln_ffn_b"])
            rowLD = self.dG.n

            self.wait("gpsimd", self.cG, self.ag2_done)
            ccm = self.dma("gpsimd", cc2_sb[:], self.cc2_out)
            self.wait("tensor", self.dG, ccm)
            self.use_bank(6)
            self.use_bank(7)
            self.mm(self.pb[6][0:1, 0:512], self.sel8_sb[:],
                    cc2_sb[:, 0:512], True, True)
            self.mm(self.pb[7][0:1, 0:512], self.sel8_sb[:],
                    cc2_sb[:, 512:1024], True, True)
            tSel = self.cT.n
            self.wait("vector", self.cT, tSel)
            self.vcopy(self.row_a[0:1, 0:512], self.pb[6][0:1, 0:512])
            self.vcopy(self.row_a[0:1, 512:1024], self.pb[7][0:1, 0:512])
            self.guard_bank(6, "vector")
            self.guard_bank(7, "vector")
            self.wait("vector", self.dG, rowLD)
            self.row_ln(self.bnd_h2, self.row_gx[:], self.row_bx[:])

            kf_spill = [0] * NT
            h2T_mark = [0] * NT
            h2evac_marks = [0] * NT
            mm_mark = [0] * NT
            for t in range(NT):
                if t > 0:
                    self.wait("scalar", self.cT, h2T_mark[t - 1])
                    self.wait("scalar", self.dA, kf_spill[t - 1])
                xm = self.dma("scalar", xb_tm[:],
                              self.s_xb[t * TT:(t + 1) * TT].rearrange(
                                  "(s p) h -> p s h", p=128))
                self.wait("scalar", self.dA, xm)
                kdump = ktmp.ap().rearrange("p a b -> p (a b)")
                for s in range(NSUB):
                    self.act(kdump, xb_tm[:, s, :],
                             Act.Identity, accum=st[:, s, 0:1])
                    self.act(kdump, xb_tm[:, s, :],
                             Act.Square, accum=st[:, s, 1:2])
                sStat = self.cS.n
                self.wait("vector", self.cS, sStat)
                self.ln_finish(st[:, :, 0], st[:, :, 1], st[:, :, 2],
                               st[:, :, 3], st[:, :, 4])
                sFin = self.cS.n
                self.wait("vector", self.cS, sFin)
                if t > 0:
                    self.wait("vector", self.dA, kf_spill[t - 1])
                for s in range(NSUB):
                    self.ts("vector", h2_tm[:, s, :], xb_tm[:, s, :],
                            st[:, s, 2:3], st[:, s, 3:4], Alu.subtract, Alu.mult)
                vH2 = self.cV.n
                if t == 0:
                    self.vcopy(h2_fm[:, :, 0:1], self.bnd_h2[:])
                else:
                    self.wait("vector", self.cS, h2evac_marks[t - 1])
                    self.vcopy(h2_fm[:, :, 0:1], h2_fm[:, :, TT:TT + 1])
                self.wait("tensor", self.cV, self.cV.n)
                for s in range(NSUB):
                    self.use_bank(4 + s)
                for s in range(NSUB):
                    for g in range(NG):
                        self.tr(self.pb16[4 + s][:, g * 128:(g + 1) * 128],
                                h2_tm[:, s, g * 128:(g + 1) * 128],
                                self.ident16[:])
                h2T_mark[t] = self.cT.n
                self.wait("scalar", self.cT, h2T_mark[t])
                for s in range(NSUB):
                    for g in range(NG):
                        self.act(h2_fm[:, g, 1 + s * 128:1 + (s + 1) * 128],
                                 self.pb16[4 + s][:, g * 128:(g + 1) * 128],
                                 Act.Identity, scale=cv[:, C.C_LFG, g:g + 1],
                                 bias=cv[:, C.C_LFB, g:g + 1])
                h2evac_marks[t] = self.cS.n
                for s in range(NSUB):
                    self.guard_bank(4 + s, "scalar")

                if t > 0:
                    self.wait("vector", self.dA, kf_spill[t - 1])
                self.wait("vector", self.cS, h2evac_marks[t])
                self.tt("vector", diff, h2_fm[:, :, 1:TT + 1],
                        h2_fm[:, :, 0:TT], Alu.subtract)
                if t > 0:
                    self.wait("vector", self.cT, mm_mark[t - 1])
                for dst, col in ((xk2, C.C_FMK), (xr2, C.C_FMR)):
                    for g in range(NG):
                        self.stt("vector", dst[:, g, :], diff[:, g, :],
                                 cv[:, col, g:g + 1], h2_fm[:, g, 0:TT],
                                 Alu.mult, Alu.add)
                vMix = self.cV.n
                if "h2" in self.taps:
                    self.wait("scalar", self.cV, vMix)
                    self.dma("scalar",
                             self.s_h2[t].rearrange("p (g c) -> p g c", g=NG),
                             h2_fm[:, :, 1:TT + 1])
                    self.dma("scalar", self.s_xk2[t], flat(xk2))

                self.wait("tensor", self.cV, vMix)
                if t == 0:
                    self.wait("tensor", self.dG, wLD)
                kf_sq_mark = [0, 0]
                for oc in range(32):
                    bank = oc % 3
                    self.use_bank(bank)
                    for ic in range(NG):
                        self.mm(self.pb[bank][:, 0:TT],
                                fwk_sb[:, ic, oc * 128:(oc + 1) * 128],
                                xk2[:, ic, :], ic == 0, ic == NG - 1)
                    tMM = self.cT.n
                    self.wait("scalar", self.cT, tMM)
                    slot = oc % 2
                    self.act(ktmp[:, slot, :], self.pb[bank][:, 0:TT], Act.Relu)
                    self.guard_bank(bank, "scalar")
                    self.act(kf16[:, oc, :], ktmp[:, slot, :], Act.Square)
                    kf_sq_mark[slot] = self.cS.n
                for oc in range(NG):
                    bank = oc % 3
                    self.use_bank(bank)
                    for ic in range(NG):
                        self.mm(self.pb[bank][:, 0:TT],
                                fwr_sb[:, ic, oc * 128:(oc + 1) * 128],
                                xr2[:, ic, :], ic == 0, ic == NG - 1)
                    tMM = self.cT.n
                    self.wait("scalar", self.cT, tMM)
                    self.act(rf16(oc), self.pb[bank][:, 0:TT],
                             Act.Tanh, scale=0.5)
                    self.guard_bank(bank, "scalar")
                mm_mark[t] = self.cT.n
                sRF = self.cS.n
                self.wait("vector", self.cS, sRF)
                self.ts("vector", rf4, rf4, 0.5, 0.5, Alu.mult, Alu.add)
                vRF = self.cV.n
                self.wait("scalar", self.cV, vRF)
                self.dma("scalar", self.s_kf[t], flat(kf16))
                kf_spill[t] = self.dma("scalar", self.s_rf[t], rf4)

    def p2b(self):
        C, cv = self, self.cvr
        nc = self.nc
        with ExitStack() as ph:
            psb = lambda n, s, d=F32: ph.enter_context(nc.sbuf_tensor(n, s, d))
            fwv_sb = psb("fwv_sb", [128, 32, D], BF16)
            kf16 = psb("kf16b", [128, 32, TT], BF16)
            xb_tm = psb("xb3_tm", [128, NSUB, D])
            x0_tm = psb("x0_tm", [128, NSUB, D])
            f_fm = psb("f_fm", [128, NG, TT])
            x1_tm = psb("x1_tm", [128, NSUB, D])
            st = psb("st3", [128, NSUB, 6])
            u1_tm = x0_tm.ap().bitcast(BF16)[:, :, 0:D]
            rf4 = x1_tm.ap().bitcast(BF16)[:, :, 0:D].rearrange(
                "p a (g2 t) -> p a g2 t", g2=2)
            rf16 = lambda oc: rf4[:, oc // 2, oc % 2, :]
            u1buf = f_fm.ap().bitcast(BF16)[:, 4:8, 0:2 * TT].rearrange(
                "p a (g2 t) -> p (a g2) t", g2=2)
            flat = lambda a: a.ap().rearrange("p a b -> p (a b)")

            self.dma("gpsimd", fwv_sb.ap(),
                     self.w_fwv.rearrange("(ic p) o -> p ic o", p=128))
            wLD = self.dG.n

            ld_mark = [0] * NT
            u1T_mark = [0] * NT
            u1_spill = [0] * NT
            mm_mark = [0] * NT
            asm_mark = [0] * NT
            for t in range(NT):
                if t > 0:
                    self.wait("scalar", self.cT, mm_mark[t - 1])
                m1 = self.dma("scalar", flat(kf16), self.s_kf[t])
                if t > 0:
                    self.wait("scalar", self.cV, ld_mark[t - 1])
                m2 = self.dma("scalar", rf4, self.s_rf[t])
                if t > 0:
                    self.wait("scalar", self.cV, asm_mark[t - 1])
                m3 = self.dma("scalar", xb_tm[:],
                              self.s_xb[t * TT:(t + 1) * TT].rearrange(
                                  "(s p) h -> p s h", p=128))
                if t > 0:
                    self.wait("scalar", self.cT, u1T_mark[t - 1])
                m4 = self.dma("scalar", x0_tm[:],
                              self.xs[t * TT:(t + 1) * TT].rearrange(
                                  "(s p) h -> p s h", p=128))
                self.wait("tensor", self.dA, m1)
                if t == 0:
                    self.wait("tensor", self.dG, wLD)
                if t > 0:
                    self.wait("vector", self.dA, u1_spill[t - 1])
                for oc in range(NG):
                    bank = oc % 3
                    self.use_bank(bank)
                    for ic in range(32):
                        self.mm(self.pb[bank][:, 0:TT],
                                fwv_sb[:, ic, oc * 128:(oc + 1) * 128],
                                kf16[:, ic, :], ic == 0, ic == 31)
                    tMM = self.cT.n
                    self.wait("vector", self.cT, tMM)
                    if oc == 0:
                        self.wait("vector", self.dA, m2)
                    self.tt("vector", f_fm[:, oc, :], self.pb[bank][:, 0:TT],
                            rf16(oc), Alu.mult)
                    self.guard_bank(bank, "vector")
                vF = self.cV.n
                mm_mark[t] = self.cT.n

                self.wait("tensor", self.cV, vF)
                self.wait("vector", self.dA, m3)
                for s in range(NSUB):
                    b0, b1 = 4 + 2 * (s % 2), 5 + 2 * (s % 2)
                    self.use_bank(b0)
                    self.use_bank(b1)
                    for g in range(NG):
                        bank = b0 if g < 4 else b1
                        self.tr(self.pb[bank].ap()[:, (g % 4) * 128:
                                                   (g % 4) * 128 + 128],
                                f_fm[:, g, s * 128:(s + 1) * 128],
                                self.ident32[:])
                    tS = self.cT.n
                    self.wait("vector", self.cT, tS)
                    self.tt("vector", x1_tm[:, s, 0:512], self.pb[b0][:],
                            xb_tm[:, s, 0:512], Alu.add)
                    self.tt("vector", x1_tm[:, s, 512:1024], self.pb[b1][:],
                            xb_tm[:, s, 512:1024], Alu.add)
                    self.guard_bank(b0, "vector")
                    self.guard_bank(b1, "vector")
                self.wait("vector", self.dA, m4)
                self.tt("vector", flat(x1_tm), flat(x1_tm), flat(x0_tm), Alu.add)
                vX1 = self.cV.n
                asm_mark[t] = vX1
                self.wait("scalar", self.cV, vX1)
                self.dma("scalar", self.s_x1[t * TT:(t + 1) * TT].rearrange(
                    "(s p) h -> p s h", p=128), x1_tm[:])

                fdump = f_fm.ap()[:, 0:2, :].rearrange("p a b -> p (a b)")
                for s in range(NSUB):
                    self.act(fdump, x1_tm[:, s, :], Act.Identity,
                             accum=st[:, s, 0:1])
                    self.act(fdump, x1_tm[:, s, :], Act.Square,
                             accum=st[:, s, 1:2])
                sStat = self.cS.n
                self.wait("vector", self.cS, sStat)
                self.ln_finish(st[:, :, 0], st[:, :, 1], st[:, :, 2],
                               st[:, :, 3], st[:, :, 4])
                sFin = self.cS.n
                self.wait("vector", self.cS, sFin)
                for s in range(NSUB):
                    self.ts("vector", u1_tm[:, s, :], x1_tm[:, s, :],
                            st[:, s, 2:3], st[:, s, 3:4], Alu.subtract, Alu.mult)
                vU1 = self.cV.n
                ld_mark[t] = vU1
                self.wait("tensor", self.cV, vU1)
                for s in range(NSUB):
                    self.use_bank(4 + s)
                for s in range(NSUB):
                    for g in range(NG):
                        self.tr(self.pb16[4 + s][:, g * 128:(g + 1) * 128],
                                u1_tm[:, s, g * 128:(g + 1) * 128],
                                self.ident16[:])
                u1T_mark[t] = self.cT.n
                self.wait("vector", self.cT, u1T_mark[t])
                for s in range(NSUB):
                    self.vcopy(
                        u1buf[:, :, s * 128:(s + 1) * 128],
                        self.pb16[4 + s].rearrange("p (g c) -> p g c", g=NG))
                    self.guard_bank(4 + s, "vector")
                vU1e = self.cV.n
                self.wait("scalar", self.cV, vU1e)
                u1_spill[t] = self.dma("scalar", self.s_u1[t],
                                       u1buf.rearrange("p g t -> p (g t)"))

    def p3(self):
        C, cv = self, self.cvr
        nc = self.nc
        with ExitStack() as ph:
            psb = lambda n, s, d=F32: ph.enter_context(nc.sbuf_tensor(n, s, d))
            shg = psb("shg", [128, NG, D], BF16)
            shu = psb("shu", [128, NG, D], BF16)
            shd = shg
            u1q = psb("u1q", [128, NG, QTR], BF16)
            acc = psb("acc", [128, NG, QTR])
            gssh = psb("gssh", [128, NG, QTR], BF16)
            wge = psb("wge0", [128, 2, NG, DI], BF16)
            wue = psb("wue0", [128, 2, NG, DI], BF16)
            wde = psb("wde0", [128, 4, D], BF16)
            gs16 = psb("gs16", [128, 4, QTR], BF16)
            wb_sb = psb("wb_sb", [128, QTR])
            sc_all = psb("sc_all", [128, 8, NE])
            wmat = psb("wmat", [128, 8, NE])
            wfm16 = psb("wfm16", [32, QTR], BF16)
            wrow = psb("wrow", [1, 2, QTR], BF16)
            gsum8 = psb("gsum8", [128, 8])
            m8 = psb("m8", [128, 8])
            msk = psb("msk", [128, NE])
            stmp = psb("stmp", [128, 8, TT], BF16)
            x1_tm = psb("x1b_tm", [128, NSUB, D])
            flat = lambda a: a.ap().rearrange("p a b -> p (a b)")
            self.memset(gsum8[:, 4:8], -1e30)
            self.wait("sync", self.cG, self.agw_done)
            self.wait("gpsimd", self.cG, self.agw_done)

            NITER = NQ * NE
            ld_gu = [0] * NITER
            ld_d = [0] * NITER
            a_done_p = [0, 0]
            d_done_last = [0]
            wflat = lambda a, par: a.ap()[:, par].rearrange("p a b -> p (a b)")

            def prefetch_gu(j):
                par, e = j % 2, j % NE
                self.wait("sync", self.cT, a_done_p[par])
                for part, dst in ((0, wge), (1, wue)):
                    src = self.ccw_out[e // EPL,
                                       ((e % EPL) * 3 + part) * EMAT:
                                       ((e % EPL) * 3 + part + 1) * EMAT]
                    ld_gu[j] = self.dma("sync", wflat(dst, par),
                                        src.rearrange("(p f) -> p f", p=128))

            def prefetch_d(j):
                e = j % NE
                self.wait("sync", self.cT, d_done_last[0])
                srcd = self.ccw_out[e // EPL,
                                    ((e % EPL) * 3 + 2) * EMAT:
                                    ((e % EPL) * 3 + 3) * EMAT]
                ld_d[j] = self.dma("sync", flat(wde),
                                   srcd.rearrange("(p f) -> p f", p=128))

            prefetch_gu(0)
            prefetch_d(0)

            self._d_mms = 0
            self._gs_read = [0, 0]
            self._shg_use = 0
            self._wrow_use = 0
            u1q_use = 0
            ylast = 0
            acc_mark = 0
            for q in range(NQ):
                self.wait("sync", self.cT, u1q_use)
                uq = self.dma("sync", u1q[:, :, 0:TT], self.s_u1[2 * q])
                uq = self.dma("sync", u1q[:, :, TT:QTR], self.s_u1[2 * q + 1])
                self.wait("tensor", self.dS, uq)

                for i in range(8):
                    self.use_bank(3)
                    for ic in range(NG):
                        self.mm(self.pb[3][:, i * 64:i * 64 + NE],
                                u1q[:, ic, i * 128:(i + 1) * 128],
                                self.gwT[:, ic, :], ic == 0, ic == NG - 1)
                tLG = self.cT.n
                self.wait("scalar", self.cT, tLG)
                for i in range(8):
                    self.act(sc_all[:, i, :], self.pb[3][:, i * 64:i * 64 + NE],
                             Act.Tanh, scale=0.5)
                self.guard_bank(3, "scalar")
                sLG = self.cS.n
                self.wait("vector", self.cS, sLG)
                self.ts("vector", flat(sc_all), flat(sc_all), 0.5, 0.5,
                        Alu.mult, Alu.add)
                for i in range(8):
                    sc = sc_all[:, i, :]
                    wm = wmat[:, i, :]
                    self.tt("vector", wm, sc, self.gbias_bc[:], Alu.add)
                    self.vsync()
                    for grp in range(4):
                        self.sched("vector", lambda e, grp=grp, wm=wm: e.max(
                            out=m8[:], in_=wm[:, grp * 8:(grp + 1) * 8]),
                            inc=self.cV)
                        self.vsync()
                        self.tt("vector", gsum8[:, grp:grp + 1], m8[:, 0:1],
                                m8[:, 1:2], Alu.add)
                        self.vsync()
                    self.sched("vector", lambda e: e.max(
                        out=m8[:], in_=gsum8[:]), inc=self.cV)
                    self.vsync()
                    self.sched("vector", lambda e: e.tensor_single_scalar(
                        out=gsum8[:, 0:4], in_=gsum8[:, 0:4],
                        scalar=m8[:, 1:2], op=Alu.is_ge), inc=self.cV)
                    self.vsync()
                    self.tt("vector",
                            msk.ap().rearrange("p (g e) -> p g e", g=4),
                            gsum8.ap()[:, 0:4].unsqueeze(2).to_broadcast(
                                [128, 4, 8]),
                            wm.rearrange("p (g e) -> p g e", g=4), Alu.mult)
                    self.vsync()
                    self.sched("vector", lambda e: e.max(
                        out=m8[:], in_=msk[:]), inc=self.cV)
                    self.vsync()
                    self.sched("vector", lambda e: e.tensor_single_scalar(
                        out=msk[:], in_=msk[:], scalar=m8[:, 7:8],
                        op=Alu.is_ge), inc=self.cV)
                    self.vsync()
                    self.tt("vector", wm, sc, msk[:], Alu.mult)
                    self.vsync()
                    self.sched("vector", lambda e, wm=wm: e.reduce_sum(
                        out=m8[:, 0:1], in_=wm, axis=mybir.AxisListType.X),
                        inc=self.cV)
                    self.vsync()
                    self.ts("vector", m8[:, 0:1], m8[:, 0:1], 1e-20, None,
                            Alu.add)
                    self.vsync()
                    self.sched("vector", lambda e: e.reciprocal(
                        out=m8[:, 1:2], in_=m8[:, 0:1]), inc=self.cV)
                    self.vsync()
                    self.ts("vector", m8[:, 1:2], m8[:, 1:2], 2.5, None,
                            Alu.mult)
                    self.vsync()
                    self.sched("vector", lambda e, wm=wm: e.tensor_single_scalar(
                        out=wm, in_=wm, scalar=m8[:, 1:2], op=Alu.mult),
                        inc=self.cV)
                    self.vsync()
                vGate = self.cV.n
                self.wait("scalar", self.cT, self._shg_use)
                self.dma("scalar", flat(shg), self.s_shT[0])
                shLD = self.dma("scalar", flat(shu), self.s_shT[1])
                self.wait("tensor", self.dA, shLD)
                sh_mark = [0, 0, 0, 0]
                for tk in range(2):
                    for oc in range(NG):
                        b0 = (oc % 2) * 2
                        b1 = b0 + 1
                        self.use_bank(b0)
                        self.use_bank(b1)
                        for ic in range(NG):
                            self.mm(self.pb[b0][:, 0:TT],
                                    shg[:, ic, oc * 128:(oc + 1) * 128],
                                    u1q[:, ic, tk * TT:(tk + 1) * TT],
                                    ic == 0, ic == NG - 1)
                        for ic in range(NG):
                            self.mm(self.pb[b1][:, 0:TT],
                                    shu[:, ic, oc * 128:(oc + 1) * 128],
                                    u1q[:, ic, tk * TT:(tk + 1) * TT],
                                    ic == 0, ic == NG - 1)
                        tMM = self.cT.n
                        self.wait("scalar", self.cT, tMM)
                        self.wait("scalar", self.cV, sh_mark[oc % 4])
                        self.act(stmp[:, oc % 4, :], self.pb[b0][:, 0:TT],
                                 Act.Silu)
                        self.guard_bank(b0, "scalar")
                        sSi = self.cS.n
                        self.wait("vector", self.cS, sSi)
                        self.tt("vector", stmp[:, 4 + oc % 4, :],
                                self.pb[b1][:, 0:TT], stmp[:, oc % 4, :],
                                Alu.mult)
                        self.guard_bank(b1, "vector")
                        self.vcopy(gssh[:, oc, tk * TT:(tk + 1) * TT],
                                   stmp[:, 4 + oc % 4, :])
                        sh_mark[oc % 4] = self.cV.n
                vSh = self.cV.n
                guDone = self.cT.n
                self.wait("scalar", self.cT, guDone)
                shdLD = self.dma("scalar", flat(shd), self.s_shT[2])
                self.wait("tensor", self.cV, vGate)
                self.use_bank(2)
                self.use_bank(3)
                for i in range(8):
                    bk = 2 + i // 4
                    self.tr(self.pb[bk][0:NE, (i % 4) * 128:(i % 4) * 128 + 128],
                            wmat[:, i, :], self.ident32[:])
                tWM = self.cT.n
                self.wait("vector", self.cT, tWM)
                for i in range(8):
                    bk = 2 + i // 4
                    self.vcopy(wfm16[:, i * 128:(i + 1) * 128],
                               self.pb[bk][0:NE, (i % 4) * 128:(i % 4) * 128 + 128])
                self.guard_bank(2, "vector")
                self.guard_bank(3, "vector")
                vWF = self.cV.n
                self.wait("tensor", self.cV, vSh)
                self.wait("tensor", self.dA, shdLD)
                for tk in range(2):
                    for oc in range(NG):
                        bank = oc % 2
                        self.use_bank(bank)
                        for ic in range(NG):
                            self.mm(self.pb[bank][:, 0:TT],
                                    shd[:, ic, oc * 128:(oc + 1) * 128],
                                    gssh[:, ic, tk * TT:(tk + 1) * TT],
                                    ic == 0, ic == NG - 1)
                        tMM = self.cT.n
                        self.wait("vector", self.cT, tMM)
                        self.vcopy(acc[:, oc, tk * TT:(tk + 1) * TT],
                                   self.pb[bank][:, 0:TT])
                        self.guard_bank(bank, "vector")
                self._shg_use = self.cT.n

                vShD = self.cV.n
                self.wait("scalar", self.cV, vWF)
                self.wait("scalar", self.cT, self._wrow_use)
                wrow_mark = [self.dma("scalar", wrow[:, 0, :], wfm16[0:1, :]), 0]
                wrow_mark_use = [0, 0]
                for e in range(NE):
                    j = q * NE + e
                    par = j % 2
                    if j + 1 < NITER:
                        prefetch_gu(j + 1)
                    self.wait("tensor", self.cV, vWF)
                    self.wait("tensor", self.dA, wrow_mark[e % 2])
                    self.use_bank(6)
                    self.use_bank(7)
                    self.mm(self.pb[6][:, 0:TT], self.ones16[:],
                            wrow[:, e % 2, 0:TT], True, True)
                    self.mm(self.pb[7][:, 0:TT], self.ones16[:],
                            wrow[:, e % 2, TT:QTR], True, True)
                    self._wrow_use = self.cT.n
                    tWB = self.cT.n
                    if e + 1 < NE:
                        self.wait("scalar", self.cT, wrow_mark_use[(e + 1) % 2])
                        wrow_mark[(e + 1) % 2] = self.dma(
                            "scalar", wrow[:, (e + 1) % 2, :],
                            wfm16[e + 1:e + 2, :])
                    wrow_mark_use[e % 2] = self.cT.n
                    self.wait("vector", self.cT, tWB)
                    self.vcopy(wb_sb[:, 0:TT], self.pb[6][:, 0:TT])
                    self.vcopy(wb_sb[:, TT:QTR], self.pb[7][:, 0:TT])
                    self.guard_bank(6, "vector")
                    self.guard_bank(7, "vector")
                    vWB = self.cV.n

                    self.wait("tensor", self.dS, ld_gu[j])
                    gu_marks = {}
                    for tk in range(2):
                        for oc in range(4):
                            self.use_bank(4 * tk + oc)
                            for ic in range(NG):
                                self.mm(self.pb[4 * tk + oc][:, 0:TT],
                                        wge[:, par, ic, oc * 128:(oc + 1) * 128],
                                        u1q[:, ic, tk * TT:(tk + 1) * TT],
                                        ic == 0, ic == NG - 1)
                        gu_marks[("g", tk)] = self.cT.n
                        self.wait("scalar", self.cT, gu_marks[("g", tk)])
                        self.wait("scalar", self.cV, self._gs_read[tk])
                        for oc in range(4):
                            self.act(stmp[:, 4 * tk + oc, :],
                                     self.pb[4 * tk + oc][:, 0:TT], Act.Silu)
                            self.guard_bank(4 * tk + oc, "scalar")
                        gu_marks[("s", tk)] = self.cS.n
                    for tk in range(2):
                        for oc in range(4):
                            self.use_bank(4 * tk + oc)
                            for ic in range(NG):
                                self.mm(self.pb[4 * tk + oc][:, 0:TT],
                                        wue[:, par, ic, oc * 128:(oc + 1) * 128],
                                        u1q[:, ic, tk * TT:(tk + 1) * TT],
                                        ic == 0, ic == NG - 1)
                        gu_marks[("u", tk)] = self.cT.n
                        self.wait("vector", self.cT, gu_marks[("u", tk)])
                        self.wait("vector", self.cS, gu_marks[("s", tk)])
                        if tk == 0:
                            self.wait("vector", self.cT, self._d_mms)
                        for oc in range(4):
                            self.tt("vector",
                                    gs16[:, oc, tk * TT:(tk + 1) * TT],
                                    self.pb[4 * tk + oc][:, 0:TT],
                                    stmp[:, 4 * tk + oc, :], Alu.mult)
                            self.guard_bank(4 * tk + oc, "vector")
                        self._gs_read[tk] = self.cV.n
                        for oc in range(4):
                            self.tt("vector",
                                    gs16[:, oc, tk * TT:(tk + 1) * TT],
                                    gs16[:, oc, tk * TT:(tk + 1) * TT],
                                    wb_sb[:, tk * TT:(tk + 1) * TT], Alu.mult)
                        gu_marks[("gs", tk)] = self.cV.n
                    a_done_p[par] = self.cT.n
                    self.wait("tensor", self.dS, ld_d[j])
                    for tk in range(2):
                        self.wait("tensor", self.cV, gu_marks[("gs", tk)])
                        for oc in range(NG):
                            bank = 2 * tk + oc % 2
                            self.use_bank(bank)
                            for ic in range(4):
                                self.mm(self.pb[bank][:, 0:TT],
                                        wde[:, ic, oc * 128:(oc + 1) * 128],
                                        gs16[:, ic, tk * TT:(tk + 1) * TT],
                                        ic == 0, ic == 3)
                            tDN = self.cT.n
                            self.wait("vector", self.cT, tDN)
                            self.tt("vector",
                                    acc[:, oc, tk * TT:(tk + 1) * TT],
                                    acc[:, oc, tk * TT:(tk + 1) * TT],
                                    self.pb[bank][:, 0:TT], Alu.add)
                            self.guard_bank(bank, "vector")
                    d_done_last[0] = self.cT.n
                    self._d_mms = self.cT.n
                    acc_mark = self.cV.n
                    if j + 1 < NITER:
                        prefetch_d(j + 1)

                u1q_use = a_done_p[(q * NE + NE - 1) % 2]
                self.wait("tensor", self.cV, acc_mark)
                for tk in range(2):
                    self.wait("scalar", self.dG, ylast)
                    xm = self.dma("scalar", x1_tm[:],
                                  self.s_x1[q * QTR + tk * TT:
                                            q * QTR + (tk + 1) * TT].rearrange(
                                      "(s p) h -> p s h", p=128))
                    self.wait("vector", self.dA, xm)
                    self.wait("tensor", self.dA, xm)
                    for s in range(NSUB):
                        b0, b1 = 4 + 2 * (s % 2), 5 + 2 * (s % 2)
                        self.use_bank(b0)
                        self.use_bank(b1)
                        for g in range(NG):
                            bank = b0 if g < 4 else b1
                            self.tr(self.pb[bank].ap()[:, (g % 4) * 128:
                                                       (g % 4) * 128 + 128],
                                    acc[:, g, tk * TT + s * 128:
                                        tk * TT + (s + 1) * 128],
                                    self.ident32[:])
                        tS = self.cT.n
                        self.wait("vector", self.cT, tS)
                        self.tt("vector", x1_tm[:, s, 0:512], self.pb[b0][:],
                                x1_tm[:, s, 0:512], Alu.add)
                        self.tt("vector", x1_tm[:, s, 512:1024], self.pb[b1][:],
                                x1_tm[:, s, 512:1024], Alu.add)
                        self.guard_bank(b0, "vector")
                        self.guard_bank(b1, "vector")
                    vOut = self.cV.n
                    self.wait("gpsimd", self.cV, vOut)
                    ylast = self.dma(
                        "gpsimd",
                        self.y[q * QTR + tk * TT:q * QTR + (tk + 1) * TT].rearrange(
                            "(s p) h -> p s h", p=128), x1_tm[:])

    def finish(self):
        for name, (tap, src) in self.tap_outs.items():
            self.wait("gpsimd", self.dA)
            self.wait("gpsimd", self.dS)
            self.dma("gpsimd", tap, src)
        self.wait("gpsimd", self.dG)
        self.wait("sync", self.dG)


_BUILT = {}


def _get_nc(taps=()):
    key = tuple(sorted(taps))
    if key not in _BUILT:
        mk = MK(taps)
        nc = mk.build()
        _BUILT[key] = (nc, mk.tap_outs)
    return _BUILT[key]


def make_in_maps(inputs):
    x = np.ascontiguousarray(np.asarray(inputs["L0_attn_x"],
                                        dtype=np.float32).reshape(-1, D))
    g = lambda n: np.ascontiguousarray(np.asarray(inputs[n], np.float32))
    vec = lambda n: g(n).reshape(1, D)
    common = {
        "w_ln_att_g": vec("L0_attn_ln_att_g"),
        "w_ln_att_b": vec("L0_attn_ln_att_b"),
        "w_ln_ffn_g": vec("L0_attn_ln_ffn_g"),
        "w_ln_ffn_b": vec("L0_attn_ln_ffn_b"),
        "w_td": vec("L0_attn_att_time_decay"),
        "w_tf": vec("L0_attn_att_time_first"),
        "w_mk": vec("L0_attn_att_mix_k"),
        "w_mv": vec("L0_attn_att_mix_v"),
        "w_mr": vec("L0_attn_att_mix_r"),
        "w_fmk": vec("L0_attn_ffn_mix_k"),
        "w_fmr": vec("L0_attn_ffn_mix_r"),
        "w_wk": g("L0_attn_att_wk"),
        "w_wv": g("L0_attn_att_wv"),
        "w_wr": g("L0_attn_att_wr"),
        "w_wo": g("L0_attn_att_wo"),
        "w_fwk": g("L0_attn_ffn_wk"),
        "w_fwv": g("L0_attn_ffn_wv"),
        "w_fwr": g("L0_attn_ffn_wr"),
        "w_gw": g("L0_ffn_gate_w"),
        "w_gb": g("L0_ffn_gate_bias").reshape(1, NE),
        "w_sg": g("L0_ffn_shared_gate"),
        "w_su": g("L0_ffn_shared_up"),
        "w_sd": g("L0_ffn_shared_down"),
    }
    eg = g("L0_ffn_expert_gate")
    eu = g("L0_ffn_expert_up")
    ed = g("L0_ffn_expert_down")
    maps = []
    for c in range(NCORES):
        seq, pos = c // 4, c % 4
        a8 = np.full(8, 1e9, np.float32)
        sel8 = np.zeros((8, 1), np.float32)
        for j in range(NCORES):
            if j // 4 == seq and j < c:
                a8[j] = float(TPC) * (c - 1 - j)
        if pos > 0:
            sel8[c - 1, 0] = 1.0
            xp = x[c * TPC - 1:c * TPC]
            pm = 1.0
        else:
            xp = np.zeros((1, D), np.float32)
            pm = 0.0
        m = dict(common)
        m["xs"] = x[c * TPC:(c + 1) * TPC]
        m["xp"] = np.ascontiguousarray(xp)
        m["a8"] = a8.reshape(1, 8)
        m["sel8"] = sel8
        m["pmask"] = np.array([[pm]], np.float32)
        m["w_meg"] = np.ascontiguousarray(eg[c * EPL:(c + 1) * EPL])
        m["w_meu"] = np.ascontiguousarray(eu[c * EPL:(c + 1) * EPL])
        m["w_med"] = np.ascontiguousarray(ed[c * EPL:(c + 1) * EPL])
        maps.append(m)
    return maps


def kernel(**inputs):
    nc, _ = _get_nc()
    maps = make_in_maps(inputs)
    res = run_bass_kernel_spmd(nc, maps, core_ids=list(range(NCORES)))
    out = np.concatenate([res.results[c]["y"] for c in range(NCORES)], axis=0)
    return out.reshape(B, L, D)
